# version 04
# baseline (speedup 1.0000x reference)
import contextlib

import numpy as np
import ml_dtypes

import concourse.bass as bass
import concourse.mybir as mybir
from concourse.bass_utils import run_bass_kernel_spmd

F32 = mybir.dt.float32
BF16 = mybir.dt.bfloat16
AF = mybir.ActivationFunctionType
ALU = mybir.AluOpType
AX = mybir.AxisListType

BF = ml_dtypes.bfloat16

S = 4096
D = 2048
HID = 8192
NT = S // 128
NH, NKV, HD = 8, 4, 256
WIN = 1024
LN_EPS = 1e-5
RMS_EPS = 1e-6

CP_TANH, CP_EXP50, CP_NEG1, CP_Q256, CP_ONE, CP_WHT = 0, 1, 2, 3, 4, 5



def host_constants():
    c = {}
    c["ones_col_bf"] = np.ones((128, 1), dtype=BF)
    c["ones_row_f32"] = np.ones((1, 128), dtype=np.float32)
    c["ident_f32"] = np.eye(128, dtype=np.float32)
    c["ident_bf"] = np.eye(128, dtype=BF)

    i = np.arange(128)
    pop = np.array([[bin(a & b).count("1") for b in i] for a in i])
    c["h128_bf"] = np.where(pop % 2 == 0, 1.0, -1.0).astype(BF)

    j = np.arange(64, dtype=np.float64)
    theta = 1.0 / (10000.0 ** (2.0 * j / 128.0))
    pos = np.arange(S, dtype=np.float64)
    ang = (pos[None, :].astype(np.float32).astype(np.float64)
           * theta[:, None]).astype(np.float32)
    cosv = np.cos(ang.astype(np.float64)).astype(np.float32)
    sinv = np.sin(ang.astype(np.float64)).astype(np.float32)
    cc = np.empty((128, S), dtype=np.float32)
    ss = np.empty((128, S), dtype=np.float32)
    cc[0::2] = cosv
    cc[1::2] = cosv
    ss[0::2] = -sinv
    ss[1::2] = sinv
    c["cc"] = cc
    c["ss"] = ss

    p = np.arange(128)[:, None]
    f = np.arange(128)[None, :]
    c["mask_diag_bf"] = (p <= f).astype(BF)
    c["mask_far_bf"] = (p > f).astype(BF)
    r = np.arange(256)[None, :]
    c["pm0"] = (r < p).astype(BF)
    c["pm1"] = (r <= p + 127).astype(BF)
    c["pm8"] = (p <= r).astype(BF)
    c["pm9"] = (r >= p + 128).astype(BF)

    cp = np.zeros((128, 8), dtype=np.float32)
    cp[:, CP_TANH] = 1.0 / 50.0
    cp[:, CP_EXP50] = 50.0
    cp[:, CP_NEG1] = -1.0
    cp[:, CP_Q256] = 1.0 / 256.0
    cp[:, CP_ONE] = 1.0
    cp[:, CP_WHT] = float(HID) ** -0.5
    c["cpool"] = cp

    c["pairswap"] = [x ^ 1 for x in range(32)]
    return c


def layout_inputs(inp):
    d = {}
    d["qs_colsT"] = np.ascontiguousarray(
        np.tile(np.asarray(inp["L0_attn_q_scale"], np.float32), NH).reshape(16, 128).T)
    d["ks_colsT"] = np.ascontiguousarray(
        np.tile(np.asarray(inp["L0_attn_k_scale"], np.float32), NKV).reshape(8, 128).T)
    for k in ("Wq", "Wk", "Wv", "Wo"):
        d[k] = np.ascontiguousarray(np.asarray(inp[f"L0_attn_{k}"], np.float32))
    for k in ("W1", "W2", "W3"):
        d[k] = np.ascontiguousarray(np.asarray(inp[f"L0_ffn_{k}"], np.float32))
    return d



ENGS = ("t", "v", "s", "g", "y")


class Sem:
    __slots__ = ("h", "n")

    def __init__(self, handle):
        self.h = handle
        self.n = 0

    def bump(self, k=1):
        self.n += k
        return self.n


class Prog:

    def __init__(self):
        self.q = {e: [] for e in ENGS}

    def op(self, eng, fn, inc=None, k=1):
        if inc is None:
            self.q[eng].append(fn)
            return None
        self.q[eng].append(lambda e, _f=fn, _h=inc.h, _k=k: _f(e).then_inc(_h, _k))
        return inc.bump(k)

    def dma(self, eng, sem, out, in_):
        return self.op(eng, lambda e, _o=out, _i=in_: e.dma_start(out=_o, in_=_i),
                       inc=sem, k=16)

    def wait(self, eng, sem, val):
        if val is not None and val > 0:
            self.q[eng].append(lambda e, _h=sem.h, _v=val: e.wait_ge(_h, _v))

    def install(self, block):
        def run(ops):
            def body(e):
                for f in ops:
                    f(e)
            return body
        if self.q["t"]:
            block.tensor(run(self.q["t"]))
        if self.q["v"]:
            block.vector(run(self.q["v"]))
        if self.q["s"]:
            block.scalar(run(self.q["s"]))
        if self.q["g"]:
            block.gpsimd(run(self.q["g"]))
        if self.q["y"]:
            block.sync(run(self.q["y"]))


def make_sems(nc, es, names):
    return {n: Sem(es.enter_context(nc.semaphore(n))) for n in names}


def ap2(tensor_handle, offset, pattern):
    return bass.AP(tensor_handle, offset, pattern)



def emit_phase_p(nc, prog, es, sems, W1, W3, W2, W1T, W3T, W2T, ident_f32_sb, cpool_sb):
    pld = [sems["p_ld0"], sems["p_ld1"]]
    w2ld = sems["p_w2ld"]
    tr = sems["p_tr"]
    evs = sems["p_ev"]
    stw = [sems["p_st0"], sems["p_st1"]]
    stw2 = [sems["p_st20"], sems["p_st21"]]
    bfv = sems["p_bfv"]
    bfg = sems["p_bfg"]
    bfs = sems["p_bfs"]

    in2 = [es.enter_context(nc.sbuf_tensor(f"p_in{i}", [128, 2 * D], F32)) for i in range(2)]
    col_t = [es.enter_context(nc.sbuf_tensor(f"p_col{i}", [128, 16 * 128], BF16)) for i in range(2)]
    w2in = es.enter_context(nc.sbuf_tensor("p_w2in", [128, HID], F32))
    w2a = es.enter_context(nc.sbuf_tensor("p_w2a", [128, HID], F32))
    w2b = es.enter_context(nc.sbuf_tensor("p_w2b", [128, HID], F32))
    col2_t = [es.enter_context(nc.sbuf_tensor(f"p_col2{i}", [128, 64 * 128], BF16)) for i in range(2)]
    ps_tr = [es.enter_context(nc.psum_tensor(f"p_ps{i}", [128, 512], F32)) for i in range(2)]

    gseq = [0]
    tr_marks = {}
    ev_marks = {}
    ld_marks = {}
    st_marks = {}
    st2_marks = {}
    pc_last_group = {}
    w2_last_group = {}
    s0_marks = {}
    w2ld_marks = {}
    bf_last = {}

    def group(src_fn, evac_fn):
        g = gseq[0]
        gseq[0] += 1
        if g >= 2:
            prog.wait("t", evs, ev_marks[g - 2])
        for k in range(4):
            fn = src_fn(k, ps_tr[g % 2])
            if k == 3:
                tr_marks[g] = prog.op("t", fn, inc=tr)
            else:
                prog.op("t", fn)
        prog.wait("s", tr, tr_marks[g])
        ev_marks[g] = prog.op("s", evac_fn(ps_tr[g % 2]), inc=evs)
        return g

    def emit_w1w3_pair(pc):
        wi = pc // 32
        Wsrc, Wdst = (W1, W1T) if wi == 0 else (W3, W3T)
        hpair = pc % 32
        slot = pc % 2
        if pc >= 2:
            prog.wait("y", tr, tr_marks[pc_last_group[pc - 2]])
        row0 = hpair * 2 * 128
        src = ap2(Wsrc.tensor, row0 * D, [[D, 128], [128 * D, 2], [1, D]])
        ld_marks[pc] = prog.dma("y", pld[slot], in2[slot][:], src)
        prog.wait("t", pld[slot], ld_marks[pc])
        for sub in range(2):
            hc = hpair * 2 + sub
            csl = hc % 2
            for grp in range(4):
                def mk(k, _ps, _sub=sub, _grp=grp, _slot=slot):
                    dc = _grp * 4 + k
                    return (lambda tt, _p=_ps, _k=k, _in=in2[_slot], _dc=dc, _id=ident_f32_sb,
                            _off=_sub * D:
                            tt.transpose(_p[:, _k * 128:(_k + 1) * 128],
                                         _in[:, _off + _dc * 128:_off + (_dc + 1) * 128],
                                         _id[:]))

                def mkev(_ps, _grp=grp, _csl=csl, _hc=hc, _wi=wi):
                    def f(s_, _p=_ps, _c=col_t[_csl], _g=_grp):
                        return s_.copy(_c[:, _g * 512:(_g + 1) * 512], _p[:])
                    return f
                if grp == 0 and (wi * 64 + hc) >= 2:
                    prog.wait("s", stw[csl], st_marks[wi * 64 + hc - 2])
                g = group(mk, mkev)
            pc_last_group[pc] = g
            prog.wait("s", evs, ev_marks[g])
            dst = ap2(Wdst.tensor, hc * 128, [[HID, 128], [128 * HID, 16], [1, 128]])
            st_marks[wi * 64 + hc] = prog.dma("s", stw[csl], dst, col_t[csl][:])

    def emit_w2_chunk(dc):
        if dc >= 1:
            prog.wait("y", bfv, s0_marks[(dc - 1, "v")])
            prog.wait("y", bfg, s0_marks[(dc - 1, "g")])
            prog.wait("y", bfv, s0_marks[(dc - 1, "s")])
        w2ld_marks[dc] = prog.dma("y", w2ld, w2in[:], W2[dc * 128:(dc + 1) * 128, :])
        for e in ("v", "g"):
            prog.wait(e, w2ld, w2ld_marks[dc])
        for st_i in range(6):
            dist = 128 * (2 ** st_i)
            npair = HID // (2 * dist)
            src = w2in if st_i == 0 else (w2a if st_i % 2 == 1 else w2b)
            dst_b = w2a if st_i % 2 == 0 else w2b
            prog.wait("v", bfg, bfg.n)
            prog.wait("g", bfv, bfv.n)
            if st_i == 1 and dc >= 1:
                for e in ("v", "g", "s"):
                    prog.wait(e, tr, tr_marks[w2_last_group[dc - 1]])

            a_ap = ap2(src, 0, [[HID, 128], [2 * dist, npair], [1, dist]])
            b_ap = ap2(src, dist, [[HID, 128], [2 * dist, npair], [1, dist]])
            lo_ap = ap2(dst_b, 0, [[HID, 128], [2 * dist, npair], [1, dist]])
            mlo = prog.op("v", lambda v, _o=lo_ap, _a=a_ap, _b=b_ap:
                          v.tensor_tensor(_o, _a, _b, ALU.add), inc=bfv)
            if dist < 2048:
                kv = max(npair // 8, 1)
                avv = ap2(src, 0, [[HID, 128], [2 * dist, kv], [1, dist]])
                bvv = ap2(src, dist, [[HID, 128], [2 * dist, kv], [1, dist]])
                hvv = ap2(dst_b, dist, [[HID, 128], [2 * dist, kv], [1, dist]])
                agg = ap2(src, kv * 2 * dist, [[HID, 128], [2 * dist, npair - kv], [1, dist]])
                bgg = ap2(src, kv * 2 * dist + dist, [[HID, 128], [2 * dist, npair - kv], [1, dist]])
                hgg = ap2(dst_b, kv * 2 * dist + dist, [[HID, 128], [2 * dist, npair - kv], [1, dist]])
            else:
                kv = dist // 8
                avv = ap2(src, 0, [[HID, 128], [2 * dist, npair], [1, kv]])
                bvv = ap2(src, dist, [[HID, 128], [2 * dist, npair], [1, kv]])
                hvv = ap2(dst_b, dist, [[HID, 128], [2 * dist, npair], [1, kv]])
                agg = ap2(src, kv, [[HID, 128], [2 * dist, npair], [1, dist - kv]])
                bgg = ap2(src, dist + kv, [[HID, 128], [2 * dist, npair], [1, dist - kv]])
                hgg = ap2(dst_b, dist + kv, [[HID, 128], [2 * dist, npair], [1, dist - kv]])
            mv = prog.op("v", lambda v, _o=hvv, _a=avv, _b=bvv:
                         v.tensor_tensor(_o, _a, _b, ALU.subtract), inc=bfv)
            mg = prog.op("g", lambda g, _o=hgg, _a=agg, _b=bgg:
                         g.tensor_tensor(_o, _a, _b, ALU.subtract), inc=bfg)
            if st_i == 0:
                s0_marks[(dc, "v")] = mv
                s0_marks[(dc, "g")] = mg
                s0_marks[(dc, "s")] = mlo
        bf_last[dc] = (bfv.n, bfg.n)

    def emit_w2_transposes(dc):
        prog.wait("t", bfv, bf_last[dc][0])
        prog.wait("t", bfg, bf_last[dc][1])
        for grp in range(16):
            def mk(k, _ps, _grp=grp):
                hb = _grp * 4 + k
                return (lambda tt, _p=_ps, _k=k, _hb=hb, _id=ident_f32_sb, _w=w2b:
                        tt.transpose(_p[:, _k * 128:(_k + 1) * 128],
                                     _w[:, _hb * 128:(_hb + 1) * 128], _id[:]))

            def mkev(_ps, _grp=grp, _dc=dc):
                def f(s_, _p=_ps, _c=col2_t[_dc % 2], _g=_grp, _cp=cpool_sb):
                    return s_.activation(_c[:, _g * 512:(_g + 1) * 512], _p[:],
                                         AF.Identity, scale=_cp[:, CP_WHT:CP_WHT + 1])
                return f
            if grp == 0 and dc >= 2:
                prog.wait("s", stw2[dc % 2], st2_marks[dc - 2])
            g = group(mk, mkev)
        w2_last_group[dc] = g
        prog.wait("y", evs, ev_marks[g])
        dst = ap2(W2T.tensor, dc * 128, [[D, 128], [128 * D, 64], [1, 128]])
        st2_marks[dc] = prog.dma("y", stw2[dc % 2], dst, col2_t[dc % 2][:])

    for step in range(64):
        if step % 4 == 0:
            if step >= 4:
                emit_w2_transposes(step // 4 - 1)
            emit_w2_chunk(step // 4)
        emit_w1w3_pair(step)
    emit_w2_transposes(15)
    return st_marks


def build_phase_p_standalone(d):
    nc = bass.Bass("TRN2", target_bir_lowering=False, debug=False,
                   detect_race_conditions=False)
    W1 = nc.dram_tensor("W1", [HID, D], F32, kind="ExternalInput").ap()
    W3 = nc.dram_tensor("W3", [HID, D], F32, kind="ExternalInput").ap()
    W2 = nc.dram_tensor("W2", [D, HID], F32, kind="ExternalInput").ap()
    identf = nc.dram_tensor("identf", [128, 128], F32, kind="ExternalInput").ap()
    cpool_d = nc.dram_tensor("cpool", [128, 8], F32, kind="ExternalInput").ap()
    W1T = nc.dram_tensor("W1T", [D, HID], BF16, kind="ExternalOutput").ap()
    W3T = nc.dram_tensor("W3T", [D, HID], BF16, kind="ExternalOutput").ap()
    W2T = nc.dram_tensor("W2T", [HID, D], BF16, kind="ExternalOutput").ap()

    with contextlib.ExitStack() as es:
        ident_sb = es.enter_context(nc.sbuf_tensor("ident_sb", [128, 128], F32))
        cpool_sb = es.enter_context(nc.sbuf_tensor("cpool_sb", [128, 8], F32))
        sems = make_sems(nc, es, P_SEMS + ["init"])
        prog = Prog()
        prog.dma("y", sems["init"], ident_sb[:], identf)
        im = prog.dma("y", sems["init"], cpool_sb[:], cpool_d)
        prog.wait("t", sems["init"], im)
        prog.wait("s", sems["init"], im)
        emit_phase_p(nc, prog, es, sems, W1, W3, W2, W1T, W3T, W2T, ident_sb, cpool_sb)
        for nme in ("p_st0", "p_st1", "p_st20", "p_st21"):
            prog.wait("y", sems[nme], sems[nme].n)
        with nc.Block("phasep") as blk:
            prog.install(blk)
    return nc



def emit_phase_a1(nc, prog, es, sems, io, cst):
    x, Wq, Wk, Wv = io["x"], io["Wq"], io["Wk"], io["Wv"]
    Qs, Ks, Vs = io["Qs"], io["Ks"], io["Vs"]
    cc_d, ss_d, qsT_d, ksT_d = io["cc"], io["ss"], io["qsT"], io["ksT"]
    ident_bf = cst["ident_bf"]
    ones_col = cst["ones_col"]
    ones_row_f = cst["ones_row_f"]
    cpool = cst["cpool"]
    pairswap = cst["pairswap"]

    a_x = [sems["a_x0"], sems["a_x1"]]
    a_cs = [sems["a_cs0"], sems["a_cs1"]]
    a_wl = sems["a_wl"]
    a_s1 = sems["a_s1"]
    a_v1 = sems["a_v1"]
    a_tr = sems["a_tr"]
    a_mm = sems["a_mm"]
    a_sq = sems["a_sq"]
    a_sk = sems["a_sk"]
    a_sv = [sems["a_sv0"], sems["a_sv1"]]

    wq_sb = es.enter_context(nc.sbuf_tensor("a_wq", [128, 16 * 2048], BF16))
    wk_sb = es.enter_context(nc.sbuf_tensor("a_wk", [128, 16 * 1024], BF16))
    wv_sb = es.enter_context(nc.sbuf_tensor("a_wv", [128, 16 * 1024], BF16))
    qs_sb = es.enter_context(nc.sbuf_tensor("a_qsc", [128, 16], F32))
    ks_sb = es.enter_context(nc.sbuf_tensor("a_ksc", [128, 8], F32))
    x_sb = [es.enter_context(nc.sbuf_tensor(f"a_x{i}", [128, D], F32)) for i in range(2)]
    ln_tm = [es.enter_context(nc.sbuf_tensor(f"a_ln{i}", [128, D], BF16)) for i in range(2)]
    lf = [es.enter_context(nc.sbuf_tensor(f"a_lf{i}", [128, 16 * 256], BF16)) for i in range(2)]
    ccs = [es.enter_context(nc.sbuf_tensor(f"a_cc{i}", [128, 256], F32)) for i in range(2)]
    sss = [es.enter_context(nc.sbuf_tensor(f"a_ssl{i}", [128, 256], F32)) for i in range(2)]
    stats = [es.enter_context(nc.sbuf_tensor(f"a_stt{i}", [128, 16], F32)) for i in range(2)]
    qf = [[es.enter_context(nc.sbuf_tensor(f"a_qf{i}{c}", [128, 256], F32)) for c in range(2)]
          for i in range(2)]
    sq = [[es.enter_context(nc.sbuf_tensor(f"a_sq{i}{c}", [128, 256], BF16)) for c in range(2)]
          for i in range(2)]
    qn = [es.enter_context(nc.sbuf_tensor("a_qn0", [128, 256], F32))] * 2
    shufb = [es.enter_context(nc.sbuf_tensor("a_sh0", [128, 256], F32))] * 2
    rt1 = [es.enter_context(nc.sbuf_tensor("a_r10", [128, 256], F32))] * 2
    rt2 = [es.enter_context(nc.sbuf_tensor("a_r20", [128, 256], F32))] * 2
    rows = [es.enter_context(nc.sbuf_tensor(f"a_ro{i}", [1, 3 * 256], F32)) for i in range(2)]
    qout = [es.enter_context(nc.sbuf_tensor("a_qo0", [128, 16 * 256], BF16))] * 2
    kout = [es.enter_context(nc.sbuf_tensor("a_ko0", [128, 8 * 256], BF16))] * 2
    vout = [es.enter_context(nc.sbuf_tensor(f"a_vo{i}", [128, 1024], BF16)) for i in range(2)]

    ps_ltr = [es.enter_context(nc.psum_tensor(f"a_pl{i}", [128, 512], BF16)) for i in range(2)]
    ps_v = es.enter_context(nc.psum_tensor("a_pv", [128, 1024], F32))
    ps_q = es.enter_context(nc.psum_tensor("a_pq", [128, 1024], F32))
    ps_misc = es.enter_context(nc.psum_tensor("a_pm", [128, 1024], F32))

    wl_marks = []
    for dc in range(16):
        wl_marks.append(prog.dma("g", a_wl, wq_sb[:, dc * 2048:(dc + 1) * 2048],
                                 Wq[dc * 128:(dc + 1) * 128, :]))
    for dc in range(16):
        wl_marks.append(prog.dma("g", a_wl, wk_sb[:, dc * 1024:(dc + 1) * 1024],
                                 Wk[dc * 128:(dc + 1) * 128, :]))
    for dc in range(16):
        wl_marks.append(prog.dma("g", a_wl, wv_sb[:, dc * 1024:(dc + 1) * 1024],
                                 Wv[dc * 128:(dc + 1) * 128, :]))
    wl_marks.append(prog.dma("g", a_wl, qs_sb[:], qsT_d))
    wl_marks.append(prog.dma("g", a_wl, ks_sb[:], ksT_d))
    wl_all = wl_marks[-1]
    wl_vv = wl_marks[7]

    lnm, redm, trm, evm = {}, {}, {}, {}
    vproj, vev = {}, {}
    qmm, evq, sqm, ssm, wrm, rsm, repm, g6m = {}, {}, {}, {}, {}, {}, {}, {}
    stm = {}
    lffree = {}
    csl = {}
    g6m_all = {}

    def emit_ln(t):
        sl = t % 2
        if t >= 2:
            prog.wait("y", a_s1, lnm[t - 2])
            prog.wait("y", a_v1, redm[t - 2])
        xm = prog.dma("y", a_x[sl], x_sb[sl][:], x[t * 128:(t + 1) * 128, :])
        st_ = stats[sl]
        prog.wait("s", a_x[sl], xm)
        if t >= 2:
            prog.wait("s", a_tr, trm[(t - 2, 3)])
        sqm_t = prog.op("s", lambda s_, _x=x_sb[sl], _d=ln_tm[sl], _a=st_:
                        s_.activation(_d[:], _x[:], AF.Square, accum_out=_a[:, 1:2]),
                        inc=a_s1)
        prog.wait("v", a_x[sl], xm)
        redm[t] = prog.op("v", lambda v, _x=x_sb[sl], _a=st_:
                          v.tensor_reduce(_a[:, 0:1], _x[:], AX.X, ALU.add), inc=a_v1)
        prog.wait("v", a_v1, redm[t])
        mb = prog.op("v", lambda v, _a=st_:
                     v.tensor_scalar(_a[:, 2:3], _a[:, 0:1], 1.0 / D, None, ALU.mult),
                     inc=a_v1)
        prog.wait("v", a_s1, sqm_t)
        prog.op("v", lambda v, _a=st_:
                v.tensor_scalar(_a[:, 3:4], _a[:, 1:2], 1.0 / D, None, ALU.mult))
        prog.wait("v", a_v1, mb)
        mb = prog.op("v", lambda v, _a=st_:
                     v.tensor_tensor(_a[:, 4:5], _a[:, 2:3], _a[:, 2:3], ALU.mult),
                     inc=a_v1)
        prog.wait("v", a_v1, mb)
        mb = prog.op("v", lambda v, _a=st_:
                     v.tensor_tensor(_a[:, 5:6], _a[:, 3:4], _a[:, 4:5], ALU.subtract),
                     inc=a_v1)
        prog.wait("v", a_v1, mb)
        mb = prog.op("v", lambda v, _a=st_:
                     v.tensor_scalar(_a[:, 6:7], _a[:, 5:6], LN_EPS, None, ALU.add),
                     inc=a_v1)
        prog.wait("v", a_v1, mb)
        vm = prog.op("v", lambda v, _a=st_:
                     v.reciprocal(_a[:, 7:8], _a[:, 6:7]), inc=a_v1)
        prog.wait("s", a_v1, vm)
        m1 = prog.op("s", lambda s_, _a=st_:
                     s_.activation(_a[:, 8:9], _a[:, 7:8], AF.Sqrt), inc=a_s1)
        prog.wait("s", a_s1, m1)
        m2 = prog.op("s", lambda s_, _a=st_:
                     s_.activation(_a[:, 9:10], _a[:, 2:3], AF.Identity,
                                   scale=_a[:, 8:9]), inc=a_s1)
        prog.wait("s", a_s1, m2)
        m3 = prog.op("s", lambda s_, _a=st_, _cp=cpool:
                     s_.activation(_a[:, 10:11], _a[:, 9:10], AF.Identity,
                                   scale=_cp[:, CP_NEG1:CP_NEG1 + 1]), inc=a_s1)
        prog.wait("s", a_s1, m3)
        lnm[t] = prog.op("s", lambda s_, _x=x_sb[sl], _d=ln_tm[sl], _a=st_:
                         s_.activation(_d[:], _x[:], AF.Identity, scale=_a[:, 8:9],
                                       bias=_a[:, 10:11]), inc=a_s1)

    def emit_transpose(t, p):
        sl = t % 2
        side = t - 2 * p
        prog.wait("t", a_s1, lnm[t])
        for grp in range(4):
            gseq = t * 4 + grp
            if gseq >= 2:
                prog.wait("t", a_v1, evm[gseq - 2])
            for k in range(4):
                dc = grp * 4 + k
                fn = (lambda tt, _ps=ps_ltr[gseq % 2], _k=k, _ln=ln_tm[sl], _dc=dc,
                      _id=ident_bf:
                      tt.transpose(_ps[:, _k * 128:(_k + 1) * 128],
                                   _ln[:, _dc * 128:(_dc + 1) * 128], _id[:]))
                if k == 3:
                    trm[(t, grp)] = prog.op("t", fn, inc=a_tr)
                else:
                    prog.op("t", fn)
            prog.wait("v", a_tr, trm[(t, grp)])
            if p >= 2 and grp == 0 and side == 0:
                prev = lffree.get(p - 2)
                if prev is not None:
                    prog.wait("v", a_mm, prev)
            dst = ap2(lf[p % 2], grp * 4 * 256 + side * 128,
                      [[16 * 256, 128], [256, 4], [1, 128]])
            evm[t * 4 + grp] = prog.op(
                "v", lambda v, _d=dst, _ps=ps_ltr[gseq % 2]: v.tensor_copy(_d, _ps[:]),
                inc=a_v1)

    def emit_vproj(t, p):
        side = t - 2 * p
        prog.wait("t", a_v1, evm[t * 4 + 3])
        if t == 0:
            prog.wait("t", a_wl, wl_all)
        if t >= 1:
            prog.wait("t", a_v1, vev[t - 1])
        for dc in range(16):
            for half in range(2):
                fn = (lambda tt, _lf=lf[p % 2], _dc=dc, _side=side, _w=wv_sb, _h=half,
                      _ps=ps_v, _st=(dc == 0), _sp=(dc == 15):
                      tt.matmul(_ps[:, _h * 512:(_h + 1) * 512],
                                _lf[:, _dc * 256 + _side * 128:_dc * 256 + _side * 128 + 128],
                                _w[:, _dc * 1024 + _h * 512:_dc * 1024 + (_h + 1) * 512],
                                start=_st, stop=_sp))
                if dc == 15 and half == 1:
                    vproj[t] = prog.op("t", fn, inc=a_mm)
                else:
                    prog.op("t", fn)
        prog.wait("v", a_mm, vproj[t])
        if t >= 2:
            prog.wait("v", a_sv[t % 2], stm[("v", t - 2)])
        vev[t] = prog.op("v", lambda v, _o=vout[t % 2], _ps=ps_v:
                         v.tensor_copy(_o[:], _ps[:]), inc=a_v1)
        prog.wait("g", a_v1, vev[t])
        stm[("v", t)] = prog.dma("g", a_sv[t % 2], Vs[t * 128:(t + 1) * 128, :], vout[t % 2][:])

    def head_params(h, p):
        if h < 8:
            return dict(w=wq_sb, wstride=2048, b0=2 * h, scol=qs_sb, sc=(2 * h, 2 * h + 1),
                        out=qout[p % 2], oblk=2 * h, sqrt_cp=CP_Q256)
        hk = h - 8
        return dict(w=wk_sb, wstride=1024, b0=2 * hk, scol=ks_sb, sc=(2 * hk, 2 * hk + 1),
                    out=kout[p % 2], oblk=2 * hk, sqrt_cp=CP_ONE)

    def emit_proj_head(h, p):
        hp = head_params(h, p)
        st_set = h % 2
        if h >= 2:
            prog.wait("t", a_v1, evq[h - 2])
        if p == 0 and h == 0:
            prog.wait("t", a_wl, wl_all)
        for c in range(2):
            b = hp["b0"] + c
            for dc in range(16):
                fn = (lambda tt, _w=hp["w"], _dc=dc, _b=b, _ws=hp["wstride"],
                      _lf=lf[p % 2], _ps=ps_q, _off=st_set * 512 + c * 256,
                      _st=(dc == 0), _sp=(dc == 15):
                      tt.matmul(_ps[:, _off:_off + 256],
                                _w[:, _dc * _ws + _b * 128:_dc * _ws + _b * 128 + 128],
                                _lf[:, _dc * 256:(_dc + 1) * 256],
                                start=_st, stop=_sp))
                if c == 1 and dc == 15:
                    qmm[h] = prog.op("t", fn, inc=a_mm)
                else:
                    prog.op("t", fn)

    def emit_evq(h):
        st_set = h % 2
        prog.wait("v", a_mm, qmm[h])
        if h >= 2:
            prog.wait("v", a_s1, sqm[h - 2])
        for c in range(2):
            fn = (lambda v, _qf=qf[st_set][c], _ps=ps_q, _off=st_set * 512 + c * 256:
                  v.tensor_copy(_qf[:], _ps[:, _off:_off + 256]))
            if c == 1:
                evq[h] = prog.op("v", fn, inc=a_v1)
            else:
                prog.op("v", fn)

    def emit_sq(h):
        st_set = h % 2
        prog.wait("s", a_v1, evq[h])
        for c in range(2):
            fn = (lambda s_, _qf=qf[st_set][c], _sq=sq[st_set][c]:
                  s_.activation(_sq[:], _qf[:], AF.Square))
            if c == 1:
                sqm[h] = prog.op("s", fn, inc=a_s1)
            else:
                prog.op("s", fn)

    def emit_ssmm(h):
        st_set = h % 2
        prog.wait("t", a_s1, sqm[h])
        if h >= 2:
            prog.wait("t", a_v1, g6m[h - 2])
        for c in range(2):
            fn = (lambda tt, _sq=sq[st_set][c], _ps=ps_misc,
                  _off=st_set * 512 + 256,
                  _oc=ones_col, _st=(c == 0), _sp=(c == 1):
                  tt.matmul(_ps[0:1, _off:_off + 256], _oc[:], _sq[:],
                            start=_st, stop=_sp))
            if c == 1:
                ssm[h] = prog.op("t", fn, inc=a_mm)
            else:
                prog.op("t", fn)

    def emit_rows(h):
        st_set = h % 2
        prog.wait("v", a_mm, ssm[h])
        mb = prog.op("v", lambda v, _ro=rows[st_set], _ps=ps_misc, _off=st_set * 512 + 256:
                     v.tensor_scalar(_ro[0:1, 0:256], _ps[0:1, _off:_off + 256],
                                     1.0 / 256.0, RMS_EPS, ALU.mult, ALU.add), inc=a_v1)
        prog.wait("v", a_v1, mb)
        wrm[h] = prog.op("v", lambda v, _ro=rows[st_set]:
                         v.reciprocal(_ro[0:1, 256:512], _ro[0:1, 0:256]), inc=a_v1)

    def emit_sqrt(h, p):
        st_set = h % 2
        hp = head_params(h, p)
        prog.wait("s", a_v1, wrm[h])
        rsm[h] = prog.op("s", lambda s_, _ro=rows[st_set], _cp=cpool, _ci=hp["sqrt_cp"]:
                         s_.activation(_ro[0:1, 512:768], _ro[0:1, 256:512], AF.Sqrt,
                                       scale=_cp[0:1, _ci:_ci + 1]), inc=a_s1)

    def emit_repmm(h):
        st_set = h % 2
        prog.wait("t", a_s1, rsm[h])
        repm[h] = prog.op("t", lambda tt, _or=ones_row_f, _ro=rows[st_set],
                          _ps=ps_misc, _off=st_set * 512:
                          tt.matmul(_ps[:, _off:_off + 256], _or[:], _ro[0:1, 512:768],
                                    start=True, stop=True), inc=a_mm)

    def emit_group6(h, p):
        hp = head_params(h, p)
        st_set = h % 2
        prog.wait("v", a_mm, repm[h])
        if h == 0:
            if p == 0:
                prog.wait("v", a_wl, wl_all)
            prog.wait("v", a_cs[p % 2], csl[p])
            if p >= 1:
                prog.wait("v", a_sq, stm[("q", 2 * p - 1)])
                prog.wait("v", a_sk, stm[("k", p - 1)])
        b0, b1 = hp["sc"]
        out = hp["out"]
        ob = hp["oblk"]
        prog.op("v", lambda v, _o=out, _ob=ob, _qf=qf[st_set][1], _sc=hp["scol"],
                _b1=b1, _ps=ps_misc, _off=st_set * 512:
                v.scalar_tensor_tensor(_o[:, (_ob + 1) * 256:(_ob + 2) * 256],
                                       _qf[:], _sc[:, _b1:_b1 + 1],
                                       _ps[:, _off:_off + 256], ALU.mult, ALU.mult))
        mb = prog.op("v", lambda v, _qn=qn[st_set], _qf=qf[st_set][0], _sc=hp["scol"],
                     _b0=b0, _ps=ps_misc, _off=st_set * 512:
                     v.scalar_tensor_tensor(_qn[:], _qf[:], _sc[:, _b0:_b0 + 1],
                                            _ps[:, _off:_off + 256], ALU.mult, ALU.mult),
                     inc=a_v1)
        prog.wait("v", a_v1, mb)
        mb = prog.op("v", lambda v, _sh=shufb[st_set], _qn=qn[st_set], _m=pairswap:
                     v.stream_shuffle(_sh[:], _qn[:], _m), inc=a_v1)
        prog.wait("v", a_v1, mb)
        prog.op("v", lambda v, _a=rt1[st_set], _qn=qn[st_set], _c=ccs[p % 2]:
                v.tensor_tensor(_a[:], _qn[:], _c[:], ALU.mult))
        mb = prog.op("v", lambda v, _b=rt2[st_set], _sh=shufb[st_set], _s=sss[p % 2]:
                     v.tensor_tensor(_b[:], _sh[:], _s[:], ALU.mult), inc=a_v1)
        prog.wait("v", a_v1, mb)
        g6m[h] = prog.op("v", lambda v, _o=out, _ob=ob, _a=rt1[st_set], _b=rt2[st_set]:
                         v.tensor_tensor(_o[:, _ob * 256:(_ob + 1) * 256],
                                         _a[:], _b[:], ALU.add), inc=a_v1)

    NTP = NT // 2
    for p in range(NTP):
        t0, t1 = 2 * p, 2 * p + 1
        if p >= 2:
            prog.wait("y", a_v1, g6m_all[p - 2])
        prog.dma("y", a_cs[p % 2], ccs[p % 2][:], cc_d[:, p * 256:(p + 1) * 256])
        csl[p] = prog.dma("y", a_cs[p % 2], sss[p % 2][:], ss_d[:, p * 256:(p + 1) * 256])
        emit_ln(t0)
        emit_transpose(t0, p)
        emit_ln(t1)
        emit_transpose(t1, p)
        emit_vproj(t0, p)
        emit_vproj(t1, p)
        for h in range(12):
            emit_proj_head(h, p)
            if h >= 1:
                emit_ssmm(h - 1)
            if h >= 2:
                emit_repmm(h - 2)
                emit_group6(h - 2, p)
            emit_evq(h)
            emit_sq(h)
            if h >= 1:
                emit_rows(h - 1)
                emit_sqrt(h - 1, p)
        emit_ssmm(11)
        emit_repmm(10)
        emit_group6(10, p)
        emit_rows(11)
        emit_sqrt(11, p)
        emit_repmm(11)
        emit_group6(11, p)
        g6m_all[p] = g6m[11]
        lffree[p] = qmm[11]
        prog.wait("g", a_v1, g6m[7])
        stm[("q", 2 * p)] = stm[("q", 2 * p + 1)] = prog.dma(
            "g", a_sq, Qs[p * 128:(p + 1) * 128, :], qout[p % 2][:])
        prog.wait("g", a_v1, g6m[11])
        stm[("k", p)] = prog.dma("g", a_sk, Ks[p * 128:(p + 1) * 128, :], kout[p % 2][:])
    if "dbg" in io:
        dbg = io["dbg"]
        dsem = sems["a_dbg"]
        prog.wait("g", a_v1, a_v1.n)
        prog.wait("g", a_s1, a_s1.n)
        prog.wait("g", a_mm, a_mm.n)
        for nm_, buf in (("dbg_x", x_sb[0]), ("dbg_ln", ln_tm[0]), ("dbg_lf", lf[0]),
                         ("dbg_vout", vout[0]), ("dbg_qf0", qf[0][0]), ("dbg_qf1", qf[0][1]),
                         ("dbg_rows", rows[0]), ("dbg_qn", qn[0]), ("dbg_cc", ccs[0]),
                         ("dbg_id", cst["ident_bf"]), ("dbg_cp", cst["cpool"]),
                         ("dbg_st", stats[0]),
                         ("dbg_oc", cst["ones_col"]), ("dbg_or", cst["ones_row_f"])):
            if nm_ in dbg:
                prog.dma("g", dsem, dbg[nm_], buf[:])
        prog.wait("g", dsem, dsem.n)
    return stm


def build_phase_a1_standalone():
    nc = bass.Bass("TRN2", target_bir_lowering=False, debug=False,
                   detect_race_conditions=False)
    x = nc.dram_tensor("x", [S, D], F32, kind="ExternalInput").ap()
    Wq = nc.dram_tensor("Wq", [D, 2048], F32, kind="ExternalInput").ap()
    Wk = nc.dram_tensor("Wk", [D, 1024], F32, kind="ExternalInput").ap()
    Wv = nc.dram_tensor("Wv", [D, 1024], F32, kind="ExternalInput").ap()
    qsT = nc.dram_tensor("qsT", [128, 16], F32, kind="ExternalInput").ap()
    ksT = nc.dram_tensor("ksT", [128, 8], F32, kind="ExternalInput").ap()
    cc_d = nc.dram_tensor("cc", [128, S], F32, kind="ExternalInput").ap()
    ss_d = nc.dram_tensor("ss", [128, S], F32, kind="ExternalInput").ap()
    identb = nc.dram_tensor("identb", [128, 128], BF16, kind="ExternalInput").ap()
    onescol = nc.dram_tensor("onescol", [128, 1], BF16, kind="ExternalInput").ap()
    onesrow = nc.dram_tensor("onesrow", [1, 128], F32, kind="ExternalInput").ap()
    cpool_d = nc.dram_tensor("cpool", [128, 8], F32, kind="ExternalInput").ap()
    Qs = nc.dram_tensor("Qs", [(NT // 2) * 128, 4096], BF16, kind="ExternalOutput").ap()
    Ks = nc.dram_tensor("Ks", [(NT // 2) * 128, 2048], BF16, kind="ExternalOutput").ap()
    Vs = nc.dram_tensor("Vs", [S, 1024], BF16, kind="ExternalOutput").ap()

    c = host_constants()
    with contextlib.ExitStack() as es:
        ident_sb = es.enter_context(nc.sbuf_tensor("c_idb", [128, 128], BF16))
        onesc_sb = es.enter_context(nc.sbuf_tensor("c_oc", [128, 1], BF16))
        onesr_sb = es.enter_context(nc.sbuf_tensor("c_or", [1, 128], F32))
        cpool_sb = es.enter_context(nc.sbuf_tensor("c_cp", [128, 8], F32))
        sems = make_sems(nc, es, ["a_x0", "a_x1", "a_cs0", "a_cs1", "a_wl",
                                  "a_s1", "a_v1", "a_tr", "a_mm",
                                  "a_sq", "a_sk", "a_sv0", "a_sv1", "init"])
        prog = Prog()
        ini = sems["init"]
        prog.dma("y", ini, ident_sb[:], identb)
        prog.dma("y", ini, onesc_sb[:], onescol)
        prog.dma("y", ini, onesr_sb[:], onesrow)
        im = prog.dma("y", ini, cpool_sb[:], cpool_d)
        for e in ("t", "v", "s"):
            prog.wait(e, ini, im)
        io = dict(x=x, Wq=Wq, Wk=Wk, Wv=Wv, qsT=qsT, ksT=ksT, cc=cc_d, ss=ss_d,
                  Qs=Qs, Ks=Ks, Vs=Vs)
        cst = dict(ident_bf=ident_sb, ones_col=onesc_sb, ones_row_f=onesr_sb,
                   cpool=cpool_sb, pairswap=c["pairswap"])
        emit_phase_a1(nc, prog, es, sems, io, cst)
        for nme in ("a_sq", "a_sk", "a_sv0", "a_sv1"):
            prog.wait("g", sems[nme], sems[nme].n)
        with nc.Block("phasea1") as blk:
            prog.install(blk)
    return nc



def emit_phase_a2(nc, prog, es, sems, io, cst):
    x, Qs, Ks, Vs, Wo, X1 = io["x"], io["Qs"], io["Ks"], io["Vs"], io["Wo"], io["X1"]
    ones_col = cst["ones_col"]
    ones_row_f = cst["ones_row_f"]
    cpool = cst["cpool"]
    pmask = cst["pmask"]

    qld = [sems["b_q0"], sems["b_q1"]]
    kld = [sems["b_k0"], sems["b_k1"]]
    vld = [sems["b_v0"], sems["b_v1"]]
    xld = [sems["b_x0"], sems["b_x1"]]
    wol = sems["b_wol"]
    b_s = sems["b_s"]
    b_v = sems["b_v"]
    b_mm = sems["b_mm"]
    x1st = [sems["b_st0"], sems["b_st1"]]

    wo_sb = es.enter_context(nc.sbuf_tensor("b_wo", [128, 16 * 2048], BF16))
    k_ring = [es.enter_context(nc.sbuf_tensor(f"b_k{i}", [128, 1024], BF16)) for i in range(10)]
    v_ring = [es.enter_context(nc.sbuf_tensor(f"b_vr{i}", [128, 1024], BF16)) for i in range(10)]
    q_cur = [es.enter_context(nc.sbuf_tensor(f"b_qc{i}", [128, 4096], BF16)) for i in range(2)]
    x_sb = [es.enter_context(nc.sbuf_tensor(f"b_x{i}", [128, D], F32)) for i in range(4)]
    p_t = [es.enter_context(nc.sbuf_tensor(f"b_pt{i}", [128, 2560], BF16)) for i in range(2)]
    attn_out = es.enter_context(nc.sbuf_tensor("b_ao", [128, 16 * 256], BF16))
    r_sb = [es.enter_context(nc.sbuf_tensor(f"b_r{i}", [128, 256], F32)) for i in range(2)]
    rrow = [es.enter_context(nc.sbuf_tensor(f"b_rr{i}", [1, 2 * 256], F32)) for i in range(2)]
    x1_sb = [es.enter_context(nc.sbuf_tensor(f"b_x1{i}", [128, D], F32)) for i in range(2)]
    ps = es.enter_context(nc.psum_tensor("b_ps", [128, 4096], F32))

    PV0 = 2560
    REP0 = 3072
    SUM0 = 3328
    WO0 = 3584
    NP = NT // 2

    for dc in range(16):
        wlm = prog.dma("g", wol, wo_sb[:, dc * 2048:(dc + 1) * 2048],
                       Wo[dc * 128:(dc + 1) * 128, :])
    wol_all = wlm

    qm, km, vm, xm = {}, {}, {}, {}
    scm, summ, repm, pvm, wom = {}, {}, {}, {}, {}
    expm, maskm, rcm, amm, lnm2 = {}, {}, {}, {}, {}
    woev, stx = {}, {}

    def nb_of(pp):
        return min(2 * pp + 2, 10)

    def emit_loads(pp):
        if pp >= NP:
            return
        sl = pp % 2
        if pp >= 2:
            prog.wait("y", b_mm, scm[(pp - 2, 7)])
        qm[pp] = prog.dma("y", qld[sl], q_cur[sl][:], Qs[pp * 128:(pp + 1) * 128, :])
        for side in range(2):
            t = 2 * pp + side
            rs = t % 10
            if pp >= 5:
                prog.wait("y", b_mm, scm[(pp - 1, 7)])
                prog.wait("y", b_mm, pvm[(pp - 1, 7)])
            ksrc = ap2(Ks.tensor, pp * 128 * 2048 + side * 128,
                       [[2048, 128], [256, 8], [1, 128]])
            km[t] = prog.dma("y", kld[sl], k_ring[rs][:], ksrc)
            vm[t] = prog.dma("y", vld[sl], v_ring[rs][:], Vs[t * 128:(t + 1) * 128, :])
            if pp >= 2:
                prog.wait("y", b_v, woev[(pp - 2, side, 3)])
            xm[t] = prog.dma("y", xld[sl], x_sb[t % 4][:], x[t * 128:(t + 1) * 128, :])

    def emit_scores(pp, h):
        nb = nb_of(pp)
        base = 2 * pp + 1 - (nb - 1)
        kv = h // 2
        if h == 0:
            prog.wait("t", qld[pp % 2], qm[pp])
            prog.wait("t", kld[pp % 2], km[2 * pp + 1])
        if (pp, h - 1) in expm:
            prog.wait("t", b_s, expm[(pp, h - 1)])
        elif pp >= 1 and (pp - 1, 7) in expm:
            prog.wait("t", b_s, expm[(pp - 1, 7)])
        for m in range(nb):
            slot = (base + m) % 10
            for c in range(2):
                fn = (lambda t, _ps=ps, _o=m * 256, _k=k_ring[slot],
                      _koff=(2 * kv + c) * 128, _q=q_cur[pp % 2],
                      _qoff=(2 * h + c) * 256, _st=(c == 0), _sp=(c == 1):
                      t.matmul(_ps[:, _o:_o + 256], _k[:, _koff:_koff + 128],
                               _q[:, _qoff:_qoff + 256], start=_st, stop=_sp))
                if m == nb - 1 and c == 1:
                    scm[(pp, h)] = prog.op("t", fn, inc=b_mm)
                else:
                    prog.op("t", fn)

    def emit_tanh(pp, h):
        nb = nb_of(pp)
        w = nb * 256
        prog.wait("s", b_mm, scm[(pp, h)])
        m1 = prog.op("s", lambda s_, _ps=ps, _w=w, _cp=cpool:
                     s_.activation(_ps[:, 0:_w], _ps[:, 0:_w], AF.Tanh,
                                   scale=_cp[:, CP_TANH:CP_TANH + 1]), inc=b_s)
        prog.wait("s", b_s, m1)

    def emit_exp(pp, h):
        nb = nb_of(pp)
        w = nb * 256
        if (pp, h - 2) in pvm:
            prog.wait("s", b_mm, pvm[(pp, h - 2)])
        elif pp >= 1 and (pp - 1, h + 6) in pvm:
            prog.wait("s", b_mm, pvm[(pp - 1, h + 6)])
        expm[(pp, h)] = prog.op("s", lambda s_, _ps=ps, _w=w, _pt=p_t[h % 2], _cp=cpool:
                                s_.activation(_pt[:, 0:_w], _ps[:, 0:_w], AF.Exp,
                                              scale=_cp[:, CP_EXP50:CP_EXP50 + 1]),
                                inc=b_s)

    def emit_lnrow(pp, h):
        sl = h % 2
        prog.wait("s", b_mm, summ[(pp, h)])
        if (pp, h - 2) in repm:
            prog.wait("s", b_mm, repm[(pp, h - 2)])
        elif pp >= 1 and (pp - 1, h + 6) in repm:
            prog.wait("s", b_mm, repm[(pp - 1, h + 6)])
        m1 = prog.op("s", lambda s_, _ps=ps, _ro=rrow[sl]:
                     s_.activation(_ro[0:1, 0:256], _ps[0:1, SUM0:SUM0 + 256], AF.Ln),
                     inc=b_s)
        prog.wait("s", b_s, m1)
        lnm2[(pp, h)] = prog.op("s", lambda s_, _ro=rrow[sl], _cp=cpool:
                                s_.activation(_ro[0:1, 256:512], _ro[0:1, 0:256], AF.Exp,
                                              scale=_cp[0:1, CP_NEG1:CP_NEG1 + 1]),
                                inc=b_s)

    def emit_mask(pp, h):
        nb = nb_of(pp)
        prog.wait("v", b_s, expm[(pp, h)])
        mlist = []
        if nb == 10:
            mlist = [(0, 0), (1, 1), (8, 2), (9, 3)]
        else:
            mlist = [(nb - 2, 2), (nb - 1, 3)]
        for idx, (m, mi) in enumerate(mlist):
            fn = (lambda v, _pt=p_t[h % 2], _m=pmask[mi], _o=m * 256:
                  v.tensor_tensor(_pt[:, _o:_o + 256], _pt[:, _o:_o + 256],
                                  _m[:], ALU.mult))
            if idx == len(mlist) - 1:
                maskm[(pp, h)] = prog.op("v", fn, inc=b_v)
            else:
                prog.op("v", fn)

    def emit_sums(pp, h):
        nb = nb_of(pp)
        prog.wait("t", b_v, maskm[(pp, h)])
        if (pp, h - 1) in rcm:
            prog.wait("t", b_v, rcm[(pp, h - 1)])
        elif pp >= 1 and (pp - 1, 7) in rcm:
            prog.wait("t", b_v, rcm[(pp - 1, 7)])
        for m in range(nb):
            fn = (lambda t, _ps=ps, _pt=p_t[h % 2], _m=m, _oc=ones_col,
                  _st=(m == 0), _sp=(m == nb - 1):
                  t.matmul(_ps[0:1, SUM0:SUM0 + 256], _oc[:],
                           _pt[:, _m * 256:(_m + 1) * 256], start=_st, stop=_sp))
            if m == nb - 1:
                summ[(pp, h)] = prog.op("t", fn, inc=b_mm)
            else:
                prog.op("t", fn)

    def emit_rep(pp, h):
        prog.wait("t", b_s, lnm2[(pp, h)])
        repm[(pp, h)] = prog.op("t", lambda t, _ps=ps, _or=ones_row_f, _ro=rrow[h % 2]:
                                t.matmul(_ps[:, REP0:REP0 + 256], _or[:],
                                         _ro[0:1, 256:512], start=True, stop=True),
                                inc=b_mm)

    def emit_pv(pp, h):
        nb = nb_of(pp)
        base = 2 * pp + 1 - (nb - 1)
        kv = h // 2
        if h == 0:
            prog.wait("t", vld[pp % 2], vm[2 * pp + 1])
        if (pp, h - 1) in amm:
            prog.wait("t", b_v, amm[(pp, h - 1)])
        elif pp >= 1 and (pp - 1, 7) in amm:
            prog.wait("t", b_v, amm[(pp - 1, 7)])
        for c in range(2):
            for m in range(nb):
                slot = (base + m) % 10
                fn = (lambda t, _ps=ps, _v=v_ring[slot], _voff=kv * 256 + c * 128,
                      _pt=p_t[h % 2], _m=m, _c=c, _st=(m == 0), _sp=(m == nb - 1):
                      t.matmul(_ps[:, PV0 + _c * 256:PV0 + (_c + 1) * 256],
                               _v[:, _voff:_voff + 128],
                               _pt[:, _m * 256:(_m + 1) * 256], start=_st, stop=_sp))
                if m == nb - 1 and c == 1:
                    pvm[(pp, h)] = prog.op("t", fn, inc=b_mm)
                else:
                    prog.op("t", fn)

    def emit_rcopy(pp, h):
        prog.wait("v", b_mm, repm[(pp, h)])
        mb = prog.op("v", lambda v, _r=r_sb[h % 2], _ps=ps:
                     v.tensor_copy(_r[:], _ps[:, REP0:REP0 + 256]), inc=b_v)
        prog.wait("v", b_v, mb)
        rcm[(pp, h)] = mb

    def emit_attnmul(pp, h):
        prog.wait("v", b_mm, pvm[(pp, h)])
        if h == 0 and (pp - 1, 7) in wom:
            prog.wait("v", b_mm, wom[(pp - 1, 7)])
        for c in range(2):
            fn = (lambda v, _ao=attn_out, _o=(2 * h + c) * 256, _ps=ps, _c=c,
                  _r=r_sb[h % 2]:
                  v.tensor_tensor(_ao[:, _o:_o + 256],
                                  _ps[:, PV0 + _c * 256:PV0 + (_c + 1) * 256],
                                  _r[:], ALU.mult))
            if c == 1:
                amm[(pp, h)] = prog.op("v", fn, inc=b_v)
            else:
                prog.op("v", fn)

    def emit_wo_quarter(pp, side, qtr):
        qq = side * 4 + qtr
        if qq == 0:
            prog.wait("t", b_v, amm[(pp, 7)])
        else:
            prog.wait("t", b_v, woev[(pp, (qq - 1) // 4, (qq - 1) % 4)])
        for dc in range(16):
            fn = (lambda t, _ps=ps, _ao=attn_out, _dc=dc, _w=wo_sb, _q=qtr, _s=side,
                  _st=(dc == 0), _sp=(dc == 15):
                  t.matmul(_ps[:, WO0:WO0 + 512],
                           _ao[:, _dc * 256 + _s * 128:_dc * 256 + _s * 128 + 128],
                           _w[:, _dc * 2048 + _q * 512:_dc * 2048 + (_q + 1) * 512],
                           start=_st, stop=_sp))
            if dc == 15:
                wom[(pp, qq)] = prog.op("t", fn, inc=b_mm)
            else:
                prog.op("t", fn)
        prog.wait("v", b_mm, wom[(pp, qq)])
        t = 2 * pp + side
        if qq == 0 or qq == 4:
            prog.wait("v", xld[pp % 2], xm[t])
            if pp >= 2:
                prog.wait("v", x1st[pp % 2], stx[(pp - 2, side)])
        woev[(pp, side, qtr)] = prog.op(
            "v", lambda v, _x1=x1_sb[side], _x=x_sb[t % 4], _ps=ps, _q=qtr:
            v.tensor_tensor(_x1[:, _q * 512:(_q + 1) * 512], _ps[:, WO0:WO0 + 512],
                            _x[:, _q * 512:(_q + 1) * 512], ALU.add), inc=b_v)

    prog.wait("t", wol, wol_all)
    emit_loads(0)
    for pp in range(NP):
        for h in range(8):
            emit_scores(pp, h)
            if h >= 1:
                emit_sums(pp, h - 1)
            emit_tanh(pp, h)
            if h >= 1:
                emit_lnrow(pp, h - 1)
            emit_exp(pp, h)
            if h >= 1:
                emit_rep(pp, h - 1)
                emit_pv(pp, h - 1)
                emit_rcopy(pp, h - 1)
                emit_attnmul(pp, h - 1)
            emit_mask(pp, h)
        emit_sums(pp, 7)
        emit_lnrow(pp, 7)
        emit_rep(pp, 7)
        emit_pv(pp, 7)
        emit_rcopy(pp, 7)
        emit_attnmul(pp, 7)
        for side in range(2):
            for qtr in range(4):
                emit_wo_quarter(pp, side, qtr)
            prog.wait("g", b_v, woev[(pp, side, 3)])
            stx[(pp, side)] = prog.dma("g", x1st[pp % 2],
                                       X1[(2 * pp + side) * 128:(2 * pp + side + 1) * 128, :],
                                       x1_sb[side][:])
        emit_loads(pp + 1)
    return stx


def build_phase_a2_standalone():
    nc = bass.Bass("TRN2", target_bir_lowering=False, debug=False,
                   detect_race_conditions=False)
    x = nc.dram_tensor("x", [S, D], F32, kind="ExternalInput").ap()
    Qs = nc.dram_tensor("Qs", [(NT // 2) * 128, 4096], BF16, kind="ExternalInput").ap()
    Ks = nc.dram_tensor("Ks", [(NT // 2) * 128, 2048], BF16, kind="ExternalInput").ap()
    Vs = nc.dram_tensor("Vs", [S, 1024], BF16, kind="ExternalInput").ap()
    Wo = nc.dram_tensor("Wo", [2048, D], F32, kind="ExternalInput").ap()
    onescol = nc.dram_tensor("onescol", [128, 1], BF16, kind="ExternalInput").ap()
    onesrow = nc.dram_tensor("onesrow", [1, 128], F32, kind="ExternalInput").ap()
    cpool_d = nc.dram_tensor("cpool", [128, 8], F32, kind="ExternalInput").ap()
    pmasks_d = [nc.dram_tensor(f"pm{i}", [128, 256], BF16, kind="ExternalInput").ap()
                for i in range(4)]
    X1 = nc.dram_tensor("X1", [S, D], F32, kind="ExternalOutput").ap()

    with contextlib.ExitStack() as es:
        onesc_sb = es.enter_context(nc.sbuf_tensor("c_oc", [128, 1], BF16))
        onesr_sb = es.enter_context(nc.sbuf_tensor("c_or", [1, 128], F32))
        cpool_sb = es.enter_context(nc.sbuf_tensor("c_cp", [128, 8], F32))
        pmask_sb = [es.enter_context(nc.sbuf_tensor(f"c_pm{i}", [128, 256], BF16))
                    for i in range(4)]
        sems = make_sems(nc, es, ["b_q0", "b_q1", "b_k0", "b_k1", "b_v0", "b_v1",
                                  "b_x0", "b_x1", "b_wol", "b_s", "b_v", "b_mm",
                                  "b_st0", "b_st1", "init"])
        prog = Prog()
        ini = sems["init"]
        prog.dma("y", ini, onesc_sb[:], onescol)
        prog.dma("y", ini, onesr_sb[:], onesrow)
        im = prog.dma("y", ini, cpool_sb[:], cpool_d)
        for i in range(4):
            im = prog.dma("y", ini, pmask_sb[i][:], pmasks_d[i])
        for e in ("t", "v", "s"):
            prog.wait(e, ini, im)
        io = dict(x=x, Qs=Qs, Ks=Ks, Vs=Vs, Wo=Wo, X1=X1)
        cst = dict(ones_col=onesc_sb, ones_row_f=onesr_sb, cpool=cpool_sb,
                   pmask=pmask_sb)
        emit_phase_a2(nc, prog, es, sems, io, cst)
        for nme in ("b_st0", "b_st1"):
            prog.wait("g", sems[nme], sems[nme].n)
        with nc.Block("phasea2") as blk:
            prog.install(blk)
    return nc



def emit_phase_f(nc, prog, es, sems, io, cst):
    X1, W1T, W3T, W2T, Y = io["X1"], io["W1T"], io["W3T"], io["W2T"], io["Y"]
    ident_f = cst["ident_f"]
    h128_sb = cst["h128"]
    cpool = cst["cpool"]

    x1ld = [sems["f_x0"], sems["f_x1"]]
    x1rld = [sems["f_xr0"], sems["f_xr1"]]
    w1ld = [sems["f_w10"], sems["f_w11"]]
    w3ld = [sems["f_w30"], sems["f_w31"]]
    w2ld = [sems["f_w20"], sems["f_w21"]]
    f_s = sems["f_s"]
    f_v = sems["f_v"]
    f_mm = sems["f_mm"]
    ost = [sems["f_o0"], sems["f_o1"]]

    w1b = [es.enter_context(nc.sbuf_tensor(f"f_w1{i}", [128, 16 * 256], BF16)) for i in range(2)]
    w3b = [es.enter_context(nc.sbuf_tensor(f"f_w3{i}", [128, 16 * 256], BF16)) for i in range(2)]
    w2b = [es.enter_context(nc.sbuf_tensor(f"f_w2{i}", [128, 4 * 1024], BF16)) for i in range(2)]
    hidden = es.enter_context(nc.sbuf_tensor("f_hid", [128, 64 * 512], BF16))
    ln2fm = es.enter_context(nc.sbuf_tensor("f_lf", [128, 16 * 512], BF16))
    x1_ln = [es.enter_context(nc.sbuf_tensor(f"f_xl{i}", [128, D], F32)) for i in range(2)]
    x1_res = [es.enter_context(nc.sbuf_tensor(f"f_xr{i}", [128, 1024], F32)) for i in range(2)]
    ln_tm = [es.enter_context(nc.sbuf_tensor(f"f_ln{i}", [128, D], F32)) for i in range(2)]
    stats = [es.enter_context(nc.sbuf_tensor(f"f_stt{i}", [128, 16], F32)) for i in range(2)]
    gs_sb = [es.enter_context(nc.sbuf_tensor(f"f_gs{i}", [128, 512], BF16)) for i in range(2)]
    out_sb = [es.enter_context(nc.sbuf_tensor(f"f_out{i}", [128, 1024], F32)) for i in range(2)]
    ps = es.enter_context(nc.psum_tensor("f_ps", [128, 4096], F32))

    NTB = NT // 4

    x1m, lnm, redm = {}, {}, {}
    trm, evm = {}, {}
    gum, silm, mulm = {}, {}, {}
    h2m, h2ev = {}, {}
    w1m, w3m, w2m = {}, {}, {}
    dnm, devm, stm = {}, {}, {}
    lffull = {}

    def emit_ln2(b, t_loc):
        t = b * 4 + t_loc
        sl = t % 2
        if t >= 2:
            prog.wait("y", f_s, lnm[t - 2])
            prog.wait("y", f_v, redm[t - 2])
        xm = prog.dma("y", x1ld[sl], x1_ln[sl][:], X1[t * 128:(t + 1) * 128, :])
        st_ = stats[sl]
        prog.wait("s", x1ld[sl], xm)
        if t >= 2:
            prog.wait("s", f_mm, trm[(t - 2, 3)])
        sqm_t = prog.op("s", lambda s_, _x=x1_ln[sl], _d=ln_tm[sl], _a=st_:
                        s_.activation(_d[:], _x[:], AF.Square, accum_out=_a[:, 1:2]),
                        inc=f_s)
        prog.wait("v", x1ld[sl], xm)
        redm[t] = prog.op("v", lambda v, _x=x1_ln[sl], _a=st_:
                          v.tensor_reduce(_a[:, 0:1], _x[:], AX.X, ALU.add), inc=f_v)
        prog.wait("v", f_v, redm[t])
        mb = prog.op("v", lambda v, _a=st_:
                     v.tensor_scalar(_a[:, 2:3], _a[:, 0:1], 1.0 / D, None, ALU.mult),
                     inc=f_v)
        prog.wait("v", f_s, sqm_t)
        prog.op("v", lambda v, _a=st_:
                v.tensor_scalar(_a[:, 3:4], _a[:, 1:2], 1.0 / D, None, ALU.mult))
        prog.wait("v", f_v, mb)
        mb = prog.op("v", lambda v, _a=st_:
                     v.tensor_tensor(_a[:, 4:5], _a[:, 2:3], _a[:, 2:3], ALU.mult),
                     inc=f_v)
        prog.wait("v", f_v, mb)
        mb = prog.op("v", lambda v, _a=st_:
                     v.tensor_tensor(_a[:, 5:6], _a[:, 3:4], _a[:, 4:5], ALU.subtract),
                     inc=f_v)
        prog.wait("v", f_v, mb)
        mb = prog.op("v", lambda v, _a=st_:
                     v.tensor_scalar(_a[:, 6:7], _a[:, 5:6], LN_EPS, None, ALU.add),
                     inc=f_v)
        prog.wait("v", f_v, mb)
        vm = prog.op("v", lambda v, _a=st_:
                     v.reciprocal(_a[:, 7:8], _a[:, 6:7]), inc=f_v)
        prog.wait("s", f_v, vm)
        m1 = prog.op("s", lambda s_, _a=st_:
                     s_.activation(_a[:, 8:9], _a[:, 7:8], AF.Sqrt), inc=f_s)
        prog.wait("s", f_s, m1)
        m2 = prog.op("s", lambda s_, _a=st_:
                     s_.activation(_a[:, 9:10], _a[:, 2:3], AF.Identity,
                                   scale=_a[:, 8:9]), inc=f_s)
        prog.wait("s", f_s, m2)
        m3 = prog.op("s", lambda s_, _a=st_, _cp=cpool:
                     s_.activation(_a[:, 10:11], _a[:, 9:10], AF.Identity,
                                   scale=_cp[:, CP_NEG1:CP_NEG1 + 1]), inc=f_s)
        prog.wait("s", f_s, m3)
        lnm[t] = prog.op("s", lambda s_, _x=x1_ln[sl], _d=ln_tm[sl], _a=st_:
                         s_.activation(_d[:], _x[:], AF.Identity, scale=_a[:, 8:9],
                                       bias=_a[:, 10:11]), inc=f_s)

    def emit_transpose2(b, t_loc):
        t = b * 4 + t_loc
        sl = t % 2
        prog.wait("t", f_s, lnm[t])
        if b >= 1 and t_loc == 0:
            prog.wait("t", f_v, devm[(b - 1, 1, 0)])
        for grp in range(4):
            gseq = t * 4 + grp
            if gseq >= 2:
                prog.wait("t", f_v, evm[gseq - 2])
            for k in range(4):
                dc = grp * 4 + k
                fn = (lambda tt, _ps=ps, _o=(gseq % 2) * 512 + k * 128, _ln=ln_tm[sl],
                      _dc=dc, _id=ident_f:
                      tt.transpose(_ps[:, _o:_o + 128],
                                   _ln[:, _dc * 128:(_dc + 1) * 128], _id[:]))
                if k == 3:
                    trm[(t, grp)] = prog.op("t", fn, inc=f_mm)
                else:
                    prog.op("t", fn)
            prog.wait("v", f_mm, trm[(t, grp)])
            if b >= 1 and t_loc == 0 and grp == 0:
                prog.wait("v", f_mm, gum[(b - 1, "u", 63)])
            dst = ap2(ln2fm, grp * 4 * 512 + t_loc * 128,
                      [[16 * 512, 128], [512, 4], [1, 128]])
            evm[gseq] = prog.op(
                "v", lambda v, _d=dst, _ps=ps, _o=(gseq % 2) * 512:
                v.tensor_copy(_d, _ps[:, _o:_o + 512]), inc=f_v)
        if t_loc == 3:
            lffull[b] = evm[t * 4 + 3]

    def emit_w13_load(b, hc):
        sl = hc % 2
        seq = b * 32 + hc
        if seq >= 2:
            pb, phc = (b, hc - 2) if hc >= 2 else (b - 1, hc + 30)
            prog.wait("y", f_mm, gum[(pb, "g", phc * 2 + 1)])
            prog.wait("y", f_mm, gum[(pb, "u", phc * 2 + 1)])
        src1 = ap2(W1T.tensor, hc * 256, [[HID, 128], [128 * HID, 16], [1, 256]])
        w1m[(b, hc)] = prog.dma("y", w1ld[sl], w1b[sl][:], src1)
        src3 = ap2(W3T.tensor, hc * 256, [[HID, 128], [128 * HID, 16], [1, 256]])
        w3m[(b, hc)] = prog.dma("y", w3ld[sl], w3b[sl][:], src3)

    def emit_gu_block(b, hc, hs):
        cblk = hc * 2 + hs
        sl = hc % 2
        par = cblk % 2
        prog.wait("t", w1ld[sl], w1m[(b, hc)])
        prog.wait("t", w3ld[sl], w3m[(b, hc)])
        if hc == 0 and hs == 0:
            prog.wait("t", f_v, lffull[b])
            if b >= 1:
                prog.wait("t", f_v, devm[(b - 1, 1, 1)])
        seq = b * 64 + cblk
        if seq >= 2:
            pb, pc = (b, cblk - 2) if cblk >= 2 else (b - 1, cblk + 62)
            prog.wait("t", f_s, silm[(pb, pc)])
            prog.wait("t", f_v, mulm[(pb, pc)])
        for dc in range(16):
            fn = (lambda tt, _ps=ps, _o=par * 512, _w=w1b[sl], _dc=dc, _hs=hs,
                  _lf=ln2fm, _st=(dc == 0), _sp=(dc == 15):
                  tt.matmul(_ps[:, _o:_o + 512],
                            _w[:, _dc * 256 + _hs * 128:_dc * 256 + _hs * 128 + 128],
                            _lf[:, _dc * 512:(_dc + 1) * 512], start=_st, stop=_sp))
            if dc == 15:
                gum[(b, "g", cblk)] = prog.op("t", fn, inc=f_mm)
            else:
                prog.op("t", fn)
        for dc in range(16):
            fn = (lambda tt, _ps=ps, _o=1024 + par * 512, _w=w3b[sl], _dc=dc, _hs=hs,
                  _lf=ln2fm, _st=(dc == 0), _sp=(dc == 15):
                  tt.matmul(_ps[:, _o:_o + 512],
                            _w[:, _dc * 256 + _hs * 128:_dc * 256 + _hs * 128 + 128],
                            _lf[:, _dc * 512:(_dc + 1) * 512], start=_st, stop=_sp))
            if dc == 15:
                gum[(b, "u", cblk)] = prog.op("t", fn, inc=f_mm)
            else:
                prog.op("t", fn)
        prog.wait("s", f_mm, gum[(b, "g", cblk)])
        if seq >= 2:
            pb, pc = (b, cblk - 2) if cblk >= 2 else (b - 1, cblk + 62)
            prog.wait("s", f_v, mulm[(pb, pc)])
        silm[(b, cblk)] = prog.op("s", lambda s_, _ps=ps, _o=par * 512, _g=gs_sb[par]:
                                  s_.activation(_g[:], _ps[:, _o:_o + 512], AF.Silu),
                                  inc=f_s)
        prog.wait("v", f_mm, gum[(b, "u", cblk)])
        prog.wait("v", f_s, silm[(b, cblk)])
        if b >= 1 and cblk == 0:
            prog.wait("v", f_mm, dnm[(b - 1, 1, 15)])
        mulm[(b, cblk)] = prog.op(
            "v", lambda v, _h=hidden, _c=cblk, _ps=ps, _o=1024 + par * 512, _g=gs_sb[par]:
            v.tensor_tensor(_h[:, _c * 512:(_c + 1) * 512], _ps[:, _o:_o + 512],
                            _g[:], ALU.mult), inc=f_v)

    def emit_h2(b, c):
        par = c % 2
        prog.wait("t", f_v, mulm[(b, c)])
        if c == 0 and b >= 1:
            prog.wait("t", f_v, devm[(b - 1, 1, 2)])
        seq = b * 64 + c
        if seq >= 2:
            pb, pc = (b, c - 2) if c >= 2 else (b - 1, c + 62)
            prog.wait("t", f_v, h2ev[(pb, pc)])
        h2m[(b, c)] = prog.op("t", lambda tt, _ps=ps, _o=2048 + par * 512, _h=hidden,
                              _c=c, _hk=h128_sb:
                              tt.matmul(_ps[:, _o:_o + 512], _hk[:],
                                        _h[:, _c * 512:(_c + 1) * 512],
                                        start=True, stop=True), inc=f_mm)
        prog.wait("v", f_mm, h2m[(b, c)])
        h2ev[(b, c)] = prog.op("v", lambda v, _ps=ps, _o=2048 + par * 512, _h=hidden,
                               _c=c:
                               v.tensor_copy(_h[:, _c * 512:(_c + 1) * 512],
                                             _ps[:, _o:_o + 512]), inc=f_v)

    def emit_w2_load(b, half, cg):
        sl = cg % 2
        seq = (b * 2 + half) * 16 + cg
        if seq >= 2:
            pb, ph, pcg = b, half, cg - 2
            if cg < 2:
                ph = half - 1
                pcg = cg + 14
                if ph < 0:
                    pb, ph = b - 1, 1
            prog.wait("y", f_mm, dnm[(pb, ph, pcg)])
        src = ap2(W2T.tensor, (cg * 4) * 128 * D + half * 1024,
                  [[D, 128], [128 * D, 4], [1, 1024]])
        w2m[(b, half, cg)] = prog.dma("y", w2ld[sl], w2b[sl][:], src)

    def emit_down_group(b, half, cg):
        sl = cg % 2
        prog.wait("t", w2ld[sl], w2m[(b, half, cg)])
        if cg == 0:
            if half == 0:
                prog.wait("t", f_v, h2ev[(b, 63)])
                if b >= 1:
                    prog.wait("t", f_v, devm[(b - 1, 1, 3)])
            else:
                for tl in range(4):
                    prog.wait("t", f_v, devm[(b, 0, tl)])
        for t_loc in range(4):
            for j in range(2):
                for c4 in range(4):
                    c = cg * 4 + c4
                    is_last = (c4 == 3 and t_loc == 3 and j == 1)
                    fn = (lambda tt, _ps=ps, _o=t_loc * 1024 + j * 512, _h=hidden,
                          _hoff=c * 512 + t_loc * 128, _w=w2b[sl],
                          _woff=c4 * 1024 + j * 512, _st=(c == 0), _sp=(c == 63):
                          tt.matmul(_ps[:, _o:_o + 512],
                                    _h[:, _hoff:_hoff + 128],
                                    _w[:, _woff:_woff + 512], start=_st, stop=_sp))
                    if is_last:
                        dnm[(b, half, cg)] = prog.op("t", fn, inc=f_mm)
                    else:
                        prog.op("t", fn)

    xresm = {}

    def emit_x1res_load(b, half, t_loc):
        t = b * 4 + t_loc
        sl = t % 2
        seq = (b * 2 + half) * 4 + t_loc
        if seq >= 2:
            pb, ph, pt = b, half, t_loc - 2
            if t_loc < 2:
                ph = half - 1
                pt = t_loc + 2
                if ph < 0:
                    pb, ph = b - 1, 1
            prog.wait("y", f_v, devm[(pb, ph, pt)])
        src = ap2(X1.tensor, t * 128 * D + half * 1024, [[D, 128], [1, 1024]])
        xresm[(b, half, t_loc)] = prog.dma("y", x1rld[sl], x1_res[sl][:], src)

    def emit_down_evac(b, half, t_loc):
        t = b * 4 + t_loc
        sl = t % 2
        prog.wait("v", f_mm, dnm[(b, half, 15)])
        prog.wait("v", x1rld[sl], xresm[(b, half, t_loc)])
        seq = (b * 2 + half) * 4 + t_loc
        if seq >= 2:
            pb, ph, pt = b, half, t_loc - 2
            if t_loc < 2:
                ph = half - 1
                pt = t_loc + 2
                if ph < 0:
                    pb, ph = b - 1, 1
            prog.wait("v", ost[sl], stm[(pb, ph, pt)])
        devm[(b, half, t_loc)] = prog.op(
            "v", lambda v, _o=out_sb[sl], _ps=ps, _po=t_loc * 1024, _x=x1_res[sl]:
            v.tensor_tensor(_o[:], _ps[:, _po:_po + 1024], _x[:], ALU.add), inc=f_v)
        prog.wait("g", f_v, devm[(b, half, t_loc)])
        dst = ap2(Y.tensor, t * 128 * D + half * 1024, [[D, 128], [1, 1024]])
        stm[(b, half, t_loc)] = prog.dma("g", ost[sl], dst, out_sb[sl][:])

    for b in range(NTB):
        for t_loc in range(4):
            emit_ln2(b, t_loc)
            emit_transpose2(b, t_loc)
        emit_w13_load(b, 0)
        emit_w13_load(b, 1)
        for hc in range(32):
            emit_gu_block(b, hc, 0)
            emit_gu_block(b, hc, 1)
            if hc + 2 < 32:
                emit_w13_load(b, hc + 2)
        for c in range(64):
            emit_h2(b, c)
        for half in range(2):
            emit_w2_load(b, half, 0)
            for cg in range(16):
                emit_down_group(b, half, cg)
                if cg + 1 < 16:
                    emit_w2_load(b, half, cg + 1)
                elif half == 0:
                    emit_w2_load(b, 1, 0)
                if cg >= 14:
                    emit_x1res_load(b, half, cg - 14)
            for t_loc in range(4):
                emit_down_evac(b, half, t_loc)
                if t_loc + 2 < 4:
                    emit_x1res_load(b, half, t_loc + 2)
    return stm


def build_phase_f_standalone():
    nc = bass.Bass("TRN2", target_bir_lowering=False, debug=False,
                   detect_race_conditions=False)
    X1 = nc.dram_tensor("X1", [S, D], F32, kind="ExternalInput").ap()
    W1T = nc.dram_tensor("W1T", [D, HID], BF16, kind="ExternalInput").ap()
    W3T = nc.dram_tensor("W3T", [D, HID], BF16, kind="ExternalInput").ap()
    W2T = nc.dram_tensor("W2T", [HID, D], BF16, kind="ExternalInput").ap()
    identf = nc.dram_tensor("identf", [128, 128], F32, kind="ExternalInput").ap()
    h128 = nc.dram_tensor("h128", [128, 128], BF16, kind="ExternalInput").ap()
    cpool_d = nc.dram_tensor("cpool", [128, 8], F32, kind="ExternalInput").ap()
    Y = nc.dram_tensor("Y", [S, D], F32, kind="ExternalOutput").ap()
    with contextlib.ExitStack() as es:
        identf_sb = es.enter_context(nc.sbuf_tensor("c_idf", [128, 128], F32))
        h128_sb = es.enter_context(nc.sbuf_tensor("c_h128", [128, 128], BF16))
        cpool_sb = es.enter_context(nc.sbuf_tensor("c_cp", [128, 8], F32))
        sems = make_sems(nc, es, F_SEMS + ["init"])
        prog = Prog()
        ini = sems["init"]
        prog.dma("y", ini, identf_sb[:], identf)
        prog.dma("y", ini, h128_sb[:], h128)
        im = prog.dma("y", ini, cpool_sb[:], cpool_d)
        for e in ("t", "v", "s"):
            prog.wait(e, ini, im)
        io = dict(X1=X1, W1T=W1T, W3T=W3T, W2T=W2T, Y=Y)
        cst = dict(ident_f=identf_sb, h128=h128_sb, cpool=cpool_sb)
        emit_phase_f(nc, prog, es, sems, io, cst)
        for nme in ("f_o0", "f_o1"):
            prog.wait("g", sems[nme], sems[nme].n)
        with nc.Block("phasef") as blk:
            prog.install(blk)
    return nc



P_SEMS = ["p_ld0", "p_ld1", "p_w2ld", "p_tr", "p_ev", "p_st0", "p_st1",
          "p_st20", "p_st21", "p_bfv", "p_bfg", "p_bfs"]
A1_SEMS = ["a_x0", "a_x1", "a_cs0", "a_cs1", "a_wl", "a_s1", "a_v1", "a_tr",
           "a_mm", "a_sq", "a_sk", "a_sv0", "a_sv1"]
A2_SEMS = ["b_q0", "b_q1", "b_k0", "b_k1", "b_v0", "b_v1", "b_x0", "b_x1",
           "b_wol", "b_s", "b_v", "b_mm", "b_st0", "b_st1"]
F_SEMS = ["f_x0", "f_x1", "f_xr0", "f_xr1", "f_w10", "f_w11", "f_w30", "f_w31",
          "f_w20", "f_w21", "f_s", "f_v", "f_mm", "f_o0", "f_o1"]


def entry_waits(prog, sems, names):
    for eng in ENGS:
        for n in names:
            prog.wait(eng, sems[n], sems[n].n)


def build_full():
    nc = bass.Bass("TRN2", target_bir_lowering=False, debug=False,
                   detect_race_conditions=False)
    x = nc.dram_tensor("x", [S, D], F32, kind="ExternalInput").ap()
    Wq = nc.dram_tensor("Wq", [D, 2048], F32, kind="ExternalInput").ap()
    Wk = nc.dram_tensor("Wk", [D, 1024], F32, kind="ExternalInput").ap()
    Wv = nc.dram_tensor("Wv", [D, 1024], F32, kind="ExternalInput").ap()
    Wo = nc.dram_tensor("Wo", [2048, D], F32, kind="ExternalInput").ap()
    W1 = nc.dram_tensor("W1", [HID, D], F32, kind="ExternalInput").ap()
    W2 = nc.dram_tensor("W2", [D, HID], F32, kind="ExternalInput").ap()
    W3 = nc.dram_tensor("W3", [HID, D], F32, kind="ExternalInput").ap()
    qsT = nc.dram_tensor("qsT", [128, 16], F32, kind="ExternalInput").ap()
    ksT = nc.dram_tensor("ksT", [128, 8], F32, kind="ExternalInput").ap()
    cc_d = nc.dram_tensor("cc", [128, S], F32, kind="ExternalInput").ap()
    ss_d = nc.dram_tensor("ss", [128, S], F32, kind="ExternalInput").ap()
    identb = nc.dram_tensor("identb", [128, 128], BF16, kind="ExternalInput").ap()
    identf = nc.dram_tensor("identf", [128, 128], F32, kind="ExternalInput").ap()
    h128 = nc.dram_tensor("h128", [128, 128], BF16, kind="ExternalInput").ap()
    onescol = nc.dram_tensor("onescol", [128, 1], BF16, kind="ExternalInput").ap()
    onesrow = nc.dram_tensor("onesrow", [1, 128], F32, kind="ExternalInput").ap()
    cpool_d = nc.dram_tensor("cpool", [128, 8], F32, kind="ExternalInput").ap()
    pmasks_d = [nc.dram_tensor(f"pm{i}", [128, 256], BF16, kind="ExternalInput").ap()
                for i in range(4)]
    Y = nc.dram_tensor("Y", [S, D], F32, kind="ExternalOutput").ap()

    W1T = nc.dram_tensor("W1T", [D, HID], BF16, kind="Internal").ap()
    W3T = nc.dram_tensor("W3T", [D, HID], BF16, kind="Internal").ap()
    W2T = nc.dram_tensor("W2T", [HID, D], BF16, kind="Internal").ap()
    Qs = nc.dram_tensor("Qs", [(NT // 2) * 128, 4096], BF16, kind="Internal").ap()
    Ks = nc.dram_tensor("Ks", [(NT // 2) * 128, 2048], BF16, kind="Internal").ap()
    Vs = nc.dram_tensor("Vs", [S, 1024], BF16, kind="Internal").ap()
    X1 = nc.dram_tensor("X1", [S, D], F32, kind="Internal").ap()

    c = host_constants()
    with contextlib.ExitStack() as es:
        identb_sb = es.enter_context(nc.sbuf_tensor("c_idb", [128, 128], BF16))
        identf_sb = es.enter_context(nc.sbuf_tensor("c_idf", [128, 128], F32))
        h128_sb = es.enter_context(nc.sbuf_tensor("c_h128", [128, 128], BF16))
        onesc_sb = es.enter_context(nc.sbuf_tensor("c_oc", [128, 1], BF16))
        onesr_sb = es.enter_context(nc.sbuf_tensor("c_or", [1, 128], F32))
        cpool_sb = es.enter_context(nc.sbuf_tensor("c_cp", [128, 8], F32))
        pmask_sb = [es.enter_context(nc.sbuf_tensor(f"c_pm{i}", [128, 256], BF16))
                    for i in range(4)]
        sems = make_sems(nc, es, ["init"] + P_SEMS + A1_SEMS + A2_SEMS + F_SEMS)
        ini = sems["init"]

        with contextlib.ExitStack() as esp:
            prog = Prog()
            for dst, src in ((identb_sb, identb), (identf_sb, identf),
                             (h128_sb, h128), (onesc_sb, onescol),
                             (onesr_sb, onesrow), (cpool_sb, cpool_d),
                             (pmask_sb[0], pmasks_d[0]), (pmask_sb[1], pmasks_d[1]),
                             (pmask_sb[2], pmasks_d[2]), (pmask_sb[3], pmasks_d[3])):
                im = prog.dma("y", ini, dst[:], src)
            for e in ENGS:
                prog.wait(e, ini, im)
            emit_phase_p(nc, prog, esp, sems, W1, W3, W2, W1T, W3T, W2T,
                         identf_sb, cpool_sb)
            with nc.Block("ph_p") as blk:
                prog.install(blk)

        with contextlib.ExitStack() as esa:
            prog = Prog()
            entry_waits(prog, sems, P_SEMS)
            io = dict(x=x, Wq=Wq, Wk=Wk, Wv=Wv, qsT=qsT, ksT=ksT, cc=cc_d, ss=ss_d,
                      Qs=Qs, Ks=Ks, Vs=Vs)
            cst = dict(ident_bf=identb_sb, ones_col=onesc_sb, ones_row_f=onesr_sb,
                       cpool=cpool_sb, pairswap=c["pairswap"])
            emit_phase_a1(nc, prog, esa, sems, io, cst)
            with nc.Block("ph_a1") as blk:
                prog.install(blk)

        with contextlib.ExitStack() as esb:
            prog = Prog()
            entry_waits(prog, sems, A1_SEMS)
            io = dict(x=x, Qs=Qs, Ks=Ks, Vs=Vs, Wo=Wo, X1=X1)
            cst = dict(ones_col=onesc_sb, ones_row_f=onesr_sb, cpool=cpool_sb,
                       pmask=pmask_sb)
            emit_phase_a2(nc, prog, esb, sems, io, cst)
            with nc.Block("ph_a2") as blk:
                prog.install(blk)

        with contextlib.ExitStack() as esf:
            prog = Prog()
            entry_waits(prog, sems, A2_SEMS)
            io = dict(X1=X1, W1T=W1T, W3T=W3T, W2T=W2T, Y=Y)
            cst = dict(ident_f=identf_sb, h128=h128_sb, cpool=cpool_sb)
            emit_phase_f(nc, prog, esf, sems, io, cst)
            for nme in ("f_o0", "f_o1"):
                prog.wait("g", sems[nme], sems[nme].n)
            with nc.Block("ph_f") as blk:
                prog.install(blk)
    return nc


_NC_CACHE = {}


def kernel(**inputs):
    lay = layout_inputs(inputs)
    c = host_constants()
    x_all = np.ascontiguousarray(np.asarray(inputs["L0_attn_x"], np.float32))
    if "full" not in _NC_CACHE:
        _NC_CACHE["full"] = build_full()
    nc = _NC_CACHE["full"]
    common = {
        "Wq": lay["Wq"], "Wk": lay["Wk"], "Wv": lay["Wv"], "Wo": lay["Wo"],
        "W1": lay["W1"], "W2": lay["W2"], "W3": lay["W3"],
        "qsT": lay["qs_colsT"], "ksT": lay["ks_colsT"],
        "cc": c["cc"], "ss": c["ss"], "identb": c["ident_bf"],
        "identf": c["ident_f32"], "h128": c["h128_bf"],
        "onescol": c["ones_col_bf"], "onesrow": c["ones_row_f32"],
        "cpool": c["cpool"], "pm0": c["pm0"], "pm1": c["pm1"],
        "pm2": c["pm8"], "pm3": c["pm9"],
    }
    in_maps = [dict(common, x=np.ascontiguousarray(x_all[core])) for core in range(8)]
    res = run_bass_kernel_spmd(nc, in_maps, core_ids=list(range(8)))
    return np.stack([np.asarray(res.results[i]["Y"]) for i in range(8)], axis=0)


# version 05
# speedup vs baseline: 1.1997x; 1.1997x over previous
import contextlib

import numpy as np
import ml_dtypes

import concourse.bass as bass
import concourse.mybir as mybir
from concourse.bass_utils import run_bass_kernel_spmd

F32 = mybir.dt.float32
BF16 = mybir.dt.bfloat16
AF = mybir.ActivationFunctionType
ALU = mybir.AluOpType
AX = mybir.AxisListType

BF = ml_dtypes.bfloat16

S = 4096
D = 2048
HID = 8192
NT = S // 128
NH, NKV, HD = 8, 4, 256
WIN = 1024
LN_EPS = 1e-5
RMS_EPS = 1e-6

CP_TANH, CP_EXP50, CP_NEG1, CP_Q256, CP_ONE, CP_WHT = 0, 1, 2, 3, 4, 5



def host_constants():
    c = {}
    c["ones_col_bf"] = np.ones((128, 1), dtype=BF)
    c["ones_row_f32"] = np.ones((1, 128), dtype=np.float32)
    c["ident_f32"] = np.eye(128, dtype=np.float32)
    c["ident_bf"] = np.eye(128, dtype=BF)

    i = np.arange(128)
    pop = np.array([[bin(a & b).count("1") for b in i] for a in i])
    c["h128_bf"] = np.where(pop % 2 == 0, 1.0, -1.0).astype(BF)

    j = np.arange(64, dtype=np.float64)
    theta = 1.0 / (10000.0 ** (2.0 * j / 128.0))
    pos = np.arange(S, dtype=np.float64)
    ang = (pos[None, :].astype(np.float32).astype(np.float64)
           * theta[:, None]).astype(np.float32)
    cosv = np.cos(ang.astype(np.float64)).astype(np.float32)
    sinv = np.sin(ang.astype(np.float64)).astype(np.float32)
    cc = np.empty((128, S), dtype=np.float32)
    ss = np.empty((128, S), dtype=np.float32)
    cc[0::2] = cosv
    cc[1::2] = cosv
    ss[0::2] = -sinv
    ss[1::2] = sinv
    c["cc"] = cc
    c["ss"] = ss

    p = np.arange(128)[:, None]
    f = np.arange(128)[None, :]
    c["mask_diag_bf"] = (p <= f).astype(BF)
    c["mask_far_bf"] = (p > f).astype(BF)
    r = np.arange(256)[None, :]
    c["pm0"] = (r < p).astype(BF)
    c["pm1"] = (r <= p + 127).astype(BF)
    c["pm8"] = (p <= r).astype(BF)
    c["pm9"] = (r >= p + 128).astype(BF)

    cp = np.zeros((128, 8), dtype=np.float32)
    cp[:, CP_TANH] = 1.0 / 50.0
    cp[:, CP_EXP50] = 50.0
    cp[:, CP_NEG1] = -1.0
    cp[:, CP_Q256] = 1.0 / 256.0
    cp[:, CP_ONE] = 1.0
    cp[:, CP_WHT] = float(HID) ** -0.5
    c["cpool"] = cp

    c["pairswap"] = [x ^ 1 for x in range(32)]
    return c


def layout_inputs(inp):
    d = {}
    d["qs_colsT"] = np.ascontiguousarray(
        np.tile(np.asarray(inp["L0_attn_q_scale"], np.float32), NH).reshape(16, 128).T)
    d["ks_colsT"] = np.ascontiguousarray(
        np.tile(np.asarray(inp["L0_attn_k_scale"], np.float32), NKV).reshape(8, 128).T)
    for k in ("Wq", "Wk", "Wv", "Wo"):
        d[k] = np.ascontiguousarray(np.asarray(inp[f"L0_attn_{k}"], np.float32))
    for k in ("W1", "W2", "W3"):
        d[k] = np.ascontiguousarray(np.asarray(inp[f"L0_ffn_{k}"], np.float32))
    return d



ENGS = ("t", "v", "s", "g", "y")


class Sem:
    __slots__ = ("h", "n")

    def __init__(self, handle):
        self.h = handle
        self.n = 0

    def bump(self, k=1):
        self.n += k
        return self.n


class Prog:

    def __init__(self):
        self.q = {e: [] for e in ENGS}

    def op(self, eng, fn, inc=None, k=1):
        if inc is None:
            self.q[eng].append(fn)
            return None
        self.q[eng].append(lambda e, _f=fn, _h=inc.h, _k=k: _f(e).then_inc(_h, _k))
        return inc.bump(k)

    def dma(self, eng, sem, out, in_):
        return self.op(eng, lambda e, _o=out, _i=in_: e.dma_start(out=_o, in_=_i),
                       inc=sem, k=16)

    def wait(self, eng, sem, val):
        if val is not None and val > 0:
            self.q[eng].append(lambda e, _h=sem.h, _v=val: e.wait_ge(_h, _v))

    def install(self, block):
        def run(ops):
            def body(e):
                for f in ops:
                    f(e)
            return body
        if self.q["t"]:
            block.tensor(run(self.q["t"]))
        if self.q["v"]:
            block.vector(run(self.q["v"]))
        if self.q["s"]:
            block.scalar(run(self.q["s"]))
        if self.q["g"]:
            block.gpsimd(run(self.q["g"]))
        if self.q["y"]:
            block.sync(run(self.q["y"]))


def make_sems(nc, es, names):
    return {n: Sem(es.enter_context(nc.semaphore(n))) for n in names}


def ap2(tensor_handle, offset, pattern):
    return bass.AP(tensor_handle, offset, pattern)



def emit_phase_p(nc, prog, es, sems, W1, W3, W2, W1T, W3T, W2T, ident_f32_sb, cpool_sb):
    pld = [sems["p_ld0"], sems["p_ld1"]]
    w2ld = sems["p_w2ld"]
    tr = sems["p_tr"]
    evs = sems["p_ev"]
    stw = [sems["p_st0"], sems["p_st1"]]
    stw2 = [sems["p_st20"], sems["p_st21"]]
    bfv = sems["p_bfv"]
    bfg = sems["p_bfg"]
    bfs = sems["p_bfs"]

    in2 = [es.enter_context(nc.sbuf_tensor(f"p_in{i}", [128, 2 * D], F32)) for i in range(2)]
    col_t = [es.enter_context(nc.sbuf_tensor(f"p_col{i}", [128, 16 * 128], BF16)) for i in range(2)]
    w2in = es.enter_context(nc.sbuf_tensor("p_w2in", [128, HID], F32))
    w2a = es.enter_context(nc.sbuf_tensor("p_w2a", [128, HID], F32))
    w2b = es.enter_context(nc.sbuf_tensor("p_w2b", [128, HID], F32))
    col2_t = [es.enter_context(nc.sbuf_tensor(f"p_col2{i}", [128, 64 * 128], BF16)) for i in range(2)]
    ps_tr = [es.enter_context(nc.psum_tensor(f"p_ps{i}", [128, 512], F32)) for i in range(2)]

    gseq = [0]
    tr_marks = {}
    ev_marks = {}
    ld_marks = {}
    st_marks = {}
    st2_marks = {}
    pc_last_group = {}
    w2_last_group = {}
    s0_marks = {}
    w2ld_marks = {}
    bf_last = {}

    def group(src_fn, evac_fn):
        g = gseq[0]
        gseq[0] += 1
        if g >= 2:
            prog.wait("t", evs, ev_marks[g - 2])
        for k in range(4):
            fn = src_fn(k, ps_tr[g % 2])
            if k == 3:
                tr_marks[g] = prog.op("t", fn, inc=tr)
            else:
                prog.op("t", fn)
        prog.wait("s", tr, tr_marks[g])
        ev_marks[g] = prog.op("s", evac_fn(ps_tr[g % 2]), inc=evs)
        return g

    def emit_w1w3_pair(pc):
        wi = pc // 32
        Wsrc, Wdst = (W1, W1T) if wi == 0 else (W3, W3T)
        hpair = pc % 32
        slot = pc % 2
        if pc >= 2:
            prog.wait("y", tr, tr_marks[pc_last_group[pc - 2]])
        row0 = hpair * 2 * 128
        src = ap2(Wsrc.tensor, row0 * D, [[D, 128], [128 * D, 2], [1, D]])
        ld_marks[pc] = prog.dma("y", pld[slot], in2[slot][:], src)
        prog.wait("t", pld[slot], ld_marks[pc])
        for sub in range(2):
            hc = hpair * 2 + sub
            csl = hc % 2
            for grp in range(4):
                def mk(k, _ps, _sub=sub, _grp=grp, _slot=slot):
                    dc = _grp * 4 + k
                    return (lambda tt, _p=_ps, _k=k, _in=in2[_slot], _dc=dc, _id=ident_f32_sb,
                            _off=_sub * D:
                            tt.transpose(_p[:, _k * 128:(_k + 1) * 128],
                                         _in[:, _off + _dc * 128:_off + (_dc + 1) * 128],
                                         _id[:]))

                def mkev(_ps, _grp=grp, _csl=csl, _hc=hc, _wi=wi):
                    def f(s_, _p=_ps, _c=col_t[_csl], _g=_grp):
                        return s_.copy(_c[:, _g * 512:(_g + 1) * 512], _p[:])
                    return f
                if grp == 0 and (wi * 64 + hc) >= 2:
                    prog.wait("s", stw[csl], st_marks[wi * 64 + hc - 2])
                g = group(mk, mkev)
            pc_last_group[pc] = g
            prog.wait("s", evs, ev_marks[g])
            dst = ap2(Wdst.tensor, hc * 128, [[HID, 128], [128 * HID, 16], [1, 128]])
            st_marks[wi * 64 + hc] = prog.dma("s", stw[csl], dst, col_t[csl][:])

    def emit_w2_chunk(dc):
        if dc >= 1:
            prog.wait("y", bfv, s0_marks[(dc - 1, "v")])
            prog.wait("y", bfg, s0_marks[(dc - 1, "g")])
            prog.wait("y", bfv, s0_marks[(dc - 1, "s")])
        w2ld_marks[dc] = prog.dma("y", w2ld, w2in[:], W2[dc * 128:(dc + 1) * 128, :])
        for e in ("v", "g"):
            prog.wait(e, w2ld, w2ld_marks[dc])
        for st_i in range(6):
            dist = 128 * (2 ** st_i)
            npair = HID // (2 * dist)
            src = w2in if st_i == 0 else (w2a if st_i % 2 == 1 else w2b)
            dst_b = w2a if st_i % 2 == 0 else w2b
            prog.wait("v", bfg, bfg.n)
            prog.wait("g", bfv, bfv.n)
            if st_i == 1 and dc >= 1:
                for e in ("v", "g", "s"):
                    prog.wait(e, tr, tr_marks[w2_last_group[dc - 1]])

            a_ap = ap2(src, 0, [[HID, 128], [2 * dist, npair], [1, dist]])
            b_ap = ap2(src, dist, [[HID, 128], [2 * dist, npair], [1, dist]])
            lo_ap = ap2(dst_b, 0, [[HID, 128], [2 * dist, npair], [1, dist]])
            mlo = prog.op("v", lambda v, _o=lo_ap, _a=a_ap, _b=b_ap:
                          v.tensor_tensor(_o, _a, _b, ALU.add), inc=bfv)
            if dist < 2048:
                kv = max(npair // 8, 1)
                avv = ap2(src, 0, [[HID, 128], [2 * dist, kv], [1, dist]])
                bvv = ap2(src, dist, [[HID, 128], [2 * dist, kv], [1, dist]])
                hvv = ap2(dst_b, dist, [[HID, 128], [2 * dist, kv], [1, dist]])
                agg = ap2(src, kv * 2 * dist, [[HID, 128], [2 * dist, npair - kv], [1, dist]])
                bgg = ap2(src, kv * 2 * dist + dist, [[HID, 128], [2 * dist, npair - kv], [1, dist]])
                hgg = ap2(dst_b, kv * 2 * dist + dist, [[HID, 128], [2 * dist, npair - kv], [1, dist]])
            else:
                kv = dist // 8
                avv = ap2(src, 0, [[HID, 128], [2 * dist, npair], [1, kv]])
                bvv = ap2(src, dist, [[HID, 128], [2 * dist, npair], [1, kv]])
                hvv = ap2(dst_b, dist, [[HID, 128], [2 * dist, npair], [1, kv]])
                agg = ap2(src, kv, [[HID, 128], [2 * dist, npair], [1, dist - kv]])
                bgg = ap2(src, dist + kv, [[HID, 128], [2 * dist, npair], [1, dist - kv]])
                hgg = ap2(dst_b, dist + kv, [[HID, 128], [2 * dist, npair], [1, dist - kv]])
            mv = prog.op("v", lambda v, _o=hvv, _a=avv, _b=bvv:
                         v.tensor_tensor(_o, _a, _b, ALU.subtract), inc=bfv)
            mg = prog.op("g", lambda g, _o=hgg, _a=agg, _b=bgg:
                         g.tensor_tensor(_o, _a, _b, ALU.subtract), inc=bfg)
            if st_i == 0:
                s0_marks[(dc, "v")] = mv
                s0_marks[(dc, "g")] = mg
                s0_marks[(dc, "s")] = mlo
        bf_last[dc] = (bfv.n, bfg.n)

    def emit_w2_transposes(dc):
        prog.wait("t", bfv, bf_last[dc][0])
        prog.wait("t", bfg, bf_last[dc][1])
        for grp in range(16):
            def mk(k, _ps, _grp=grp):
                hb = _grp * 4 + k
                return (lambda tt, _p=_ps, _k=k, _hb=hb, _id=ident_f32_sb, _w=w2b:
                        tt.transpose(_p[:, _k * 128:(_k + 1) * 128],
                                     _w[:, _hb * 128:(_hb + 1) * 128], _id[:]))

            def mkev(_ps, _grp=grp, _dc=dc):
                def f(s_, _p=_ps, _c=col2_t[_dc % 2], _g=_grp, _cp=cpool_sb):
                    return s_.activation(_c[:, _g * 512:(_g + 1) * 512], _p[:],
                                         AF.Identity, scale=_cp[:, CP_WHT:CP_WHT + 1])
                return f
            if grp == 0 and dc >= 2:
                prog.wait("s", stw2[dc % 2], st2_marks[dc - 2])
            g = group(mk, mkev)
        w2_last_group[dc] = g
        prog.wait("y", evs, ev_marks[g])
        dst = ap2(W2T.tensor, dc * 128, [[D, 128], [128 * D, 64], [1, 128]])
        st2_marks[dc] = prog.dma("y", stw2[dc % 2], dst, col2_t[dc % 2][:])

    for step in range(64):
        if step % 4 == 0:
            if step >= 4:
                emit_w2_transposes(step // 4 - 1)
            emit_w2_chunk(step // 4)
        emit_w1w3_pair(step)
    emit_w2_transposes(15)
    return st_marks


def build_phase_p_standalone(d):
    nc = bass.Bass("TRN2", target_bir_lowering=False, debug=False,
                   detect_race_conditions=False)
    W1 = nc.dram_tensor("W1", [HID, D], F32, kind="ExternalInput").ap()
    W3 = nc.dram_tensor("W3", [HID, D], F32, kind="ExternalInput").ap()
    W2 = nc.dram_tensor("W2", [D, HID], F32, kind="ExternalInput").ap()
    identf = nc.dram_tensor("identf", [128, 128], F32, kind="ExternalInput").ap()
    cpool_d = nc.dram_tensor("cpool", [128, 8], F32, kind="ExternalInput").ap()
    W1T = nc.dram_tensor("W1T", [D, HID], BF16, kind="ExternalOutput").ap()
    W3T = nc.dram_tensor("W3T", [D, HID], BF16, kind="ExternalOutput").ap()
    W2T = nc.dram_tensor("W2T", [HID, D], BF16, kind="ExternalOutput").ap()

    with contextlib.ExitStack() as es:
        ident_sb = es.enter_context(nc.sbuf_tensor("ident_sb", [128, 128], F32))
        cpool_sb = es.enter_context(nc.sbuf_tensor("cpool_sb", [128, 8], F32))
        sems = make_sems(nc, es, P_SEMS + ["init"])
        prog = Prog()
        prog.dma("y", sems["init"], ident_sb[:], identf)
        im = prog.dma("y", sems["init"], cpool_sb[:], cpool_d)
        prog.wait("t", sems["init"], im)
        prog.wait("s", sems["init"], im)
        emit_phase_p(nc, prog, es, sems, W1, W3, W2, W1T, W3T, W2T, ident_sb, cpool_sb)
        for nme in ("p_st0", "p_st1", "p_st20", "p_st21"):
            prog.wait("y", sems[nme], sems[nme].n)
        with nc.Block("phasep") as blk:
            prog.install(blk)
    return nc



def emit_phase_a1(nc, prog, es, sems, io, cst):
    x, Wq, Wk, Wv = io["x"], io["Wq"], io["Wk"], io["Wv"]
    Qs, Ks, Vs = io["Qs"], io["Ks"], io["Vs"]
    cc_d, ss_d, qsT_d, ksT_d = io["cc"], io["ss"], io["qsT"], io["ksT"]
    ident_bf = cst["ident_bf"]
    ones_col = cst["ones_col"]
    ones_row_f = cst["ones_row_f"]
    cpool = cst["cpool"]
    pairswap = cst["pairswap"]

    a_x = [sems["a_x0"], sems["a_x1"]]
    a_cs = [sems["a_cs0"], sems["a_cs1"]]
    a_wl = sems["a_wl"]
    a_s1 = sems["a_s1"]
    a_v1 = sems["a_v1"]
    a_tr = sems["a_tr"]
    a_mm = sems["a_mm"]
    a_sq = sems["a_sq"]
    a_sk = sems["a_sk"]
    a_sv = [sems["a_sv0"], sems["a_sv1"]]

    wq_sb = es.enter_context(nc.sbuf_tensor("a_wq", [128, 16 * 2048], BF16))
    wk_sb = es.enter_context(nc.sbuf_tensor("a_wk", [128, 16 * 1024], BF16))
    wv_sb = es.enter_context(nc.sbuf_tensor("a_wv", [128, 16 * 1024], BF16))
    qs_sb = es.enter_context(nc.sbuf_tensor("a_qsc", [128, 16], F32))
    ks_sb = es.enter_context(nc.sbuf_tensor("a_ksc", [128, 8], F32))
    x_sb = [es.enter_context(nc.sbuf_tensor(f"a_x{i}", [128, D], F32)) for i in range(2)]
    ln_tm = [es.enter_context(nc.sbuf_tensor(f"a_ln{i}", [128, D], BF16)) for i in range(2)]
    lf = [es.enter_context(nc.sbuf_tensor(f"a_lf{i}", [128, 16 * 256], BF16)) for i in range(2)]
    ccs = [es.enter_context(nc.sbuf_tensor(f"a_cc{i}", [128, 256], F32)) for i in range(2)]
    sss = [es.enter_context(nc.sbuf_tensor(f"a_ssl{i}", [128, 256], F32)) for i in range(2)]
    stats = [es.enter_context(nc.sbuf_tensor(f"a_stt{i}", [128, 16], F32)) for i in range(2)]
    qf = [[es.enter_context(nc.sbuf_tensor(f"a_qf{i}{c}", [128, 256], F32)) for c in range(2)]
          for i in range(2)]
    sq = [[es.enter_context(nc.sbuf_tensor(f"a_sq{i}{c}", [128, 256], BF16)) for c in range(2)]
          for i in range(2)]
    qn = [es.enter_context(nc.sbuf_tensor("a_qn0", [128, 256], F32))] * 2
    shufb = [es.enter_context(nc.sbuf_tensor("a_sh0", [128, 256], F32))] * 2
    rt1 = [es.enter_context(nc.sbuf_tensor("a_r10", [128, 256], F32))] * 2
    rt2 = [es.enter_context(nc.sbuf_tensor("a_r20", [128, 256], F32))] * 2
    rows = [es.enter_context(nc.sbuf_tensor(f"a_ro{i}", [1, 3 * 256], F32)) for i in range(2)]
    qout = [es.enter_context(nc.sbuf_tensor("a_qo0", [128, 16 * 256], BF16))] * 2
    kout = [es.enter_context(nc.sbuf_tensor("a_ko0", [128, 8 * 256], BF16))] * 2
    vout = [es.enter_context(nc.sbuf_tensor(f"a_vo{i}", [128, 1024], BF16)) for i in range(2)]

    ps_ltr = [es.enter_context(nc.psum_tensor(f"a_pl{i}", [128, 512], BF16)) for i in range(2)]
    ps_v = es.enter_context(nc.psum_tensor("a_pv", [128, 1024], F32))
    ps_q = es.enter_context(nc.psum_tensor("a_pq", [128, 1024], F32))
    ps_misc = es.enter_context(nc.psum_tensor("a_pm", [128, 1024], F32))

    wl_marks = []
    for dc in range(16):
        wl_marks.append(prog.dma("g", a_wl, wq_sb[:, dc * 2048:(dc + 1) * 2048],
                                 Wq[dc * 128:(dc + 1) * 128, :]))
    for dc in range(16):
        wl_marks.append(prog.dma("g", a_wl, wk_sb[:, dc * 1024:(dc + 1) * 1024],
                                 Wk[dc * 128:(dc + 1) * 128, :]))
    for dc in range(16):
        wl_marks.append(prog.dma("g", a_wl, wv_sb[:, dc * 1024:(dc + 1) * 1024],
                                 Wv[dc * 128:(dc + 1) * 128, :]))
    wl_marks.append(prog.dma("g", a_wl, qs_sb[:], qsT_d))
    wl_marks.append(prog.dma("g", a_wl, ks_sb[:], ksT_d))
    wl_all = wl_marks[-1]
    wl_vv = wl_marks[7]

    lnm, redm, trm, evm = {}, {}, {}, {}
    vproj, vev = {}, {}
    qmm, evq, sqm, ssm, wrm, rsm, repm, g6m = {}, {}, {}, {}, {}, {}, {}, {}
    stm = {}
    lffree = {}
    csl = {}
    g6m_all = {}

    def emit_ln(t):
        sl = t % 2
        if t >= 2:
            prog.wait("y", a_s1, lnm[t - 2])
            prog.wait("y", a_v1, redm[t - 2])
        xm = prog.dma("y", a_x[sl], x_sb[sl][:], x[t * 128:(t + 1) * 128, :])
        st_ = stats[sl]
        prog.wait("s", a_x[sl], xm)
        if t >= 2:
            prog.wait("s", a_tr, trm[(t - 2, 3)])
        sqm_t = prog.op("s", lambda s_, _x=x_sb[sl], _d=ln_tm[sl], _a=st_:
                        s_.activation(_d[:], _x[:], AF.Square, accum_out=_a[:, 1:2]),
                        inc=a_s1)
        prog.wait("v", a_x[sl], xm)
        redm[t] = prog.op("v", lambda v, _x=x_sb[sl], _a=st_:
                          v.tensor_reduce(_a[:, 0:1], _x[:], AX.X, ALU.add), inc=a_v1)
        prog.wait("v", a_v1, redm[t])
        mb = prog.op("v", lambda v, _a=st_:
                     v.tensor_scalar(_a[:, 2:3], _a[:, 0:1], 1.0 / D, None, ALU.mult),
                     inc=a_v1)
        prog.wait("v", a_s1, sqm_t)
        prog.op("v", lambda v, _a=st_:
                v.tensor_scalar(_a[:, 3:4], _a[:, 1:2], 1.0 / D, None, ALU.mult))
        prog.wait("v", a_v1, mb)
        mb = prog.op("v", lambda v, _a=st_:
                     v.tensor_tensor(_a[:, 4:5], _a[:, 2:3], _a[:, 2:3], ALU.mult),
                     inc=a_v1)
        prog.wait("v", a_v1, mb)
        mb = prog.op("v", lambda v, _a=st_:
                     v.tensor_tensor(_a[:, 5:6], _a[:, 3:4], _a[:, 4:5], ALU.subtract),
                     inc=a_v1)
        prog.wait("v", a_v1, mb)
        mb = prog.op("v", lambda v, _a=st_:
                     v.tensor_scalar(_a[:, 6:7], _a[:, 5:6], LN_EPS, None, ALU.add),
                     inc=a_v1)
        prog.wait("v", a_v1, mb)
        vm = prog.op("v", lambda v, _a=st_:
                     v.reciprocal(_a[:, 7:8], _a[:, 6:7]), inc=a_v1)
        prog.wait("s", a_v1, vm)
        m1 = prog.op("s", lambda s_, _a=st_:
                     s_.activation(_a[:, 8:9], _a[:, 7:8], AF.Sqrt), inc=a_s1)
        prog.wait("s", a_s1, m1)
        m2 = prog.op("s", lambda s_, _a=st_:
                     s_.activation(_a[:, 9:10], _a[:, 2:3], AF.Identity,
                                   scale=_a[:, 8:9]), inc=a_s1)
        prog.wait("s", a_s1, m2)
        m3 = prog.op("s", lambda s_, _a=st_, _cp=cpool:
                     s_.activation(_a[:, 10:11], _a[:, 9:10], AF.Identity,
                                   scale=_cp[:, CP_NEG1:CP_NEG1 + 1]), inc=a_s1)
        prog.wait("s", a_s1, m3)
        lnm[t] = prog.op("s", lambda s_, _x=x_sb[sl], _d=ln_tm[sl], _a=st_:
                         s_.activation(_d[:], _x[:], AF.Identity, scale=_a[:, 8:9],
                                       bias=_a[:, 10:11]), inc=a_s1)

    def emit_transpose(t, p):
        sl = t % 2
        side = t - 2 * p
        prog.wait("t", a_s1, lnm[t])
        for grp in range(4):
            gseq = t * 4 + grp
            if gseq >= 2:
                prog.wait("t", a_v1, evm[gseq - 2])
            for k in range(4):
                dc = grp * 4 + k
                fn = (lambda tt, _ps=ps_ltr[gseq % 2], _k=k, _ln=ln_tm[sl], _dc=dc,
                      _id=ident_bf:
                      tt.transpose(_ps[:, _k * 128:(_k + 1) * 128],
                                   _ln[:, _dc * 128:(_dc + 1) * 128], _id[:]))
                if k == 3:
                    trm[(t, grp)] = prog.op("t", fn, inc=a_tr)
                else:
                    prog.op("t", fn)
            prog.wait("v", a_tr, trm[(t, grp)])
            if p >= 2 and grp == 0 and side == 0:
                prev = lffree.get(p - 2)
                if prev is not None:
                    prog.wait("v", a_mm, prev)
            dst = ap2(lf[p % 2], grp * 4 * 256 + side * 128,
                      [[16 * 256, 128], [256, 4], [1, 128]])
            evm[t * 4 + grp] = prog.op(
                "v", lambda v, _d=dst, _ps=ps_ltr[gseq % 2]: v.tensor_copy(_d, _ps[:]),
                inc=a_v1)

    def emit_vproj(t, p):
        side = t - 2 * p
        prog.wait("t", a_v1, evm[t * 4 + 3])
        if t == 0:
            prog.wait("t", a_wl, wl_all)
        if t >= 1:
            prog.wait("t", a_v1, vev[t - 1])
        for dc in range(16):
            for half in range(2):
                fn = (lambda tt, _lf=lf[p % 2], _dc=dc, _side=side, _w=wv_sb, _h=half,
                      _ps=ps_v, _st=(dc == 0), _sp=(dc == 15):
                      tt.matmul(_ps[:, _h * 512:(_h + 1) * 512],
                                _lf[:, _dc * 256 + _side * 128:_dc * 256 + _side * 128 + 128],
                                _w[:, _dc * 1024 + _h * 512:_dc * 1024 + (_h + 1) * 512],
                                start=_st, stop=_sp))
                if dc == 15 and half == 1:
                    vproj[t] = prog.op("t", fn, inc=a_mm)
                else:
                    prog.op("t", fn)
        prog.wait("v", a_mm, vproj[t])
        if t >= 2:
            prog.wait("v", a_sv[t % 2], stm[("v", t - 2)])
        vev[t] = prog.op("v", lambda v, _o=vout[t % 2], _ps=ps_v:
                         v.tensor_copy(_o[:], _ps[:]), inc=a_v1)
        prog.wait("g", a_v1, vev[t])
        stm[("v", t)] = prog.dma("g", a_sv[t % 2], Vs[t * 128:(t + 1) * 128, :], vout[t % 2][:])

    def head_params(h, p):
        if h < 8:
            return dict(w=wq_sb, wstride=2048, b0=2 * h, scol=qs_sb, sc=(2 * h, 2 * h + 1),
                        out=qout[p % 2], oblk=2 * h, sqrt_cp=CP_Q256)
        hk = h - 8
        return dict(w=wk_sb, wstride=1024, b0=2 * hk, scol=ks_sb, sc=(2 * hk, 2 * hk + 1),
                    out=kout[p % 2], oblk=2 * hk, sqrt_cp=CP_ONE)

    def emit_proj_head(h, p):
        hp = head_params(h, p)
        st_set = h % 2
        if h >= 2:
            prog.wait("t", a_v1, evq[h - 2])
        if p == 0 and h == 0:
            prog.wait("t", a_wl, wl_all)
        for c in range(2):
            b = hp["b0"] + c
            for dc in range(16):
                fn = (lambda tt, _w=hp["w"], _dc=dc, _b=b, _ws=hp["wstride"],
                      _lf=lf[p % 2], _ps=ps_q, _off=st_set * 512 + c * 256,
                      _st=(dc == 0), _sp=(dc == 15):
                      tt.matmul(_ps[:, _off:_off + 256],
                                _w[:, _dc * _ws + _b * 128:_dc * _ws + _b * 128 + 128],
                                _lf[:, _dc * 256:(_dc + 1) * 256],
                                start=_st, stop=_sp))
                if c == 1 and dc == 15:
                    qmm[h] = prog.op("t", fn, inc=a_mm)
                else:
                    prog.op("t", fn)

    def emit_evq(h):
        st_set = h % 2
        prog.wait("v", a_mm, qmm[h])
        if h >= 2:
            prog.wait("v", a_s1, sqm[h - 2])
        for c in range(2):
            fn = (lambda v, _qf=qf[st_set][c], _ps=ps_q, _off=st_set * 512 + c * 256:
                  v.tensor_copy(_qf[:], _ps[:, _off:_off + 256]))
            if c == 1:
                evq[h] = prog.op("v", fn, inc=a_v1)
            else:
                prog.op("v", fn)

    def emit_sq(h):
        st_set = h % 2
        prog.wait("s", a_v1, evq[h])
        for c in range(2):
            fn = (lambda s_, _qf=qf[st_set][c], _sq=sq[st_set][c]:
                  s_.activation(_sq[:], _qf[:], AF.Square))
            if c == 1:
                sqm[h] = prog.op("s", fn, inc=a_s1)
            else:
                prog.op("s", fn)

    def emit_ssmm(h):
        st_set = h % 2
        prog.wait("t", a_s1, sqm[h])
        if h >= 2:
            prog.wait("t", a_v1, g6m[h - 2])
        for c in range(2):
            fn = (lambda tt, _sq=sq[st_set][c], _ps=ps_misc,
                  _off=st_set * 512 + 256,
                  _oc=ones_col, _st=(c == 0), _sp=(c == 1):
                  tt.matmul(_ps[0:1, _off:_off + 256], _oc[:], _sq[:],
                            start=_st, stop=_sp))
            if c == 1:
                ssm[h] = prog.op("t", fn, inc=a_mm)
            else:
                prog.op("t", fn)

    def emit_rows(h):
        st_set = h % 2
        prog.wait("v", a_mm, ssm[h])
        mb = prog.op("v", lambda v, _ro=rows[st_set], _ps=ps_misc, _off=st_set * 512 + 256:
                     v.tensor_scalar(_ro[0:1, 0:256], _ps[0:1, _off:_off + 256],
                                     1.0 / 256.0, RMS_EPS, ALU.mult, ALU.add), inc=a_v1)
        prog.wait("v", a_v1, mb)
        wrm[h] = prog.op("v", lambda v, _ro=rows[st_set]:
                         v.reciprocal(_ro[0:1, 256:512], _ro[0:1, 0:256]), inc=a_v1)

    def emit_sqrt(h, p):
        st_set = h % 2
        hp = head_params(h, p)
        prog.wait("s", a_v1, wrm[h])
        rsm[h] = prog.op("s", lambda s_, _ro=rows[st_set], _cp=cpool, _ci=hp["sqrt_cp"]:
                         s_.activation(_ro[0:1, 512:768], _ro[0:1, 256:512], AF.Sqrt,
                                       scale=_cp[0:1, _ci:_ci + 1]), inc=a_s1)

    def emit_repmm(h):
        st_set = h % 2
        prog.wait("t", a_s1, rsm[h])
        repm[h] = prog.op("t", lambda tt, _or=ones_row_f, _ro=rows[st_set],
                          _ps=ps_misc, _off=st_set * 512:
                          tt.matmul(_ps[:, _off:_off + 256], _or[:], _ro[0:1, 512:768],
                                    start=True, stop=True), inc=a_mm)

    def emit_group6(h, p):
        hp = head_params(h, p)
        st_set = h % 2
        prog.wait("v", a_mm, repm[h])
        if h == 0:
            if p == 0:
                prog.wait("v", a_wl, wl_all)
            prog.wait("v", a_cs[p % 2], csl[p])
            if p >= 1:
                prog.wait("v", a_sq, stm[("q", 2 * p - 1)])
                prog.wait("v", a_sk, stm[("k", p - 1)])
        b0, b1 = hp["sc"]
        out = hp["out"]
        ob = hp["oblk"]
        prog.op("v", lambda v, _o=out, _ob=ob, _qf=qf[st_set][1], _sc=hp["scol"],
                _b1=b1, _ps=ps_misc, _off=st_set * 512:
                v.scalar_tensor_tensor(_o[:, (_ob + 1) * 256:(_ob + 2) * 256],
                                       _qf[:], _sc[:, _b1:_b1 + 1],
                                       _ps[:, _off:_off + 256], ALU.mult, ALU.mult))
        mb = prog.op("v", lambda v, _qn=qn[st_set], _qf=qf[st_set][0], _sc=hp["scol"],
                     _b0=b0, _ps=ps_misc, _off=st_set * 512:
                     v.scalar_tensor_tensor(_qn[:], _qf[:], _sc[:, _b0:_b0 + 1],
                                            _ps[:, _off:_off + 256], ALU.mult, ALU.mult),
                     inc=a_v1)
        prog.wait("v", a_v1, mb)
        mb = prog.op("v", lambda v, _sh=shufb[st_set], _qn=qn[st_set], _m=pairswap:
                     v.stream_shuffle(_sh[:], _qn[:], _m), inc=a_v1)
        prog.wait("v", a_v1, mb)
        prog.op("v", lambda v, _a=rt1[st_set], _qn=qn[st_set], _c=ccs[p % 2]:
                v.tensor_tensor(_a[:], _qn[:], _c[:], ALU.mult))
        mb = prog.op("v", lambda v, _b=rt2[st_set], _sh=shufb[st_set], _s=sss[p % 2]:
                     v.tensor_tensor(_b[:], _sh[:], _s[:], ALU.mult), inc=a_v1)
        prog.wait("v", a_v1, mb)
        g6m[h] = prog.op("v", lambda v, _o=out, _ob=ob, _a=rt1[st_set], _b=rt2[st_set]:
                         v.tensor_tensor(_o[:, _ob * 256:(_ob + 1) * 256],
                                         _a[:], _b[:], ALU.add), inc=a_v1)

    NTP = NT // 2

    def emit_prologue(p):
        t0, t1 = 2 * p, 2 * p + 1
        if p >= 2:
            prog.wait("y", a_v1, g6m_all[p - 2])
        prog.dma("y", a_cs[p % 2], ccs[p % 2][:], cc_d[:, p * 256:(p + 1) * 256])
        csl[p] = prog.dma("y", a_cs[p % 2], sss[p % 2][:], ss_d[:, p * 256:(p + 1) * 256])
        emit_ln(t0)
        emit_transpose(t0, p)
        emit_ln(t1)
        emit_transpose(t1, p)
        emit_vproj(t0, p)
        emit_vproj(t1, p)

    emit_prologue(0)
    for p in range(NTP):
        for h in range(12):
            emit_proj_head(h, p)
            if h >= 1:
                emit_ssmm(h - 1)
            if h >= 2:
                emit_repmm(h - 2)
                emit_group6(h - 2, p)
            emit_evq(h)
            emit_sq(h)
            if h >= 1:
                emit_rows(h - 1)
                emit_sqrt(h - 1, p)
            if h == 3 and p + 1 < NTP:
                emit_prologue(p + 1)
        emit_ssmm(11)
        emit_repmm(10)
        emit_group6(10, p)
        emit_rows(11)
        emit_sqrt(11, p)
        emit_repmm(11)
        emit_group6(11, p)
        g6m_all[p] = g6m[11]
        lffree[p] = qmm[11]
        prog.wait("g", a_v1, g6m[7])
        stm[("q", 2 * p)] = stm[("q", 2 * p + 1)] = prog.dma(
            "g", a_sq, Qs[p * 128:(p + 1) * 128, :], qout[p % 2][:])
        prog.wait("g", a_v1, g6m[11])
        stm[("k", p)] = prog.dma("g", a_sk, Ks[p * 128:(p + 1) * 128, :], kout[p % 2][:])
    if "dbg" in io:
        dbg = io["dbg"]
        dsem = sems["a_dbg"]
        prog.wait("g", a_v1, a_v1.n)
        prog.wait("g", a_s1, a_s1.n)
        prog.wait("g", a_mm, a_mm.n)
        for nm_, buf in (("dbg_x", x_sb[0]), ("dbg_ln", ln_tm[0]), ("dbg_lf", lf[0]),
                         ("dbg_vout", vout[0]), ("dbg_qf0", qf[0][0]), ("dbg_qf1", qf[0][1]),
                         ("dbg_rows", rows[0]), ("dbg_qn", qn[0]), ("dbg_cc", ccs[0]),
                         ("dbg_id", cst["ident_bf"]), ("dbg_cp", cst["cpool"]),
                         ("dbg_st", stats[0]),
                         ("dbg_oc", cst["ones_col"]), ("dbg_or", cst["ones_row_f"])):
            if nm_ in dbg:
                prog.dma("g", dsem, dbg[nm_], buf[:])
        prog.wait("g", dsem, dsem.n)
    return stm


def build_phase_a1_standalone():
    nc = bass.Bass("TRN2", target_bir_lowering=False, debug=False,
                   detect_race_conditions=False)
    x = nc.dram_tensor("x", [S, D], F32, kind="ExternalInput").ap()
    Wq = nc.dram_tensor("Wq", [D, 2048], F32, kind="ExternalInput").ap()
    Wk = nc.dram_tensor("Wk", [D, 1024], F32, kind="ExternalInput").ap()
    Wv = nc.dram_tensor("Wv", [D, 1024], F32, kind="ExternalInput").ap()
    qsT = nc.dram_tensor("qsT", [128, 16], F32, kind="ExternalInput").ap()
    ksT = nc.dram_tensor("ksT", [128, 8], F32, kind="ExternalInput").ap()
    cc_d = nc.dram_tensor("cc", [128, S], F32, kind="ExternalInput").ap()
    ss_d = nc.dram_tensor("ss", [128, S], F32, kind="ExternalInput").ap()
    identb = nc.dram_tensor("identb", [128, 128], BF16, kind="ExternalInput").ap()
    onescol = nc.dram_tensor("onescol", [128, 1], BF16, kind="ExternalInput").ap()
    onesrow = nc.dram_tensor("onesrow", [1, 128], F32, kind="ExternalInput").ap()
    cpool_d = nc.dram_tensor("cpool", [128, 8], F32, kind="ExternalInput").ap()
    Qs = nc.dram_tensor("Qs", [(NT // 2) * 128, 4096], BF16, kind="ExternalOutput").ap()
    Ks = nc.dram_tensor("Ks", [(NT // 2) * 128, 2048], BF16, kind="ExternalOutput").ap()
    Vs = nc.dram_tensor("Vs", [S, 1024], BF16, kind="ExternalOutput").ap()

    c = host_constants()
    with contextlib.ExitStack() as es:
        ident_sb = es.enter_context(nc.sbuf_tensor("c_idb", [128, 128], BF16))
        onesc_sb = es.enter_context(nc.sbuf_tensor("c_oc", [128, 1], BF16))
        onesr_sb = es.enter_context(nc.sbuf_tensor("c_or", [1, 128], F32))
        cpool_sb = es.enter_context(nc.sbuf_tensor("c_cp", [128, 8], F32))
        sems = make_sems(nc, es, ["a_x0", "a_x1", "a_cs0", "a_cs1", "a_wl",
                                  "a_s1", "a_v1", "a_tr", "a_mm",
                                  "a_sq", "a_sk", "a_sv0", "a_sv1", "init"])
        prog = Prog()
        ini = sems["init"]
        prog.dma("y", ini, ident_sb[:], identb)
        prog.dma("y", ini, onesc_sb[:], onescol)
        prog.dma("y", ini, onesr_sb[:], onesrow)
        im = prog.dma("y", ini, cpool_sb[:], cpool_d)
        for e in ("t", "v", "s"):
            prog.wait(e, ini, im)
        io = dict(x=x, Wq=Wq, Wk=Wk, Wv=Wv, qsT=qsT, ksT=ksT, cc=cc_d, ss=ss_d,
                  Qs=Qs, Ks=Ks, Vs=Vs)
        cst = dict(ident_bf=ident_sb, ones_col=onesc_sb, ones_row_f=onesr_sb,
                   cpool=cpool_sb, pairswap=c["pairswap"])
        emit_phase_a1(nc, prog, es, sems, io, cst)
        for nme in ("a_sq", "a_sk", "a_sv0", "a_sv1"):
            prog.wait("g", sems[nme], sems[nme].n)
        with nc.Block("phasea1") as blk:
            prog.install(blk)
    return nc



def emit_phase_a2(nc, prog, es, sems, io, cst):
    x, Qs, Ks, Vs, Wo, X1 = io["x"], io["Qs"], io["Ks"], io["Vs"], io["Wo"], io["X1"]
    ones_col = cst["ones_col"]
    ones_row_f = cst["ones_row_f"]
    cpool = cst["cpool"]
    pmask = cst["pmask"]

    qld = [sems["b_q0"], sems["b_q1"]]
    kld = [sems["b_k0"], sems["b_k1"]]
    vld = [sems["b_v0"], sems["b_v1"]]
    xld = [sems["b_x0"], sems["b_x1"]]
    wol = sems["b_wol"]
    b_s = sems["b_s"]
    b_v = sems["b_v"]
    b_mm = sems["b_mm"]
    x1st = [sems["b_st0"], sems["b_st1"]]

    wo_sb = es.enter_context(nc.sbuf_tensor("b_wo", [128, 16 * 2048], BF16))
    k_ring = [es.enter_context(nc.sbuf_tensor(f"b_k{i}", [128, 1024], BF16)) for i in range(10)]
    v_ring = [es.enter_context(nc.sbuf_tensor(f"b_vr{i}", [128, 1024], BF16)) for i in range(10)]
    q_cur = [es.enter_context(nc.sbuf_tensor(f"b_qc{i}", [128, 4096], BF16)) for i in range(2)]
    x_sb = [es.enter_context(nc.sbuf_tensor(f"b_x{i}", [128, D], F32)) for i in range(4)]
    p_t = [es.enter_context(nc.sbuf_tensor(f"b_pt{i}", [128, 2560], BF16)) for i in range(2)]
    attn_out = [es.enter_context(nc.sbuf_tensor(f"b_ao{i}", [128, 16 * 256], BF16))
                for i in range(2)]
    r_sb = [es.enter_context(nc.sbuf_tensor(f"b_r{i}", [128, 256], F32)) for i in range(2)]
    rrow = [es.enter_context(nc.sbuf_tensor(f"b_rr{i}", [1, 2 * 256], F32)) for i in range(2)]
    x1_sb = [es.enter_context(nc.sbuf_tensor(f"b_x1{i}", [128, D], F32)) for i in range(2)]
    ps = es.enter_context(nc.psum_tensor("b_ps", [128, 4096], F32))

    PV0 = 2560
    REP0 = 3072
    SUM0 = 3328
    WO0 = 3584
    NP = NT // 2

    for dc in range(16):
        wlm = prog.dma("g", wol, wo_sb[:, dc * 2048:(dc + 1) * 2048],
                       Wo[dc * 128:(dc + 1) * 128, :])
    wol_all = wlm

    qm, km, vm, xm = {}, {}, {}, {}
    scm, summ, repm, pvm, wom = {}, {}, {}, {}, {}
    expm, maskm, rcm, amm, lnm2 = {}, {}, {}, {}, {}
    woev, stx = {}, {}

    def nb_of(pp):
        return min(2 * pp + 2, 10)

    def emit_loads(pp):
        if pp >= NP:
            return
        sl = pp % 2
        if pp >= 2:
            prog.wait("y", b_mm, scm[(pp - 2, 7)])
        qm[pp] = prog.dma("y", qld[sl], q_cur[sl][:], Qs[pp * 128:(pp + 1) * 128, :])
        for side in range(2):
            t = 2 * pp + side
            rs = t % 10
            if pp >= 5:
                prog.wait("y", b_mm, scm[(pp - 1, 7)])
                prog.wait("y", b_mm, pvm[(pp - 1, 7)])
            ksrc = ap2(Ks.tensor, pp * 128 * 2048 + side * 128,
                       [[2048, 128], [256, 8], [1, 128]])
            km[t] = prog.dma("y", kld[sl], k_ring[rs][:], ksrc)
            vm[t] = prog.dma("y", vld[sl], v_ring[rs][:], Vs[t * 128:(t + 1) * 128, :])
            if pp >= 2:
                prog.wait("y", b_v, woev[(pp - 2, side, 3)])
            xm[t] = prog.dma("y", xld[sl], x_sb[t % 4][:], x[t * 128:(t + 1) * 128, :])

    def emit_scores(pp, h):
        nb = nb_of(pp)
        base = 2 * pp + 1 - (nb - 1)
        kv = h // 2
        if h == 0:
            prog.wait("t", qld[pp % 2], qm[pp])
            prog.wait("t", kld[pp % 2], km[2 * pp + 1])
        if (pp, h - 1) in expm:
            prog.wait("t", b_s, expm[(pp, h - 1)])
        elif pp >= 1 and (pp - 1, 7) in expm:
            prog.wait("t", b_s, expm[(pp - 1, 7)])
        for m in range(nb):
            slot = (base + m) % 10
            for c in range(2):
                fn = (lambda t, _ps=ps, _o=m * 256, _k=k_ring[slot],
                      _koff=(2 * kv + c) * 128, _q=q_cur[pp % 2],
                      _qoff=(2 * h + c) * 256, _st=(c == 0), _sp=(c == 1):
                      t.matmul(_ps[:, _o:_o + 256], _k[:, _koff:_koff + 128],
                               _q[:, _qoff:_qoff + 256], start=_st, stop=_sp))
                if m == nb - 1 and c == 1:
                    scm[(pp, h)] = prog.op("t", fn, inc=b_mm)
                else:
                    prog.op("t", fn)

    def emit_tanh(pp, h):
        nb = nb_of(pp)
        w = nb * 256
        prog.wait("s", b_mm, scm[(pp, h)])
        m1 = prog.op("s", lambda s_, _ps=ps, _w=w, _cp=cpool:
                     s_.activation(_ps[:, 0:_w], _ps[:, 0:_w], AF.Tanh,
                                   scale=_cp[:, CP_TANH:CP_TANH + 1]), inc=b_s)
        prog.wait("s", b_s, m1)

    def emit_exp(pp, h):
        nb = nb_of(pp)
        w = nb * 256
        if (pp, h - 2) in pvm:
            prog.wait("s", b_mm, pvm[(pp, h - 2)])
        elif pp >= 1 and (pp - 1, h + 6) in pvm:
            prog.wait("s", b_mm, pvm[(pp - 1, h + 6)])
        expm[(pp, h)] = prog.op("s", lambda s_, _ps=ps, _w=w, _pt=p_t[h % 2], _cp=cpool:
                                s_.activation(_pt[:, 0:_w], _ps[:, 0:_w], AF.Exp,
                                              scale=_cp[:, CP_EXP50:CP_EXP50 + 1]),
                                inc=b_s)

    def emit_lnrow(pp, h):
        sl = h % 2
        prog.wait("s", b_mm, summ[(pp, h)])
        if (pp, h - 2) in repm:
            prog.wait("s", b_mm, repm[(pp, h - 2)])
        elif pp >= 1 and (pp - 1, h + 6) in repm:
            prog.wait("s", b_mm, repm[(pp - 1, h + 6)])
        m1 = prog.op("s", lambda s_, _ps=ps, _ro=rrow[sl]:
                     s_.activation(_ro[0:1, 0:256], _ps[0:1, SUM0:SUM0 + 256], AF.Ln),
                     inc=b_s)
        prog.wait("s", b_s, m1)
        lnm2[(pp, h)] = prog.op("s", lambda s_, _ro=rrow[sl], _cp=cpool:
                                s_.activation(_ro[0:1, 256:512], _ro[0:1, 0:256], AF.Exp,
                                              scale=_cp[0:1, CP_NEG1:CP_NEG1 + 1]),
                                inc=b_s)

    b_g = sems["b_g"]

    def emit_mask(pp, h):
        nb = nb_of(pp)
        prog.wait("g", b_s, expm[(pp, h)])
        if nb == 10:
            mlist = [(0, 0), (1, 1), (8, 2), (9, 3)]
        else:
            mlist = [(nb - 2, 2), (nb - 1, 3)]
        for idx, (m, mi) in enumerate(mlist):
            fn = (lambda g, _pt=p_t[h % 2], _m=pmask[mi], _o=m * 256:
                  g.tensor_tensor(_pt[:, _o:_o + 256], _pt[:, _o:_o + 256],
                                  _m[:], ALU.mult))
            if idx == len(mlist) - 1:
                maskm[(pp, h)] = prog.op("g", fn, inc=b_g)
            else:
                prog.op("g", fn)

    def emit_sums(pp, h):
        nb = nb_of(pp)
        prog.wait("t", b_g, maskm[(pp, h)])
        if (pp, h - 1) in rcm:
            prog.wait("t", b_v, rcm[(pp, h - 1)])
        elif pp >= 1 and (pp - 1, 7) in rcm:
            prog.wait("t", b_v, rcm[(pp - 1, 7)])
        for m in range(nb):
            fn = (lambda t, _ps=ps, _pt=p_t[h % 2], _m=m, _oc=ones_col,
                  _st=(m == 0), _sp=(m == nb - 1):
                  t.matmul(_ps[0:1, SUM0:SUM0 + 256], _oc[:],
                           _pt[:, _m * 256:(_m + 1) * 256], start=_st, stop=_sp))
            if m == nb - 1:
                summ[(pp, h)] = prog.op("t", fn, inc=b_mm)
            else:
                prog.op("t", fn)

    def emit_rep(pp, h):
        prog.wait("t", b_s, lnm2[(pp, h)])
        repm[(pp, h)] = prog.op("t", lambda t, _ps=ps, _or=ones_row_f, _ro=rrow[h % 2]:
                                t.matmul(_ps[:, REP0:REP0 + 256], _or[:],
                                         _ro[0:1, 256:512], start=True, stop=True),
                                inc=b_mm)

    def emit_pv(pp, h):
        nb = nb_of(pp)
        base = 2 * pp + 1 - (nb - 1)
        kv = h // 2
        if h == 0:
            prog.wait("t", vld[pp % 2], vm[2 * pp + 1])
        if (pp, h - 1) in amm:
            prog.wait("t", b_v, amm[(pp, h - 1)])
        elif pp >= 1 and (pp - 1, 7) in amm:
            prog.wait("t", b_v, amm[(pp - 1, 7)])
        for c in range(2):
            for m in range(nb):
                slot = (base + m) % 10
                fn = (lambda t, _ps=ps, _v=v_ring[slot], _voff=kv * 256 + c * 128,
                      _pt=p_t[h % 2], _m=m, _c=c, _st=(m == 0), _sp=(m == nb - 1):
                      t.matmul(_ps[:, PV0 + _c * 256:PV0 + (_c + 1) * 256],
                               _v[:, _voff:_voff + 128],
                               _pt[:, _m * 256:(_m + 1) * 256], start=_st, stop=_sp))
                if m == nb - 1 and c == 1:
                    pvm[(pp, h)] = prog.op("t", fn, inc=b_mm)
                else:
                    prog.op("t", fn)

    def emit_rcopy(pp, h):
        prog.wait("v", b_mm, repm[(pp, h)])
        mb = prog.op("v", lambda v, _r=r_sb[h % 2], _ps=ps:
                     v.tensor_copy(_r[:], _ps[:, REP0:REP0 + 256]), inc=b_v)
        prog.wait("v", b_v, mb)
        rcm[(pp, h)] = mb

    def emit_attnmul(pp, h):
        prog.wait("v", b_mm, pvm[(pp, h)])
        if h == 0 and (pp - 2, 7) in wom:
            prog.wait("v", b_mm, wom[(pp - 2, 7)])
        for c in range(2):
            fn = (lambda v, _ao=attn_out[pp % 2], _o=(2 * h + c) * 256, _ps=ps, _c=c,
                  _r=r_sb[h % 2]:
                  v.tensor_tensor(_ao[:, _o:_o + 256],
                                  _ps[:, PV0 + _c * 256:PV0 + (_c + 1) * 256],
                                  _r[:], ALU.mult))
            if c == 1:
                amm[(pp, h)] = prog.op("v", fn, inc=b_v)
            else:
                prog.op("v", fn)

    def emit_wo_quarter(pp, side, qtr):
        qq = side * 4 + qtr
        if qq == 0:
            prog.wait("t", b_v, amm[(pp, 7)])
        else:
            prog.wait("t", b_v, woev[(pp, (qq - 1) // 4, (qq - 1) % 4)])
        for dc in range(16):
            fn = (lambda t, _ps=ps, _ao=attn_out[pp % 2], _dc=dc, _w=wo_sb, _q=qtr, _s=side,
                  _st=(dc == 0), _sp=(dc == 15):
                  t.matmul(_ps[:, WO0:WO0 + 512],
                           _ao[:, _dc * 256 + _s * 128:_dc * 256 + _s * 128 + 128],
                           _w[:, _dc * 2048 + _q * 512:_dc * 2048 + (_q + 1) * 512],
                           start=_st, stop=_sp))
            if dc == 15:
                wom[(pp, qq)] = prog.op("t", fn, inc=b_mm)
            else:
                prog.op("t", fn)
        prog.wait("v", b_mm, wom[(pp, qq)])
        t = 2 * pp + side
        if qq == 0 or qq == 4:
            prog.wait("v", xld[pp % 2], xm[t])
            if pp >= 2:
                prog.wait("v", x1st[pp % 2], stx[(pp - 2, side)])
        woev[(pp, side, qtr)] = prog.op(
            "v", lambda v, _x1=x1_sb[side], _x=x_sb[t % 4], _ps=ps, _q=qtr:
            v.tensor_tensor(_x1[:, _q * 512:(_q + 1) * 512], _ps[:, WO0:WO0 + 512],
                            _x[:, _q * 512:(_q + 1) * 512], ALU.add), inc=b_v)

    prog.wait("t", wol, wol_all)
    emit_loads(0)
    for pp in range(NP):
        for h in range(8):
            emit_scores(pp, h)
            if h >= 1:
                emit_sums(pp, h - 1)
            emit_tanh(pp, h)
            if h >= 1:
                emit_lnrow(pp, h - 1)
            emit_exp(pp, h)
            if h >= 1:
                emit_pv(pp, h - 1)
                emit_rep(pp, h - 1)
                emit_rcopy(pp, h - 1)
                emit_attnmul(pp, h - 1)
            emit_mask(pp, h)
            if pp >= 1:
                emit_wo_quarter(pp - 1, h // 4, h % 4)
                if h == 7:
                    for side in range(2):
                        prog.wait("g", b_v, woev[(pp - 1, side, 3)])
                        stx[(pp - 1, side)] = prog.dma(
                            "g", x1st[(pp - 1) % 2],
                            X1[(2 * (pp - 1) + side) * 128:(2 * (pp - 1) + side + 1) * 128, :],
                            x1_sb[side][:])
        emit_sums(pp, 7)
        emit_lnrow(pp, 7)
        emit_pv(pp, 7)
        emit_rep(pp, 7)
        emit_rcopy(pp, 7)
        emit_attnmul(pp, 7)
        emit_loads(pp + 1)
    pp = NP - 1
    for side in range(2):
        for qtr in range(4):
            emit_wo_quarter(pp, side, qtr)
        prog.wait("g", b_v, woev[(pp, side, 3)])
        stx[(pp, side)] = prog.dma("g", x1st[pp % 2],
                                   X1[(2 * pp + side) * 128:(2 * pp + side + 1) * 128, :],
                                   x1_sb[side][:])
    return stx


def build_phase_a2_standalone():
    nc = bass.Bass("TRN2", target_bir_lowering=False, debug=False,
                   detect_race_conditions=False)
    x = nc.dram_tensor("x", [S, D], F32, kind="ExternalInput").ap()
    Qs = nc.dram_tensor("Qs", [(NT // 2) * 128, 4096], BF16, kind="ExternalInput").ap()
    Ks = nc.dram_tensor("Ks", [(NT // 2) * 128, 2048], BF16, kind="ExternalInput").ap()
    Vs = nc.dram_tensor("Vs", [S, 1024], BF16, kind="ExternalInput").ap()
    Wo = nc.dram_tensor("Wo", [2048, D], F32, kind="ExternalInput").ap()
    onescol = nc.dram_tensor("onescol", [128, 1], BF16, kind="ExternalInput").ap()
    onesrow = nc.dram_tensor("onesrow", [1, 128], F32, kind="ExternalInput").ap()
    cpool_d = nc.dram_tensor("cpool", [128, 8], F32, kind="ExternalInput").ap()
    pmasks_d = [nc.dram_tensor(f"pm{i}", [128, 256], BF16, kind="ExternalInput").ap()
                for i in range(4)]
    X1 = nc.dram_tensor("X1", [S, D], F32, kind="ExternalOutput").ap()

    with contextlib.ExitStack() as es:
        onesc_sb = es.enter_context(nc.sbuf_tensor("c_oc", [128, 1], BF16))
        onesr_sb = es.enter_context(nc.sbuf_tensor("c_or", [1, 128], F32))
        cpool_sb = es.enter_context(nc.sbuf_tensor("c_cp", [128, 8], F32))
        pmask_sb = [es.enter_context(nc.sbuf_tensor(f"c_pm{i}", [128, 256], BF16))
                    for i in range(4)]
        sems = make_sems(nc, es, A2_SEMS + ["init"])
        prog = Prog()
        ini = sems["init"]
        prog.dma("y", ini, onesc_sb[:], onescol)
        prog.dma("y", ini, onesr_sb[:], onesrow)
        im = prog.dma("y", ini, cpool_sb[:], cpool_d)
        for i in range(4):
            im = prog.dma("y", ini, pmask_sb[i][:], pmasks_d[i])
        for e in ("t", "v", "s"):
            prog.wait(e, ini, im)
        io = dict(x=x, Qs=Qs, Ks=Ks, Vs=Vs, Wo=Wo, X1=X1)
        cst = dict(ones_col=onesc_sb, ones_row_f=onesr_sb, cpool=cpool_sb,
                   pmask=pmask_sb)
        emit_phase_a2(nc, prog, es, sems, io, cst)
        for nme in ("b_st0", "b_st1"):
            prog.wait("g", sems[nme], sems[nme].n)
        with nc.Block("phasea2") as blk:
            prog.install(blk)
    return nc



def emit_phase_f(nc, prog, es, sems, io, cst):
    X1, W1T, W3T, W2T, Y = io["X1"], io["W1T"], io["W3T"], io["W2T"], io["Y"]
    ident_f = cst["ident_f"]
    h128_sb = cst["h128"]
    cpool = cst["cpool"]

    x1ld = [sems["f_x0"], sems["f_x1"]]
    x1rld = [sems["f_xr0"], sems["f_xr1"]]
    w1ld = [sems["f_w10"], sems["f_w11"]]
    w3ld = [sems["f_w30"], sems["f_w31"]]
    w2ld = [sems["f_w20"], sems["f_w21"]]
    f_s = sems["f_s"]
    f_v = sems["f_v"]
    f_mm = sems["f_mm"]
    ost = [sems["f_o0"], sems["f_o1"]]

    w1b = [es.enter_context(nc.sbuf_tensor(f"f_w1{i}", [128, 16 * 256], BF16)) for i in range(2)]
    w3b = [es.enter_context(nc.sbuf_tensor(f"f_w3{i}", [128, 16 * 256], BF16)) for i in range(2)]
    w2b = [es.enter_context(nc.sbuf_tensor(f"f_w2{i}", [128, 4 * 1024], BF16)) for i in range(2)]
    hidden = es.enter_context(nc.sbuf_tensor("f_hid", [128, 64 * 512], BF16))
    ln2fm = es.enter_context(nc.sbuf_tensor("f_lf", [128, 16 * 512], BF16))
    x1_ln = [es.enter_context(nc.sbuf_tensor(f"f_xl{i}", [128, D], F32)) for i in range(2)]
    x1_res = [es.enter_context(nc.sbuf_tensor(f"f_xr{i}", [128, 1024], F32)) for i in range(2)]
    ln_tm = [es.enter_context(nc.sbuf_tensor(f"f_ln{i}", [128, D], F32)) for i in range(2)]
    stats = [es.enter_context(nc.sbuf_tensor(f"f_stt{i}", [128, 16], F32)) for i in range(2)]
    gs_sb = [es.enter_context(nc.sbuf_tensor(f"f_gs{i}", [128, 512], BF16)) for i in range(2)]
    out_sb = [es.enter_context(nc.sbuf_tensor(f"f_out{i}", [128, 1024], F32)) for i in range(2)]
    ps = es.enter_context(nc.psum_tensor("f_ps", [128, 4096], F32))

    NTB = NT // 4

    x1m, lnm, redm = {}, {}, {}
    trm, evm = {}, {}
    gum, silm, mulm = {}, {}, {}
    h2m, h2ev = {}, {}
    w1m, w3m, w2m = {}, {}, {}
    dnm, devm, stm = {}, {}, {}
    lffull = {}

    def emit_ln2(b, t_loc):
        t = b * 4 + t_loc
        sl = t % 2
        if t >= 2:
            prog.wait("y", f_s, lnm[t - 2])
            prog.wait("y", f_v, redm[t - 2])
        xm = prog.dma("y", x1ld[sl], x1_ln[sl][:], X1[t * 128:(t + 1) * 128, :])
        st_ = stats[sl]
        prog.wait("s", x1ld[sl], xm)
        if t >= 2:
            prog.wait("s", f_mm, trm[(t - 2, 3)])
        sqm_t = prog.op("s", lambda s_, _x=x1_ln[sl], _d=ln_tm[sl], _a=st_:
                        s_.activation(_d[:], _x[:], AF.Square, accum_out=_a[:, 1:2]),
                        inc=f_s)
        prog.wait("v", x1ld[sl], xm)
        redm[t] = prog.op("v", lambda v, _x=x1_ln[sl], _a=st_:
                          v.tensor_reduce(_a[:, 0:1], _x[:], AX.X, ALU.add), inc=f_v)
        prog.wait("v", f_v, redm[t])
        mb = prog.op("v", lambda v, _a=st_:
                     v.tensor_scalar(_a[:, 2:3], _a[:, 0:1], 1.0 / D, None, ALU.mult),
                     inc=f_v)
        prog.wait("v", f_s, sqm_t)
        prog.op("v", lambda v, _a=st_:
                v.tensor_scalar(_a[:, 3:4], _a[:, 1:2], 1.0 / D, None, ALU.mult))
        prog.wait("v", f_v, mb)
        mb = prog.op("v", lambda v, _a=st_:
                     v.tensor_tensor(_a[:, 4:5], _a[:, 2:3], _a[:, 2:3], ALU.mult),
                     inc=f_v)
        prog.wait("v", f_v, mb)
        mb = prog.op("v", lambda v, _a=st_:
                     v.tensor_tensor(_a[:, 5:6], _a[:, 3:4], _a[:, 4:5], ALU.subtract),
                     inc=f_v)
        prog.wait("v", f_v, mb)
        mb = prog.op("v", lambda v, _a=st_:
                     v.tensor_scalar(_a[:, 6:7], _a[:, 5:6], LN_EPS, None, ALU.add),
                     inc=f_v)
        prog.wait("v", f_v, mb)
        vm = prog.op("v", lambda v, _a=st_:
                     v.reciprocal(_a[:, 7:8], _a[:, 6:7]), inc=f_v)
        prog.wait("s", f_v, vm)
        m1 = prog.op("s", lambda s_, _a=st_:
                     s_.activation(_a[:, 8:9], _a[:, 7:8], AF.Sqrt), inc=f_s)
        prog.wait("s", f_s, m1)
        m2 = prog.op("s", lambda s_, _a=st_:
                     s_.activation(_a[:, 9:10], _a[:, 2:3], AF.Identity,
                                   scale=_a[:, 8:9]), inc=f_s)
        prog.wait("s", f_s, m2)
        m3 = prog.op("s", lambda s_, _a=st_, _cp=cpool:
                     s_.activation(_a[:, 10:11], _a[:, 9:10], AF.Identity,
                                   scale=_cp[:, CP_NEG1:CP_NEG1 + 1]), inc=f_s)
        prog.wait("s", f_s, m3)
        lnm[t] = prog.op("s", lambda s_, _x=x1_ln[sl], _d=ln_tm[sl], _a=st_:
                         s_.activation(_d[:], _x[:], AF.Identity, scale=_a[:, 8:9],
                                       bias=_a[:, 10:11]), inc=f_s)

    def emit_transpose2(b, t_loc):
        t = b * 4 + t_loc
        sl = t % 2
        prog.wait("t", f_s, lnm[t])
        if b >= 1 and t_loc == 0:
            prog.wait("t", f_v, devm[(b - 1, 1, 0)])
        for grp in range(4):
            gseq = t * 4 + grp
            if gseq >= 2:
                prog.wait("t", f_v, evm[gseq - 2])
            for k in range(4):
                dc = grp * 4 + k
                fn = (lambda tt, _ps=ps, _o=(gseq % 2) * 512 + k * 128, _ln=ln_tm[sl],
                      _dc=dc, _id=ident_f:
                      tt.transpose(_ps[:, _o:_o + 128],
                                   _ln[:, _dc * 128:(_dc + 1) * 128], _id[:]))
                if k == 3:
                    trm[(t, grp)] = prog.op("t", fn, inc=f_mm)
                else:
                    prog.op("t", fn)
            prog.wait("v", f_mm, trm[(t, grp)])
            if b >= 1 and t_loc == 0 and grp == 0:
                prog.wait("v", f_mm, gum[(b - 1, "u", 63)])
            dst = ap2(ln2fm, grp * 4 * 512 + t_loc * 128,
                      [[16 * 512, 128], [512, 4], [1, 128]])
            evm[gseq] = prog.op(
                "v", lambda v, _d=dst, _ps=ps, _o=(gseq % 2) * 512:
                v.tensor_copy(_d, _ps[:, _o:_o + 512]), inc=f_v)
        if t_loc == 3:
            lffull[b] = evm[t * 4 + 3]

    def emit_w13_load(b, hc):
        sl = hc % 2
        seq = b * 32 + hc
        if seq >= 2:
            pb, phc = (b, hc - 2) if hc >= 2 else (b - 1, hc + 30)
            prog.wait("y", f_mm, gum[(pb, "g", phc * 2 + 1)])
            prog.wait("y", f_mm, gum[(pb, "u", phc * 2 + 1)])
        src1 = ap2(W1T.tensor, hc * 256, [[HID, 128], [128 * HID, 16], [1, 256]])
        w1m[(b, hc)] = prog.dma("y", w1ld[sl], w1b[sl][:], src1)
        src3 = ap2(W3T.tensor, hc * 256, [[HID, 128], [128 * HID, 16], [1, 256]])
        w3m[(b, hc)] = prog.dma("y", w3ld[sl], w3b[sl][:], src3)

    def emit_gu_block(b, hc, hs):
        cblk = hc * 2 + hs
        sl = hc % 2
        par = cblk % 2
        prog.wait("t", w1ld[sl], w1m[(b, hc)])
        prog.wait("t", w3ld[sl], w3m[(b, hc)])
        if hc == 0 and hs == 0:
            prog.wait("t", f_v, lffull[b])
            if b >= 1:
                prog.wait("t", f_v, devm[(b - 1, 1, 1)])
        seq = b * 64 + cblk
        if seq >= 2:
            pb, pc = (b, cblk - 2) if cblk >= 2 else (b - 1, cblk + 62)
            prog.wait("t", f_s, silm[(pb, pc)])
            prog.wait("t", f_v, mulm[(pb, pc)])
        for dc in range(16):
            fn = (lambda tt, _ps=ps, _o=par * 512, _w=w1b[sl], _dc=dc, _hs=hs,
                  _lf=ln2fm, _st=(dc == 0), _sp=(dc == 15):
                  tt.matmul(_ps[:, _o:_o + 512],
                            _w[:, _dc * 256 + _hs * 128:_dc * 256 + _hs * 128 + 128],
                            _lf[:, _dc * 512:(_dc + 1) * 512], start=_st, stop=_sp))
            if dc == 15:
                gum[(b, "g", cblk)] = prog.op("t", fn, inc=f_mm)
            else:
                prog.op("t", fn)
        for dc in range(16):
            fn = (lambda tt, _ps=ps, _o=1024 + par * 512, _w=w3b[sl], _dc=dc, _hs=hs,
                  _lf=ln2fm, _st=(dc == 0), _sp=(dc == 15):
                  tt.matmul(_ps[:, _o:_o + 512],
                            _w[:, _dc * 256 + _hs * 128:_dc * 256 + _hs * 128 + 128],
                            _lf[:, _dc * 512:(_dc + 1) * 512], start=_st, stop=_sp))
            if dc == 15:
                gum[(b, "u", cblk)] = prog.op("t", fn, inc=f_mm)
            else:
                prog.op("t", fn)
        prog.wait("s", f_mm, gum[(b, "g", cblk)])
        if seq >= 2:
            pb, pc = (b, cblk - 2) if cblk >= 2 else (b - 1, cblk + 62)
            prog.wait("s", f_v, mulm[(pb, pc)])
        silm[(b, cblk)] = prog.op("s", lambda s_, _ps=ps, _o=par * 512, _g=gs_sb[par]:
                                  s_.activation(_g[:], _ps[:, _o:_o + 512], AF.Silu),
                                  inc=f_s)
        prog.wait("v", f_mm, gum[(b, "u", cblk)])
        prog.wait("v", f_s, silm[(b, cblk)])
        if b >= 1 and cblk == 0:
            prog.wait("v", f_mm, dnm[(b - 1, 1, 15)])
        mulm[(b, cblk)] = prog.op(
            "v", lambda v, _h=hidden, _c=cblk, _ps=ps, _o=1024 + par * 512, _g=gs_sb[par]:
            v.tensor_tensor(_h[:, _c * 512:(_c + 1) * 512], _ps[:, _o:_o + 512],
                            _g[:], ALU.mult), inc=f_v)

    def emit_h2(b, c):
        par = c % 2
        prog.wait("t", f_v, mulm[(b, c)])
        if c == 0 and b >= 1:
            prog.wait("t", f_v, devm[(b - 1, 1, 2)])
        seq = b * 64 + c
        if seq >= 2:
            pb, pc = (b, c - 2) if c >= 2 else (b - 1, c + 62)
            prog.wait("t", f_v, h2ev[(pb, pc)])
        h2m[(b, c)] = prog.op("t", lambda tt, _ps=ps, _o=2048 + par * 512, _h=hidden,
                              _c=c, _hk=h128_sb:
                              tt.matmul(_ps[:, _o:_o + 512], _hk[:],
                                        _h[:, _c * 512:(_c + 1) * 512],
                                        start=True, stop=True), inc=f_mm)
        prog.wait("v", f_mm, h2m[(b, c)])
        h2ev[(b, c)] = prog.op("v", lambda v, _ps=ps, _o=2048 + par * 512, _h=hidden,
                               _c=c:
                               v.tensor_copy(_h[:, _c * 512:(_c + 1) * 512],
                                             _ps[:, _o:_o + 512]), inc=f_v)

    def emit_w2_load(b, half, cg):
        sl = cg % 2
        seq = (b * 2 + half) * 16 + cg
        if seq >= 2:
            pb, ph, pcg = b, half, cg - 2
            if cg < 2:
                ph = half - 1
                pcg = cg + 14
                if ph < 0:
                    pb, ph = b - 1, 1
            prog.wait("y", f_mm, dnm[(pb, ph, pcg)])
        src = ap2(W2T.tensor, (cg * 4) * 128 * D + half * 1024,
                  [[D, 128], [128 * D, 4], [1, 1024]])
        w2m[(b, half, cg)] = prog.dma("y", w2ld[sl], w2b[sl][:], src)

    def emit_down_group(b, half, cg):
        sl = cg % 2
        prog.wait("t", w2ld[sl], w2m[(b, half, cg)])
        if cg == 0:
            if half == 0:
                prog.wait("t", f_v, h2ev[(b, 63)])
                if b >= 1:
                    prog.wait("t", f_v, devm[(b - 1, 1, 3)])
            else:
                for tl in range(4):
                    prog.wait("t", f_v, devm[(b, 0, tl)])
        for t_loc in range(4):
            for j in range(2):
                for c4 in range(4):
                    c = cg * 4 + c4
                    is_last = (c4 == 3 and t_loc == 3 and j == 1)
                    fn = (lambda tt, _ps=ps, _o=t_loc * 1024 + j * 512, _h=hidden,
                          _hoff=c * 512 + t_loc * 128, _w=w2b[sl],
                          _woff=c4 * 1024 + j * 512, _st=(c == 0), _sp=(c == 63):
                          tt.matmul(_ps[:, _o:_o + 512],
                                    _h[:, _hoff:_hoff + 128],
                                    _w[:, _woff:_woff + 512], start=_st, stop=_sp))
                    if is_last:
                        dnm[(b, half, cg)] = prog.op("t", fn, inc=f_mm)
                    else:
                        prog.op("t", fn)

    xresm = {}

    def emit_x1res_load(b, half, t_loc):
        t = b * 4 + t_loc
        sl = t % 2
        seq = (b * 2 + half) * 4 + t_loc
        if seq >= 2:
            pb, ph, pt = b, half, t_loc - 2
            if t_loc < 2:
                ph = half - 1
                pt = t_loc + 2
                if ph < 0:
                    pb, ph = b - 1, 1
            prog.wait("y", f_v, devm[(pb, ph, pt)])
        src = ap2(X1.tensor, t * 128 * D + half * 1024, [[D, 128], [1, 1024]])
        xresm[(b, half, t_loc)] = prog.dma("y", x1rld[sl], x1_res[sl][:], src)

    def emit_down_evac(b, half, t_loc):
        t = b * 4 + t_loc
        sl = t % 2
        prog.wait("v", f_mm, dnm[(b, half, 15)])
        prog.wait("v", x1rld[sl], xresm[(b, half, t_loc)])
        seq = (b * 2 + half) * 4 + t_loc
        if seq >= 2:
            pb, ph, pt = b, half, t_loc - 2
            if t_loc < 2:
                ph = half - 1
                pt = t_loc + 2
                if ph < 0:
                    pb, ph = b - 1, 1
            prog.wait("v", ost[sl], stm[(pb, ph, pt)])
        devm[(b, half, t_loc)] = prog.op(
            "v", lambda v, _o=out_sb[sl], _ps=ps, _po=t_loc * 1024, _x=x1_res[sl]:
            v.tensor_tensor(_o[:], _ps[:, _po:_po + 1024], _x[:], ALU.add), inc=f_v)
        prog.wait("g", f_v, devm[(b, half, t_loc)])
        dst = ap2(Y.tensor, t * 128 * D + half * 1024, [[D, 128], [1, 1024]])
        stm[(b, half, t_loc)] = prog.dma("g", ost[sl], dst, out_sb[sl][:])

    for b in range(NTB):
        for t_loc in range(4):
            emit_ln2(b, t_loc)
            emit_transpose2(b, t_loc)
        emit_w13_load(b, 0)
        emit_w13_load(b, 1)
        for hc in range(32):
            emit_gu_block(b, hc, 0)
            emit_gu_block(b, hc, 1)
            if hc + 2 < 32:
                emit_w13_load(b, hc + 2)
        for c in range(64):
            emit_h2(b, c)
        for half in range(2):
            emit_w2_load(b, half, 0)
            for cg in range(16):
                emit_down_group(b, half, cg)
                if cg + 1 < 16:
                    emit_w2_load(b, half, cg + 1)
                elif half == 0:
                    emit_w2_load(b, 1, 0)
                if cg >= 14:
                    emit_x1res_load(b, half, cg - 14)
            for t_loc in range(4):
                emit_down_evac(b, half, t_loc)
                if t_loc + 2 < 4:
                    emit_x1res_load(b, half, t_loc + 2)
    return stm


def build_phase_f_standalone():
    nc = bass.Bass("TRN2", target_bir_lowering=False, debug=False,
                   detect_race_conditions=False)
    X1 = nc.dram_tensor("X1", [S, D], F32, kind="ExternalInput").ap()
    W1T = nc.dram_tensor("W1T", [D, HID], BF16, kind="ExternalInput").ap()
    W3T = nc.dram_tensor("W3T", [D, HID], BF16, kind="ExternalInput").ap()
    W2T = nc.dram_tensor("W2T", [HID, D], BF16, kind="ExternalInput").ap()
    identf = nc.dram_tensor("identf", [128, 128], F32, kind="ExternalInput").ap()
    h128 = nc.dram_tensor("h128", [128, 128], BF16, kind="ExternalInput").ap()
    cpool_d = nc.dram_tensor("cpool", [128, 8], F32, kind="ExternalInput").ap()
    Y = nc.dram_tensor("Y", [S, D], F32, kind="ExternalOutput").ap()
    with contextlib.ExitStack() as es:
        identf_sb = es.enter_context(nc.sbuf_tensor("c_idf", [128, 128], F32))
        h128_sb = es.enter_context(nc.sbuf_tensor("c_h128", [128, 128], BF16))
        cpool_sb = es.enter_context(nc.sbuf_tensor("c_cp", [128, 8], F32))
        sems = make_sems(nc, es, F_SEMS + ["init"])
        prog = Prog()
        ini = sems["init"]
        prog.dma("y", ini, identf_sb[:], identf)
        prog.dma("y", ini, h128_sb[:], h128)
        im = prog.dma("y", ini, cpool_sb[:], cpool_d)
        for e in ("t", "v", "s"):
            prog.wait(e, ini, im)
        io = dict(X1=X1, W1T=W1T, W3T=W3T, W2T=W2T, Y=Y)
        cst = dict(ident_f=identf_sb, h128=h128_sb, cpool=cpool_sb)
        emit_phase_f(nc, prog, es, sems, io, cst)
        for nme in ("f_o0", "f_o1"):
            prog.wait("g", sems[nme], sems[nme].n)
        with nc.Block("phasef") as blk:
            prog.install(blk)
    return nc



P_SEMS = ["p_ld0", "p_ld1", "p_w2ld", "p_tr", "p_ev", "p_st0", "p_st1",
          "p_st20", "p_st21", "p_bfv", "p_bfg", "p_bfs"]
A1_SEMS = ["a_x0", "a_x1", "a_cs0", "a_cs1", "a_wl", "a_s1", "a_v1", "a_tr",
           "a_mm", "a_sq", "a_sk", "a_sv0", "a_sv1"]
A2_SEMS = ["b_q0", "b_q1", "b_k0", "b_k1", "b_v0", "b_v1", "b_x0", "b_x1",
           "b_wol", "b_s", "b_v", "b_g", "b_mm", "b_st0", "b_st1"]
F_SEMS = ["f_x0", "f_x1", "f_xr0", "f_xr1", "f_w10", "f_w11", "f_w30", "f_w31",
          "f_w20", "f_w21", "f_s", "f_v", "f_mm", "f_o0", "f_o1"]


def entry_waits(prog, sems, names):
    for eng in ENGS:
        for n in names:
            prog.wait(eng, sems[n], sems[n].n)


def build_full():
    nc = bass.Bass("TRN2", target_bir_lowering=False, debug=False,
                   detect_race_conditions=False)
    x = nc.dram_tensor("x", [S, D], F32, kind="ExternalInput").ap()
    Wq = nc.dram_tensor("Wq", [D, 2048], F32, kind="ExternalInput").ap()
    Wk = nc.dram_tensor("Wk", [D, 1024], F32, kind="ExternalInput").ap()
    Wv = nc.dram_tensor("Wv", [D, 1024], F32, kind="ExternalInput").ap()
    Wo = nc.dram_tensor("Wo", [2048, D], F32, kind="ExternalInput").ap()
    W1 = nc.dram_tensor("W1", [HID, D], F32, kind="ExternalInput").ap()
    W2 = nc.dram_tensor("W2", [D, HID], F32, kind="ExternalInput").ap()
    W3 = nc.dram_tensor("W3", [HID, D], F32, kind="ExternalInput").ap()
    qsT = nc.dram_tensor("qsT", [128, 16], F32, kind="ExternalInput").ap()
    ksT = nc.dram_tensor("ksT", [128, 8], F32, kind="ExternalInput").ap()
    cc_d = nc.dram_tensor("cc", [128, S], F32, kind="ExternalInput").ap()
    ss_d = nc.dram_tensor("ss", [128, S], F32, kind="ExternalInput").ap()
    identb = nc.dram_tensor("identb", [128, 128], BF16, kind="ExternalInput").ap()
    identf = nc.dram_tensor("identf", [128, 128], F32, kind="ExternalInput").ap()
    h128 = nc.dram_tensor("h128", [128, 128], BF16, kind="ExternalInput").ap()
    onescol = nc.dram_tensor("onescol", [128, 1], BF16, kind="ExternalInput").ap()
    onesrow = nc.dram_tensor("onesrow", [1, 128], F32, kind="ExternalInput").ap()
    cpool_d = nc.dram_tensor("cpool", [128, 8], F32, kind="ExternalInput").ap()
    pmasks_d = [nc.dram_tensor(f"pm{i}", [128, 256], BF16, kind="ExternalInput").ap()
                for i in range(4)]
    Y = nc.dram_tensor("Y", [S, D], F32, kind="ExternalOutput").ap()

    W1T = nc.dram_tensor("W1T", [D, HID], BF16, kind="Internal").ap()
    W3T = nc.dram_tensor("W3T", [D, HID], BF16, kind="Internal").ap()
    W2T = nc.dram_tensor("W2T", [HID, D], BF16, kind="Internal").ap()
    Qs = nc.dram_tensor("Qs", [(NT // 2) * 128, 4096], BF16, kind="Internal").ap()
    Ks = nc.dram_tensor("Ks", [(NT // 2) * 128, 2048], BF16, kind="Internal").ap()
    Vs = nc.dram_tensor("Vs", [S, 1024], BF16, kind="Internal").ap()
    X1 = nc.dram_tensor("X1", [S, D], F32, kind="Internal").ap()

    c = host_constants()
    with contextlib.ExitStack() as es:
        identb_sb = es.enter_context(nc.sbuf_tensor("c_idb", [128, 128], BF16))
        identf_sb = es.enter_context(nc.sbuf_tensor("c_idf", [128, 128], F32))
        h128_sb = es.enter_context(nc.sbuf_tensor("c_h128", [128, 128], BF16))
        onesc_sb = es.enter_context(nc.sbuf_tensor("c_oc", [128, 1], BF16))
        onesr_sb = es.enter_context(nc.sbuf_tensor("c_or", [1, 128], F32))
        cpool_sb = es.enter_context(nc.sbuf_tensor("c_cp", [128, 8], F32))
        pmask_sb = [es.enter_context(nc.sbuf_tensor(f"c_pm{i}", [128, 256], BF16))
                    for i in range(4)]
        sems = make_sems(nc, es, ["init"] + P_SEMS + A1_SEMS + A2_SEMS + F_SEMS)
        ini = sems["init"]

        with contextlib.ExitStack() as esp:
            prog = Prog()
            for dst, src in ((identb_sb, identb), (identf_sb, identf),
                             (h128_sb, h128), (onesc_sb, onescol),
                             (onesr_sb, onesrow), (cpool_sb, cpool_d),
                             (pmask_sb[0], pmasks_d[0]), (pmask_sb[1], pmasks_d[1]),
                             (pmask_sb[2], pmasks_d[2]), (pmask_sb[3], pmasks_d[3])):
                im = prog.dma("y", ini, dst[:], src)
            for e in ENGS:
                prog.wait(e, ini, im)
            emit_phase_p(nc, prog, esp, sems, W1, W3, W2, W1T, W3T, W2T,
                         identf_sb, cpool_sb)
            with nc.Block("ph_p") as blk:
                prog.install(blk)

        with contextlib.ExitStack() as esa:
            prog = Prog()
            entry_waits(prog, sems, P_SEMS)
            io = dict(x=x, Wq=Wq, Wk=Wk, Wv=Wv, qsT=qsT, ksT=ksT, cc=cc_d, ss=ss_d,
                      Qs=Qs, Ks=Ks, Vs=Vs)
            cst = dict(ident_bf=identb_sb, ones_col=onesc_sb, ones_row_f=onesr_sb,
                       cpool=cpool_sb, pairswap=c["pairswap"])
            emit_phase_a1(nc, prog, esa, sems, io, cst)
            with nc.Block("ph_a1") as blk:
                prog.install(blk)

        with contextlib.ExitStack() as esb:
            prog = Prog()
            entry_waits(prog, sems, A1_SEMS)
            io = dict(x=x, Qs=Qs, Ks=Ks, Vs=Vs, Wo=Wo, X1=X1)
            cst = dict(ones_col=onesc_sb, ones_row_f=onesr_sb, cpool=cpool_sb,
                       pmask=pmask_sb)
            emit_phase_a2(nc, prog, esb, sems, io, cst)
            with nc.Block("ph_a2") as blk:
                prog.install(blk)

        with contextlib.ExitStack() as esf:
            prog = Prog()
            entry_waits(prog, sems, A2_SEMS)
            io = dict(X1=X1, W1T=W1T, W3T=W3T, W2T=W2T, Y=Y)
            cst = dict(ident_f=identf_sb, h128=h128_sb, cpool=cpool_sb)
            emit_phase_f(nc, prog, esf, sems, io, cst)
            for nme in ("f_o0", "f_o1"):
                prog.wait("g", sems[nme], sems[nme].n)
            with nc.Block("ph_f") as blk:
                prog.install(blk)
    return nc


_NC_CACHE = {}


def kernel(**inputs):
    lay = layout_inputs(inputs)
    c = host_constants()
    x_all = np.ascontiguousarray(np.asarray(inputs["L0_attn_x"], np.float32))
    if "full" not in _NC_CACHE:
        _NC_CACHE["full"] = build_full()
    nc = _NC_CACHE["full"]
    common = {
        "Wq": lay["Wq"], "Wk": lay["Wk"], "Wv": lay["Wv"], "Wo": lay["Wo"],
        "W1": lay["W1"], "W2": lay["W2"], "W3": lay["W3"],
        "qsT": lay["qs_colsT"], "ksT": lay["ks_colsT"],
        "cc": c["cc"], "ss": c["ss"], "identb": c["ident_bf"],
        "identf": c["ident_f32"], "h128": c["h128_bf"],
        "onescol": c["ones_col_bf"], "onesrow": c["ones_row_f32"],
        "cpool": c["cpool"], "pm0": c["pm0"], "pm1": c["pm1"],
        "pm2": c["pm8"], "pm3": c["pm9"],
    }
    in_maps = [dict(common, x=np.ascontiguousarray(x_all[core])) for core in range(8)]
    res = run_bass_kernel_spmd(nc, in_maps, core_ids=list(range(8)))
    return np.stack([np.asarray(res.results[i]["Y"]) for i in range(8)], axis=0)


# version 07
# speedup vs baseline: 1.2414x; 1.0348x over previous
import contextlib

import numpy as np
import ml_dtypes

import concourse.bass as bass
import concourse.mybir as mybir
from concourse.bass_utils import run_bass_kernel_spmd

F32 = mybir.dt.float32
BF16 = mybir.dt.bfloat16
AF = mybir.ActivationFunctionType
ALU = mybir.AluOpType
AX = mybir.AxisListType

BF = ml_dtypes.bfloat16

S = 4096
D = 2048
HID = 8192
NT = S // 128
NH, NKV, HD = 8, 4, 256
WIN = 1024
LN_EPS = 1e-5
RMS_EPS = 1e-6

CP_TANH, CP_EXP50, CP_NEG1, CP_Q256, CP_ONE, CP_WHT = 0, 1, 2, 3, 4, 5
CP_RMSEPS, CP_NEGHALF, CP_LN16N, CP_ZERO = 6, 7, 8, 9



def host_constants():
    c = {}
    c["ones_col_bf"] = np.ones((128, 1), dtype=BF)
    c["ones_row_f32"] = np.ones((1, 128), dtype=np.float32)
    c["ident_f32"] = np.eye(128, dtype=np.float32)
    c["ident_bf"] = np.eye(128, dtype=BF)

    i = np.arange(128)
    pop = np.array([[bin(a & b).count("1") for b in i] for a in i])
    c["h128_bf"] = np.where(pop % 2 == 0, 1.0, -1.0).astype(BF)

    j = np.arange(64, dtype=np.float64)
    theta = 1.0 / (10000.0 ** (2.0 * j / 128.0))
    pos = np.arange(S, dtype=np.float64)
    ang = (pos[None, :].astype(np.float32).astype(np.float64)
           * theta[:, None]).astype(np.float32)
    cosv = np.cos(ang.astype(np.float64)).astype(np.float32)
    sinv = np.sin(ang.astype(np.float64)).astype(np.float32)
    cc = np.empty((128, S), dtype=np.float32)
    ss = np.empty((128, S), dtype=np.float32)
    cc[0::2] = cosv
    cc[1::2] = cosv
    ss[0::2] = -sinv
    ss[1::2] = sinv
    c["cc"] = cc
    c["ss"] = ss

    p = np.arange(128)[:, None]
    f = np.arange(128)[None, :]
    c["mask_diag_bf"] = (p <= f).astype(BF)
    c["mask_far_bf"] = (p > f).astype(BF)
    r = np.arange(256)[None, :]
    c["pm0"] = (r < p).astype(BF)
    c["pm1"] = (r <= p + 127).astype(BF)
    c["pm8"] = (p <= r).astype(BF)
    c["pm9"] = (r >= p + 128).astype(BF)

    cp = np.zeros((128, 12), dtype=np.float32)
    cp[:, CP_TANH] = 1.0 / 50.0
    cp[:, CP_EXP50] = 50.0
    cp[:, CP_NEG1] = -1.0
    cp[:, CP_Q256] = 1.0 / 256.0
    cp[:, CP_ONE] = 1.0
    cp[:, CP_WHT] = float(HID) ** -0.5
    cp[:, CP_RMSEPS] = RMS_EPS
    cp[:, CP_NEGHALF] = -0.5
    cp[:, CP_LN16N] = -float(np.log(16.0))
    c["cpool"] = cp

    c["pairswap"] = [x ^ 1 for x in range(32)]
    return c


def layout_inputs(inp):
    d = {}
    d["qs_colsT"] = np.ascontiguousarray(
        np.tile(np.asarray(inp["L0_attn_q_scale"], np.float32), NH).reshape(16, 128).T)
    d["ks_colsT"] = np.ascontiguousarray(
        np.tile(np.asarray(inp["L0_attn_k_scale"], np.float32), NKV).reshape(8, 128).T)
    for k in ("Wq", "Wk", "Wv", "Wo"):
        d[k] = np.ascontiguousarray(np.asarray(inp[f"L0_attn_{k}"], np.float32))
    for k in ("W1", "W2", "W3"):
        d[k] = np.ascontiguousarray(np.asarray(inp[f"L0_ffn_{k}"], np.float32))
    return d



ENGS = ("t", "v", "s", "g", "y")


class Sem:
    __slots__ = ("h", "n")

    def __init__(self, handle):
        self.h = handle
        self.n = 0

    def bump(self, k=1):
        self.n += k
        return self.n


class Prog:

    def __init__(self):
        self.q = {e: [] for e in ENGS}

    def op(self, eng, fn, inc=None, k=1):
        if inc is None:
            self.q[eng].append(fn)
            return None
        self.q[eng].append(lambda e, _f=fn, _h=inc.h, _k=k: _f(e).then_inc(_h, _k))
        return inc.bump(k)

    def dma(self, eng, sem, out, in_):
        return self.op(eng, lambda e, _o=out, _i=in_: e.dma_start(out=_o, in_=_i),
                       inc=sem, k=16)

    def wait(self, eng, sem, val):
        if val is not None and val > 0:
            self.q[eng].append(lambda e, _h=sem.h, _v=val: e.wait_ge(_h, _v))

    def install(self, block):
        def run(ops):
            def body(e):
                for f in ops:
                    f(e)
            return body
        if self.q["t"]:
            block.tensor(run(self.q["t"]))
        if self.q["v"]:
            block.vector(run(self.q["v"]))
        if self.q["s"]:
            block.scalar(run(self.q["s"]))
        if self.q["g"]:
            block.gpsimd(run(self.q["g"]))
        if self.q["y"]:
            block.sync(run(self.q["y"]))


def make_sems(nc, es, names):
    return {n: Sem(es.enter_context(nc.semaphore(n))) for n in names}


def ap2(tensor_handle, offset, pattern):
    return bass.AP(tensor_handle, offset, pattern)



def emit_phase_p(nc, prog, es, sems, W1, W3, W2, W1T, W3T, W2T, ident_f32_sb, cpool_sb):
    pld = [sems["p_ld0"], sems["p_ld1"]]
    w2ld = sems["p_w2ld"]
    tr = sems["p_tr"]
    evs = sems["p_ev"]
    stw = [sems["p_st0"], sems["p_st1"]]
    stw2 = [sems["p_st20"], sems["p_st21"]]
    bfv = sems["p_bfv"]
    bfg = sems["p_bfg"]
    bfs = sems["p_bfs"]

    in2 = [es.enter_context(nc.sbuf_tensor(f"p_in{i}", [128, 2 * D], F32)) for i in range(2)]
    col_t = [es.enter_context(nc.sbuf_tensor(f"p_col{i}", [128, 16 * 128], BF16)) for i in range(2)]
    w2in = es.enter_context(nc.sbuf_tensor("p_w2in", [128, HID], F32))
    w2a = es.enter_context(nc.sbuf_tensor("p_w2a", [128, HID], F32))
    w2b = es.enter_context(nc.sbuf_tensor("p_w2b", [128, HID], F32))
    col2_t = [es.enter_context(nc.sbuf_tensor(f"p_col2{i}", [128, 64 * 128], BF16)) for i in range(2)]
    ps_tr = [es.enter_context(nc.psum_tensor(f"p_ps{i}", [128, 512], F32)) for i in range(2)]

    gseq = [0]
    tr_marks = {}
    ev_marks = {}
    ld_marks = {}
    st_marks = {}
    st2_marks = {}
    pc_last_group = {}
    w2_last_group = {}
    s0_marks = {}
    w2ld_marks = {}
    bf_last = {}

    def group(src_fn, evac_fn):
        g = gseq[0]
        gseq[0] += 1
        if g >= 2:
            prog.wait("t", evs, ev_marks[g - 2])
        for k in range(4):
            fn = src_fn(k, ps_tr[g % 2])
            if k == 3:
                tr_marks[g] = prog.op("t", fn, inc=tr)
            else:
                prog.op("t", fn)
        prog.wait("s", tr, tr_marks[g])
        ev_marks[g] = prog.op("s", evac_fn(ps_tr[g % 2]), inc=evs)
        return g

    def emit_w1w3_pair(pc):
        wi = pc // 32
        Wsrc, Wdst = (W1, W1T) if wi == 0 else (W3, W3T)
        hpair = pc % 32
        slot = pc % 2
        if pc >= 2:
            prog.wait("y", tr, tr_marks[pc_last_group[pc - 2]])
        row0 = hpair * 2 * 128
        src = ap2(Wsrc.tensor, row0 * D, [[D, 128], [128 * D, 2], [1, D]])
        ld_marks[pc] = prog.dma("y", pld[slot], in2[slot][:], src)
        prog.wait("t", pld[slot], ld_marks[pc])
        for sub in range(2):
            hc = hpair * 2 + sub
            csl = hc % 2
            for grp in range(4):
                def mk(k, _ps, _sub=sub, _grp=grp, _slot=slot):
                    dc = _grp * 4 + k
                    return (lambda tt, _p=_ps, _k=k, _in=in2[_slot], _dc=dc, _id=ident_f32_sb,
                            _off=_sub * D:
                            tt.transpose(_p[:, _k * 128:(_k + 1) * 128],
                                         _in[:, _off + _dc * 128:_off + (_dc + 1) * 128],
                                         _id[:]))

                def mkev(_ps, _grp=grp, _csl=csl, _hc=hc, _wi=wi):
                    def f(s_, _p=_ps, _c=col_t[_csl], _g=_grp):
                        return s_.copy(_c[:, _g * 512:(_g + 1) * 512], _p[:])
                    return f
                if grp == 0 and (wi * 64 + hc) >= 2:
                    prog.wait("s", stw[csl], st_marks[wi * 64 + hc - 2])
                g = group(mk, mkev)
            pc_last_group[pc] = g
            prog.wait("s", evs, ev_marks[g])
            dst = ap2(Wdst.tensor, hc * 128, [[HID, 128], [128 * HID, 16], [1, 128]])
            st_marks[wi * 64 + hc] = prog.dma("s", stw[csl], dst, col_t[csl][:])

    def emit_w2_chunk(dc):
        if dc >= 1:
            prog.wait("y", bfv, s0_marks[(dc - 1, "v")])
            prog.wait("y", bfg, s0_marks[(dc - 1, "g")])
            prog.wait("y", bfv, s0_marks[(dc - 1, "s")])
        w2ld_marks[dc] = prog.dma("y", w2ld, w2in[:], W2[dc * 128:(dc + 1) * 128, :])
        for e in ("v", "g"):
            prog.wait(e, w2ld, w2ld_marks[dc])
        for st_i in range(6):
            dist = 128 * (2 ** st_i)
            npair = HID // (2 * dist)
            src = w2in if st_i == 0 else (w2a if st_i % 2 == 1 else w2b)
            dst_b = w2a if st_i % 2 == 0 else w2b
            prog.wait("v", bfg, bfg.n)
            prog.wait("g", bfv, bfv.n)
            if st_i == 1 and dc >= 1:
                for e in ("v", "g", "s"):
                    prog.wait(e, tr, tr_marks[w2_last_group[dc - 1]])

            a_ap = ap2(src, 0, [[HID, 128], [2 * dist, npair], [1, dist]])
            b_ap = ap2(src, dist, [[HID, 128], [2 * dist, npair], [1, dist]])
            lo_ap = ap2(dst_b, 0, [[HID, 128], [2 * dist, npair], [1, dist]])
            mlo = prog.op("v", lambda v, _o=lo_ap, _a=a_ap, _b=b_ap:
                          v.tensor_tensor(_o, _a, _b, ALU.add), inc=bfv)
            if dist < 2048:
                kv = max(npair // 8, 1)
                avv = ap2(src, 0, [[HID, 128], [2 * dist, kv], [1, dist]])
                bvv = ap2(src, dist, [[HID, 128], [2 * dist, kv], [1, dist]])
                hvv = ap2(dst_b, dist, [[HID, 128], [2 * dist, kv], [1, dist]])
                agg = ap2(src, kv * 2 * dist, [[HID, 128], [2 * dist, npair - kv], [1, dist]])
                bgg = ap2(src, kv * 2 * dist + dist, [[HID, 128], [2 * dist, npair - kv], [1, dist]])
                hgg = ap2(dst_b, kv * 2 * dist + dist, [[HID, 128], [2 * dist, npair - kv], [1, dist]])
            else:
                kv = dist // 8
                avv = ap2(src, 0, [[HID, 128], [2 * dist, npair], [1, kv]])
                bvv = ap2(src, dist, [[HID, 128], [2 * dist, npair], [1, kv]])
                hvv = ap2(dst_b, dist, [[HID, 128], [2 * dist, npair], [1, kv]])
                agg = ap2(src, kv, [[HID, 128], [2 * dist, npair], [1, dist - kv]])
                bgg = ap2(src, dist + kv, [[HID, 128], [2 * dist, npair], [1, dist - kv]])
                hgg = ap2(dst_b, dist + kv, [[HID, 128], [2 * dist, npair], [1, dist - kv]])
            mv = prog.op("v", lambda v, _o=hvv, _a=avv, _b=bvv:
                         v.tensor_tensor(_o, _a, _b, ALU.subtract), inc=bfv)
            mg = prog.op("g", lambda g, _o=hgg, _a=agg, _b=bgg:
                         g.tensor_tensor(_o, _a, _b, ALU.subtract), inc=bfg)
            if st_i == 0:
                s0_marks[(dc, "v")] = mv
                s0_marks[(dc, "g")] = mg
                s0_marks[(dc, "s")] = mlo
        bf_last[dc] = (bfv.n, bfg.n)

    def emit_w2_transposes(dc):
        prog.wait("t", bfv, bf_last[dc][0])
        prog.wait("t", bfg, bf_last[dc][1])
        for grp in range(16):
            def mk(k, _ps, _grp=grp):
                hb = _grp * 4 + k
                return (lambda tt, _p=_ps, _k=k, _hb=hb, _id=ident_f32_sb, _w=w2b:
                        tt.transpose(_p[:, _k * 128:(_k + 1) * 128],
                                     _w[:, _hb * 128:(_hb + 1) * 128], _id[:]))

            def mkev(_ps, _grp=grp, _dc=dc):
                def f(s_, _p=_ps, _c=col2_t[_dc % 2], _g=_grp, _cp=cpool_sb):
                    return s_.activation(_c[:, _g * 512:(_g + 1) * 512], _p[:],
                                         AF.Identity, scale=_cp[:, CP_WHT:CP_WHT + 1])
                return f
            if grp == 0 and dc >= 2:
                prog.wait("s", stw2[dc % 2], st2_marks[dc - 2])
            g = group(mk, mkev)
        w2_last_group[dc] = g
        prog.wait("y", evs, ev_marks[g])
        dst = ap2(W2T.tensor, dc * 128, [[D, 128], [128 * D, 64], [1, 128]])
        st2_marks[dc] = prog.dma("y", stw2[dc % 2], dst, col2_t[dc % 2][:])

    for step in range(64):
        if step % 4 == 0:
            if step >= 4:
                emit_w2_transposes(step // 4 - 1)
            emit_w2_chunk(step // 4)
        emit_w1w3_pair(step)
    emit_w2_transposes(15)
    return st_marks


def build_phase_p_standalone(d):
    nc = bass.Bass("TRN2", target_bir_lowering=False, debug=False,
                   detect_race_conditions=False)
    W1 = nc.dram_tensor("W1", [HID, D], F32, kind="ExternalInput").ap()
    W3 = nc.dram_tensor("W3", [HID, D], F32, kind="ExternalInput").ap()
    W2 = nc.dram_tensor("W2", [D, HID], F32, kind="ExternalInput").ap()
    identf = nc.dram_tensor("identf", [128, 128], F32, kind="ExternalInput").ap()
    cpool_d = nc.dram_tensor("cpool", [128, 12], F32, kind="ExternalInput").ap()
    W1T = nc.dram_tensor("W1T", [D, HID], BF16, kind="ExternalOutput").ap()
    W3T = nc.dram_tensor("W3T", [D, HID], BF16, kind="ExternalOutput").ap()
    W2T = nc.dram_tensor("W2T", [HID, D], BF16, kind="ExternalOutput").ap()

    with contextlib.ExitStack() as es:
        ident_sb = es.enter_context(nc.sbuf_tensor("ident_sb", [128, 128], F32))
        cpool_sb = es.enter_context(nc.sbuf_tensor("cpool_sb", [128, 12], F32))
        sems = make_sems(nc, es, P_SEMS + ["init"])
        prog = Prog()
        prog.dma("y", sems["init"], ident_sb[:], identf)
        im = prog.dma("y", sems["init"], cpool_sb[:], cpool_d)
        prog.wait("t", sems["init"], im)
        prog.wait("s", sems["init"], im)
        emit_phase_p(nc, prog, es, sems, W1, W3, W2, W1T, W3T, W2T, ident_sb, cpool_sb)
        for nme in ("p_st0", "p_st1", "p_st20", "p_st21"):
            prog.wait("y", sems[nme], sems[nme].n)
        with nc.Block("phasep") as blk:
            prog.install(blk)
    return nc



def emit_phase_a1(nc, prog, es, sems, io, cst):
    x, Wq, Wk, Wv = io["x"], io["Wq"], io["Wk"], io["Wv"]
    Qs, Ks, Vs = io["Qs"], io["Ks"], io["Vs"]
    cc_d, ss_d, qsT_d, ksT_d = io["cc"], io["ss"], io["qsT"], io["ksT"]
    ident_bf = cst["ident_bf"]
    ones_col = cst["ones_col"]
    ones_row_f = cst["ones_row_f"]
    cpool = cst["cpool"]
    pairswap = cst["pairswap"]

    a_x = [sems["a_x0"], sems["a_x1"]]
    a_cs = [sems["a_cs0"], sems["a_cs1"]]
    a_wl = sems["a_wl"]
    a_s1 = sems["a_s1"]
    a_v1 = sems["a_v1"]
    a_tr = sems["a_tr"]
    a_mm = sems["a_mm"]
    a_sq = sems["a_sq"]
    a_sk = sems["a_sk"]
    a_sv = [sems["a_sv0"], sems["a_sv1"]]

    wq_sb = es.enter_context(nc.sbuf_tensor("a_wq", [128, 16 * 2048], BF16))
    wk_sb = es.enter_context(nc.sbuf_tensor("a_wk", [128, 16 * 1024], BF16))
    wv_sb = es.enter_context(nc.sbuf_tensor("a_wv", [128, 16 * 1024], BF16))
    qs_sb = es.enter_context(nc.sbuf_tensor("a_qsc", [128, 16], F32))
    ks_sb = es.enter_context(nc.sbuf_tensor("a_ksc", [128, 8], F32))
    x_sb = [es.enter_context(nc.sbuf_tensor(f"a_x{i}", [128, D], F32)) for i in range(2)]
    ln_tm = [es.enter_context(nc.sbuf_tensor(f"a_ln{i}", [128, D], BF16)) for i in range(2)]
    lf = [es.enter_context(nc.sbuf_tensor(f"a_lf{i}", [128, 16 * 256], BF16)) for i in range(2)]
    ccs = [es.enter_context(nc.sbuf_tensor(f"a_cc{i}", [128, 256], F32)) for i in range(3)]
    sss = [es.enter_context(nc.sbuf_tensor(f"a_ssl{i}", [128, 256], F32)) for i in range(3)]
    stats = [es.enter_context(nc.sbuf_tensor(f"a_stt{i}", [128, 16], F32)) for i in range(2)]
    qf = [[es.enter_context(nc.sbuf_tensor(f"a_qf{i}{c}", [128, 256], F32)) for c in range(2)]
          for i in range(2)]
    sq = [[es.enter_context(nc.sbuf_tensor(f"a_sq{i}{c}", [128, 256], BF16)) for c in range(2)]
          for i in range(2)]
    qn = [es.enter_context(nc.sbuf_tensor("a_qn0", [128, 256], F32))] * 2
    shufb = [es.enter_context(nc.sbuf_tensor("a_sh0", [128, 256], F32))] * 2
    rt1 = [es.enter_context(nc.sbuf_tensor("a_r10", [128, 256], F32))] * 2
    rt2 = [es.enter_context(nc.sbuf_tensor("a_r20", [128, 256], F32))] * 2
    rows = [es.enter_context(nc.sbuf_tensor(f"a_ro{i}", [1, 3 * 256], F32)) for i in range(2)]
    qout = [es.enter_context(nc.sbuf_tensor("a_qo0", [128, 16 * 256], BF16))] * 2
    kout = [es.enter_context(nc.sbuf_tensor("a_ko0", [128, 8 * 256], BF16))] * 2
    vout = [es.enter_context(nc.sbuf_tensor("a_vo0", [128, 1024], BF16))] * 2

    ps_ltr = [es.enter_context(nc.psum_tensor(f"a_pl{i}", [128, 512], BF16)) for i in range(2)]
    ps_v = es.enter_context(nc.psum_tensor("a_pv", [128, 1024], F32))
    ps_q = es.enter_context(nc.psum_tensor("a_pq", [128, 1024], F32))
    ps_misc = es.enter_context(nc.psum_tensor("a_pm", [128, 1024], F32))

    wl_marks = []
    for dc in range(16):
        wl_marks.append(prog.dma("g", a_wl, wq_sb[:, dc * 2048:(dc + 1) * 2048],
                                 Wq[dc * 128:(dc + 1) * 128, :]))
    for dc in range(16):
        wl_marks.append(prog.dma("g", a_wl, wk_sb[:, dc * 1024:(dc + 1) * 1024],
                                 Wk[dc * 128:(dc + 1) * 128, :]))
    for dc in range(16):
        wl_marks.append(prog.dma("g", a_wl, wv_sb[:, dc * 1024:(dc + 1) * 1024],
                                 Wv[dc * 128:(dc + 1) * 128, :]))
    wl_marks.append(prog.dma("g", a_wl, qs_sb[:], qsT_d))
    wl_marks.append(prog.dma("g", a_wl, ks_sb[:], ksT_d))
    wl_all = wl_marks[-1]
    wl_vv = wl_marks[7]

    lnm, redm, trm, evm = {}, {}, {}, {}
    vproj, vev = {}, {}
    qmm, evq, sqm, ssm, wrm, rsm, repm, g6m = {}, {}, {}, {}, {}, {}, {}, {}
    stm = {}
    lffree = {}
    csl = {}
    g6m_all = {}

    def emit_ln(t):
        sl = t % 2
        if t >= 2:
            prog.wait("y", a_s1, lnm[t - 2])
            prog.wait("y", a_v1, redm[t - 2])
        xm = prog.dma("y", a_x[sl], x_sb[sl][:], x[t * 128:(t + 1) * 128, :])
        st_ = stats[sl]
        prog.wait("s", a_x[sl], xm)
        if t >= 2:
            prog.wait("s", a_tr, trm[(t - 2, 3)])
        sqm_t = prog.op("s", lambda s_, _x=x_sb[sl], _d=ln_tm[sl], _a=st_:
                        s_.activation(_d[:], _x[:], AF.Square, accum_out=_a[:, 1:2]),
                        inc=a_s1)
        prog.wait("v", a_x[sl], xm)
        redm[t] = prog.op("v", lambda v, _x=x_sb[sl], _a=st_:
                          v.tensor_reduce(_a[:, 0:1], _x[:], AX.X, ALU.add), inc=a_v1)
        prog.wait("v", a_v1, redm[t])
        mb = prog.op("v", lambda v, _a=st_:
                     v.tensor_scalar(_a[:, 2:3], _a[:, 0:1], 1.0 / D, None, ALU.mult),
                     inc=a_v1)
        prog.wait("v", a_s1, sqm_t)
        prog.op("v", lambda v, _a=st_:
                v.tensor_scalar(_a[:, 3:4], _a[:, 1:2], 1.0 / D, None, ALU.mult))
        prog.wait("v", a_v1, mb)
        mb = prog.op("v", lambda v, _a=st_:
                     v.tensor_tensor(_a[:, 4:5], _a[:, 2:3], _a[:, 2:3], ALU.mult),
                     inc=a_v1)
        prog.wait("v", a_v1, mb)
        mb = prog.op("v", lambda v, _a=st_:
                     v.tensor_tensor(_a[:, 5:6], _a[:, 3:4], _a[:, 4:5], ALU.subtract),
                     inc=a_v1)
        prog.wait("v", a_v1, mb)
        mb = prog.op("v", lambda v, _a=st_:
                     v.tensor_scalar(_a[:, 6:7], _a[:, 5:6], LN_EPS, None, ALU.add),
                     inc=a_v1)
        prog.wait("v", a_v1, mb)
        vm = prog.op("v", lambda v, _a=st_:
                     v.reciprocal(_a[:, 7:8], _a[:, 6:7]), inc=a_v1)
        prog.wait("s", a_v1, vm)
        m1 = prog.op("s", lambda s_, _a=st_:
                     s_.activation(_a[:, 8:9], _a[:, 7:8], AF.Sqrt), inc=a_s1)
        prog.wait("s", a_s1, m1)
        m2 = prog.op("s", lambda s_, _a=st_:
                     s_.activation(_a[:, 9:10], _a[:, 2:3], AF.Identity,
                                   scale=_a[:, 8:9]), inc=a_s1)
        prog.wait("s", a_s1, m2)
        m3 = prog.op("s", lambda s_, _a=st_, _cp=cpool:
                     s_.activation(_a[:, 10:11], _a[:, 9:10], AF.Identity,
                                   scale=_cp[:, CP_NEG1:CP_NEG1 + 1]), inc=a_s1)
        prog.wait("s", a_s1, m3)
        lnm[t] = prog.op("s", lambda s_, _x=x_sb[sl], _d=ln_tm[sl], _a=st_:
                         s_.activation(_d[:], _x[:], AF.Identity, scale=_a[:, 8:9],
                                       bias=_a[:, 10:11]), inc=a_s1)

    def emit_transpose(t, p):
        sl = t % 2
        side = t - 2 * p
        prog.wait("t", a_s1, lnm[t])
        for grp in range(4):
            gseq = t * 4 + grp
            if gseq >= 2:
                prog.wait("t", a_v1, evm[gseq - 2])
            for k in range(4):
                dc = grp * 4 + k
                fn = (lambda tt, _ps=ps_ltr[gseq % 2], _k=k, _ln=ln_tm[sl], _dc=dc,
                      _id=ident_bf:
                      tt.transpose(_ps[:, _k * 128:(_k + 1) * 128],
                                   _ln[:, _dc * 128:(_dc + 1) * 128], _id[:]))
                if k == 3:
                    trm[(t, grp)] = prog.op("t", fn, inc=a_tr)
                else:
                    prog.op("t", fn)
            prog.wait("v", a_tr, trm[(t, grp)])
            if p >= 2 and grp == 0 and side == 0:
                prev = lffree.get(p - 2)
                if prev is not None:
                    prog.wait("v", a_mm, prev)
            dst = ap2(lf[p % 2], grp * 4 * 256 + side * 128,
                      [[16 * 256, 128], [256, 4], [1, 128]])
            evm[t * 4 + grp] = prog.op(
                "v", lambda v, _d=dst, _ps=ps_ltr[gseq % 2]: v.tensor_copy(_d, _ps[:]),
                inc=a_v1)

    def emit_vproj(t, p):
        side = t - 2 * p
        prog.wait("t", a_v1, evm[t * 4 + 3])
        if t == 0:
            prog.wait("t", a_wl, wl_all)
        if t >= 1:
            prog.wait("t", a_v1, vev[t - 1])
        for dc in range(16):
            for half in range(2):
                fn = (lambda tt, _lf=lf[p % 2], _dc=dc, _side=side, _w=wv_sb, _h=half,
                      _ps=ps_v, _st=(dc == 0), _sp=(dc == 15):
                      tt.matmul(_ps[:, _h * 512:(_h + 1) * 512],
                                _lf[:, _dc * 256 + _side * 128:_dc * 256 + _side * 128 + 128],
                                _w[:, _dc * 1024 + _h * 512:_dc * 1024 + (_h + 1) * 512],
                                start=_st, stop=_sp))
                if dc == 15 and half == 1:
                    vproj[t] = prog.op("t", fn, inc=a_mm)
                else:
                    prog.op("t", fn)
        prog.wait("v", a_mm, vproj[t])
        if t >= 1:
            prog.wait("v", a_sv[(t - 1) % 2], stm[("v", t - 1)])
        vev[t] = prog.op("v", lambda v, _o=vout[t % 2], _ps=ps_v:
                         v.tensor_copy(_o[:], _ps[:]), inc=a_v1)
        prog.wait("g", a_v1, vev[t])
        stm[("v", t)] = prog.dma("g", a_sv[t % 2], Vs[t * 128:(t + 1) * 128, :], vout[t % 2][:])

    def head_params(h, p):
        if h < 8:
            return dict(w=wq_sb, wstride=2048, b0=2 * h, scol=qs_sb, sc=(2 * h, 2 * h + 1),
                        out=qout[p % 2], oblk=2 * h, sqrt_cp=CP_Q256)
        hk = h - 8
        return dict(w=wk_sb, wstride=1024, b0=2 * hk, scol=ks_sb, sc=(2 * hk, 2 * hk + 1),
                    out=kout[p % 2], oblk=2 * hk, sqrt_cp=CP_ONE)

    def emit_proj_head(h, p):
        hp = head_params(h, p)
        st_set = h % 2
        if h >= 2:
            prog.wait("t", a_v1, evq[h - 2])
        if p == 0 and h == 0:
            prog.wait("t", a_wl, wl_all)
        for c in range(2):
            b = hp["b0"] + c
            for dc in range(16):
                fn = (lambda tt, _w=hp["w"], _dc=dc, _b=b, _ws=hp["wstride"],
                      _lf=lf[p % 2], _ps=ps_q, _off=st_set * 512 + c * 256,
                      _st=(dc == 0), _sp=(dc == 15):
                      tt.matmul(_ps[:, _off:_off + 256],
                                _w[:, _dc * _ws + _b * 128:_dc * _ws + _b * 128 + 128],
                                _lf[:, _dc * 256:(_dc + 1) * 256],
                                start=_st, stop=_sp))
                if c == 1 and dc == 15:
                    qmm[h] = prog.op("t", fn, inc=a_mm)
                else:
                    prog.op("t", fn)

    def emit_evq(h):
        st_set = h % 2
        prog.wait("v", a_mm, qmm[h])
        if h >= 2:
            prog.wait("v", a_s1, sqm[h - 2])
        for c in range(2):
            fn = (lambda v, _qf=qf[st_set][c], _ps=ps_q, _off=st_set * 512 + c * 256:
                  v.tensor_copy(_qf[:], _ps[:, _off:_off + 256]))
            if c == 1:
                evq[h] = prog.op("v", fn, inc=a_v1)
            else:
                prog.op("v", fn)

    def emit_sq(h):
        st_set = h % 2
        prog.wait("s", a_v1, evq[h])
        for c in range(2):
            fn = (lambda s_, _qf=qf[st_set][c], _sq=sq[st_set][c]:
                  s_.activation(_sq[:], _qf[:], AF.Square))
            if c == 1:
                sqm[h] = prog.op("s", fn, inc=a_s1)
            else:
                prog.op("s", fn)

    def emit_ssmm(h):
        st_set = h % 2
        prog.wait("t", a_s1, sqm[h])
        if h >= 2:
            prog.wait("t", a_v1, g6m[h - 2])
        for c in range(2):
            fn = (lambda tt, _sq=sq[st_set][c], _ps=ps_misc,
                  _off=st_set * 512 + 256,
                  _oc=ones_col, _st=(c == 0), _sp=(c == 1):
                  tt.matmul(_ps[0:1, _off:_off + 256], _oc[:], _sq[:],
                            start=_st, stop=_sp))
            if c == 1:
                ssm[h] = prog.op("t", fn, inc=a_mm)
            else:
                prog.op("t", fn)

    def emit_rstd(h, p):
        st_set = h % 2
        hp = head_params(h, p)
        prog.wait("s", a_mm, ssm[h])
        m1 = prog.op("s", lambda s_, _ro=rows[st_set], _ps=ps_misc,
                     _off=st_set * 512 + 256, _cp=cpool:
                     s_.activation(_ro[0:1, 0:256], _ps[0:1, _off:_off + 256], AF.Ln,
                                   scale=_cp[0:1, CP_Q256:CP_Q256 + 1],
                                   bias=_cp[0:1, CP_RMSEPS:CP_RMSEPS + 1]), inc=a_s1)
        prog.wait("s", a_s1, m1)
        bci = CP_LN16N if h < 8 else CP_ZERO
        rsm[h] = prog.op("s", lambda s_, _ro=rows[st_set], _cp=cpool, _b=bci:
                         s_.activation(_ro[0:1, 512:768], _ro[0:1, 0:256], AF.Exp,
                                       scale=_cp[0:1, CP_NEGHALF:CP_NEGHALF + 1],
                                       bias=_cp[0:1, _b:_b + 1]), inc=a_s1)

    def emit_repmm(h):
        st_set = h % 2
        prog.wait("t", a_s1, rsm[h])
        repm[h] = prog.op("t", lambda tt, _or=ones_row_f, _ro=rows[st_set],
                          _ps=ps_misc, _off=st_set * 512:
                          tt.matmul(_ps[:, _off:_off + 256], _or[:], _ro[0:1, 512:768],
                                    start=True, stop=True), inc=a_mm)

    def emit_group6(h, p):
        hp = head_params(h, p)
        st_set = h % 2
        prog.wait("v", a_mm, repm[h])
        if h == 0:
            if p == 0:
                prog.wait("v", a_wl, wl_all)
            prog.wait("v", a_cs[p % 2], csl[p])
            if p >= 1:
                prog.wait("v", a_sq, stm[("q", 2 * p - 1)])
                prog.wait("v", a_sk, stm[("k", p - 1)])
        b0, b1 = hp["sc"]
        out = hp["out"]
        ob = hp["oblk"]
        prog.op("v", lambda v, _o=out, _ob=ob, _qf=qf[st_set][1], _sc=hp["scol"],
                _b1=b1, _ps=ps_misc, _off=st_set * 512:
                v.scalar_tensor_tensor(_o[:, (_ob + 1) * 256:(_ob + 2) * 256],
                                       _qf[:], _sc[:, _b1:_b1 + 1],
                                       _ps[:, _off:_off + 256], ALU.mult, ALU.mult))
        mb = prog.op("v", lambda v, _qn=qn[st_set], _qf=qf[st_set][0], _sc=hp["scol"],
                     _b0=b0, _ps=ps_misc, _off=st_set * 512:
                     v.scalar_tensor_tensor(_qn[:], _qf[:], _sc[:, _b0:_b0 + 1],
                                            _ps[:, _off:_off + 256], ALU.mult, ALU.mult),
                     inc=a_v1)
        prog.wait("v", a_v1, mb)
        mb = prog.op("v", lambda v, _sh=shufb[st_set], _qn=qn[st_set], _m=pairswap:
                     v.stream_shuffle(_sh[:], _qn[:], _m), inc=a_v1)
        prog.wait("v", a_v1, mb)
        prog.op("v", lambda v, _a=rt1[st_set], _qn=qn[st_set], _c=ccs[p % 3]:
                v.tensor_tensor(_a[:], _qn[:], _c[:], ALU.mult))
        mb = prog.op("v", lambda v, _b=rt2[st_set], _sh=shufb[st_set], _s=sss[p % 3]:
                     v.tensor_tensor(_b[:], _sh[:], _s[:], ALU.mult), inc=a_v1)
        prog.wait("v", a_v1, mb)
        g6m[h] = prog.op("v", lambda v, _o=out, _ob=ob, _a=rt1[st_set], _b=rt2[st_set]:
                         v.tensor_tensor(_o[:, _ob * 256:(_ob + 1) * 256],
                                         _a[:], _b[:], ALU.add), inc=a_v1)

    NTP = NT // 2

    def emit_prologue_a(p):
        t0, t1 = 2 * p, 2 * p + 1
        if p >= 3:
            prog.wait("y", a_v1, g6m_all[p - 3])
        prog.dma("y", a_cs[p % 2], ccs[p % 3][:], cc_d[:, p * 256:(p + 1) * 256])
        csl[p] = prog.dma("y", a_cs[p % 2], sss[p % 3][:], ss_d[:, p * 256:(p + 1) * 256])
        emit_ln(t0)
        emit_ln(t1)

    def emit_prologue_b(p):
        t0, t1 = 2 * p, 2 * p + 1
        emit_transpose(t0, p)
        emit_transpose(t1, p)
        emit_vproj(t0, p)
        emit_vproj(t1, p)

    emit_prologue_a(0)
    emit_prologue_b(0)
    for p in range(NTP):
        for h in range(12):
            emit_proj_head(h, p)
            if h >= 1:
                emit_ssmm(h - 1)
            if h >= 2:
                emit_repmm(h - 2)
                emit_group6(h - 2, p)
            emit_evq(h)
            emit_sq(h)
            if h >= 1:
                emit_rstd(h - 1, p)
            if h == 2 and p + 1 < NTP:
                emit_prologue_a(p + 1)
            if h == 6 and p + 1 < NTP:
                emit_prologue_b(p + 1)
        emit_ssmm(11)
        emit_repmm(10)
        emit_group6(10, p)
        emit_rstd(11, p)
        emit_repmm(11)
        emit_group6(11, p)
        g6m_all[p] = g6m[11]
        lffree[p] = qmm[11]
        prog.wait("g", a_v1, g6m[7])
        stm[("q", 2 * p)] = stm[("q", 2 * p + 1)] = prog.dma(
            "g", a_sq, Qs[p * 128:(p + 1) * 128, :], qout[p % 2][:])
        prog.wait("g", a_v1, g6m[11])
        stm[("k", p)] = prog.dma("g", a_sk, Ks[p * 128:(p + 1) * 128, :], kout[p % 2][:])
    if "dbg" in io:
        dbg = io["dbg"]
        dsem = sems["a_dbg"]
        prog.wait("g", a_v1, a_v1.n)
        prog.wait("g", a_s1, a_s1.n)
        prog.wait("g", a_mm, a_mm.n)
        for nm_, buf in (("dbg_x", x_sb[0]), ("dbg_ln", ln_tm[0]), ("dbg_lf", lf[0]),
                         ("dbg_vout", vout[0]), ("dbg_qf0", qf[0][0]), ("dbg_qf1", qf[0][1]),
                         ("dbg_rows", rows[0]), ("dbg_qn", qn[0]), ("dbg_cc", ccs[0]),
                         ("dbg_id", cst["ident_bf"]), ("dbg_cp", cst["cpool"]),
                         ("dbg_st", stats[0]),
                         ("dbg_oc", cst["ones_col"]), ("dbg_or", cst["ones_row_f"])):
            if nm_ in dbg:
                prog.dma("g", dsem, dbg[nm_], buf[:])
        prog.wait("g", dsem, dsem.n)
    return stm


def build_phase_a1_standalone():
    nc = bass.Bass("TRN2", target_bir_lowering=False, debug=False,
                   detect_race_conditions=False)
    x = nc.dram_tensor("x", [S, D], F32, kind="ExternalInput").ap()
    Wq = nc.dram_tensor("Wq", [D, 2048], F32, kind="ExternalInput").ap()
    Wk = nc.dram_tensor("Wk", [D, 1024], F32, kind="ExternalInput").ap()
    Wv = nc.dram_tensor("Wv", [D, 1024], F32, kind="ExternalInput").ap()
    qsT = nc.dram_tensor("qsT", [128, 16], F32, kind="ExternalInput").ap()
    ksT = nc.dram_tensor("ksT", [128, 8], F32, kind="ExternalInput").ap()
    cc_d = nc.dram_tensor("cc", [128, S], F32, kind="ExternalInput").ap()
    ss_d = nc.dram_tensor("ss", [128, S], F32, kind="ExternalInput").ap()
    identb = nc.dram_tensor("identb", [128, 128], BF16, kind="ExternalInput").ap()
    onescol = nc.dram_tensor("onescol", [128, 1], BF16, kind="ExternalInput").ap()
    onesrow = nc.dram_tensor("onesrow", [1, 128], F32, kind="ExternalInput").ap()
    cpool_d = nc.dram_tensor("cpool", [128, 12], F32, kind="ExternalInput").ap()
    Qs = nc.dram_tensor("Qs", [(NT // 2) * 128, 4096], BF16, kind="ExternalOutput").ap()
    Ks = nc.dram_tensor("Ks", [(NT // 2) * 128, 2048], BF16, kind="ExternalOutput").ap()
    Vs = nc.dram_tensor("Vs", [S, 1024], BF16, kind="ExternalOutput").ap()

    c = host_constants()
    with contextlib.ExitStack() as es:
        ident_sb = es.enter_context(nc.sbuf_tensor("c_idb", [128, 128], BF16))
        onesc_sb = es.enter_context(nc.sbuf_tensor("c_oc", [128, 1], BF16))
        onesr_sb = es.enter_context(nc.sbuf_tensor("c_or", [1, 128], F32))
        cpool_sb = es.enter_context(nc.sbuf_tensor("c_cp", [128, 12], F32))
        sems = make_sems(nc, es, ["a_x0", "a_x1", "a_cs0", "a_cs1", "a_wl",
                                  "a_s1", "a_v1", "a_tr", "a_mm",
                                  "a_sq", "a_sk", "a_sv0", "a_sv1", "init"])
        prog = Prog()
        ini = sems["init"]
        prog.dma("y", ini, ident_sb[:], identb)
        prog.dma("y", ini, onesc_sb[:], onescol)
        prog.dma("y", ini, onesr_sb[:], onesrow)
        im = prog.dma("y", ini, cpool_sb[:], cpool_d)
        for e in ("t", "v", "s"):
            prog.wait(e, ini, im)
        io = dict(x=x, Wq=Wq, Wk=Wk, Wv=Wv, qsT=qsT, ksT=ksT, cc=cc_d, ss=ss_d,
                  Qs=Qs, Ks=Ks, Vs=Vs)
        cst = dict(ident_bf=ident_sb, ones_col=onesc_sb, ones_row_f=onesr_sb,
                   cpool=cpool_sb, pairswap=c["pairswap"])
        emit_phase_a1(nc, prog, es, sems, io, cst)
        for nme in ("a_sq", "a_sk", "a_sv0", "a_sv1"):
            prog.wait("g", sems[nme], sems[nme].n)
        with nc.Block("phasea1") as blk:
            prog.install(blk)
    return nc



def emit_phase_a2(nc, prog, es, sems, io, cst):
    x, Qs, Ks, Vs, Wo, X1 = io["x"], io["Qs"], io["Ks"], io["Vs"], io["Wo"], io["X1"]
    ones_col = cst["ones_col"]
    ones_row_f = cst["ones_row_f"]
    cpool = cst["cpool"]
    pmask = cst["pmask"]

    qld = [sems["b_q0"], sems["b_q1"]]
    kld = [sems["b_k0"], sems["b_k1"]]
    vld = [sems["b_v0"], sems["b_v1"]]
    xld = [sems["b_x0"], sems["b_x1"]]
    wol = sems["b_wol"]
    b_s = sems["b_s"]
    b_v = sems["b_v"]
    b_mm = sems["b_mm"]
    x1st = [sems["b_st0"], sems["b_st1"]]

    wo_sb = es.enter_context(nc.sbuf_tensor("b_wo", [128, 16 * 2048], BF16))
    k_ring = [es.enter_context(nc.sbuf_tensor(f"b_k{i}", [128, 1024], BF16)) for i in range(10)]
    v_ring = [es.enter_context(nc.sbuf_tensor(f"b_vr{i}", [128, 1024], BF16)) for i in range(10)]
    q_cur = [es.enter_context(nc.sbuf_tensor(f"b_qc{i}", [128, 4096], BF16)) for i in range(2)]
    x_sb = [es.enter_context(nc.sbuf_tensor(f"b_x{i}", [128, D], F32)) for i in range(4)]
    p_t = [es.enter_context(nc.sbuf_tensor(f"b_pt{i}", [128, 2560], BF16)) for i in range(2)]
    attn_out = [es.enter_context(nc.sbuf_tensor(f"b_ao{i}", [128, 16 * 256], BF16))
                for i in range(2)]
    r_sb = [es.enter_context(nc.sbuf_tensor(f"b_r{i}", [128, 256], F32)) for i in range(2)]
    rrow = [es.enter_context(nc.sbuf_tensor(f"b_rr{i}", [1, 2 * 256], F32)) for i in range(2)]
    x1_sb = [es.enter_context(nc.sbuf_tensor(f"b_x1{i}", [128, D], F32)) for i in range(2)]
    ps = es.enter_context(nc.psum_tensor("b_ps", [128, 4096], F32))

    PV0 = 2560
    REP0 = 3072
    SUM0 = 3328
    WO0 = 3584
    NP = NT // 2

    for dc in range(16):
        wlm = prog.dma("g", wol, wo_sb[:, dc * 2048:(dc + 1) * 2048],
                       Wo[dc * 128:(dc + 1) * 128, :])
    wol_all = wlm

    qm, km, vm, xm = {}, {}, {}, {}
    scm, summ, repm, pvm, wom = {}, {}, {}, {}, {}
    expm, maskm, rcm, amm, lnm2 = {}, {}, {}, {}, {}
    woev, stx = {}, {}

    def nb_of(pp):
        return min(2 * pp + 2, 10)

    def emit_loads(pp):
        if pp >= NP:
            return
        sl = pp % 2
        if pp >= 2:
            prog.wait("y", b_mm, scm[(pp - 2, 7)])
        qm[pp] = prog.dma("y", qld[sl], q_cur[sl][:], Qs[pp * 128:(pp + 1) * 128, :])
        for side in range(2):
            t = 2 * pp + side
            rs = t % 10
            if pp >= 5:
                prog.wait("y", b_mm, scm[(pp - 1, 7)])
                prog.wait("y", b_mm, pvm[(pp - 1, 7)])
            ksrc = ap2(Ks.tensor, pp * 128 * 2048 + side * 128,
                       [[2048, 128], [256, 8], [1, 128]])
            km[t] = prog.dma("y", kld[sl], k_ring[rs][:], ksrc)
            vm[t] = prog.dma("y", vld[sl], v_ring[rs][:], Vs[t * 128:(t + 1) * 128, :])
            if pp >= 2:
                prog.wait("y", b_v, woev[(pp - 2, side, 3)])
            xm[t] = prog.dma("y", xld[sl], x_sb[t % 4][:], x[t * 128:(t + 1) * 128, :])

    def emit_scores(pp, h):
        nb = nb_of(pp)
        base = 2 * pp + 1 - (nb - 1)
        kv = h // 2
        if h == 0:
            prog.wait("t", qld[pp % 2], qm[pp])
            prog.wait("t", kld[pp % 2], km[2 * pp + 1])
        if (pp, h - 1) in expm:
            prog.wait("t", b_s, expm[(pp, h - 1)])
        elif pp >= 1 and (pp - 1, 7) in expm:
            prog.wait("t", b_s, expm[(pp - 1, 7)])
        for m in range(nb):
            slot = (base + m) % 10
            for c in range(2):
                fn = (lambda t, _ps=ps, _o=m * 256, _k=k_ring[slot],
                      _koff=(2 * kv + c) * 128, _q=q_cur[pp % 2],
                      _qoff=(2 * h + c) * 256, _st=(c == 0), _sp=(c == 1):
                      t.matmul(_ps[:, _o:_o + 256], _k[:, _koff:_koff + 128],
                               _q[:, _qoff:_qoff + 256], start=_st, stop=_sp))
                if m == nb - 1 and c == 1:
                    scm[(pp, h)] = prog.op("t", fn, inc=b_mm)
                else:
                    prog.op("t", fn)

    def emit_tanh(pp, h):
        nb = nb_of(pp)
        w = nb * 256
        prog.wait("s", b_mm, scm[(pp, h)])
        m1 = prog.op("s", lambda s_, _ps=ps, _w=w, _cp=cpool:
                     s_.activation(_ps[:, 0:_w], _ps[:, 0:_w], AF.Tanh,
                                   scale=_cp[:, CP_TANH:CP_TANH + 1]), inc=b_s)
        prog.wait("s", b_s, m1)

    def emit_exp(pp, h):
        nb = nb_of(pp)
        w = nb * 256
        if (pp, h - 2) in pvm:
            prog.wait("s", b_mm, pvm[(pp, h - 2)])
        elif pp >= 1 and (pp - 1, h + 6) in pvm:
            prog.wait("s", b_mm, pvm[(pp - 1, h + 6)])
        expm[(pp, h)] = prog.op("s", lambda s_, _ps=ps, _w=w, _pt=p_t[h % 2], _cp=cpool:
                                s_.activation(_pt[:, 0:_w], _ps[:, 0:_w], AF.Exp,
                                              scale=_cp[:, CP_EXP50:CP_EXP50 + 1]),
                                inc=b_s)

    def emit_lnrow(pp, h):
        sl = h % 2
        prog.wait("s", b_mm, summ[(pp, h)])
        if (pp, h - 2) in repm:
            prog.wait("s", b_mm, repm[(pp, h - 2)])
        elif pp >= 1 and (pp - 1, h + 6) in repm:
            prog.wait("s", b_mm, repm[(pp - 1, h + 6)])
        m1 = prog.op("s", lambda s_, _ps=ps, _ro=rrow[sl]:
                     s_.activation(_ro[0:1, 0:256], _ps[0:1, SUM0:SUM0 + 256], AF.Ln),
                     inc=b_s)
        prog.wait("s", b_s, m1)
        lnm2[(pp, h)] = prog.op("s", lambda s_, _ro=rrow[sl], _cp=cpool:
                                s_.activation(_ro[0:1, 256:512], _ro[0:1, 0:256], AF.Exp,
                                              scale=_cp[0:1, CP_NEG1:CP_NEG1 + 1]),
                                inc=b_s)

    b_g = sems["b_g"]

    def emit_mask(pp, h):
        nb = nb_of(pp)
        prog.wait("g", b_s, expm[(pp, h)])
        if nb == 10:
            mlist = [(0, 0), (1, 1), (8, 2), (9, 3)]
        else:
            mlist = [(nb - 2, 2), (nb - 1, 3)]
        for idx, (m, mi) in enumerate(mlist):
            fn = (lambda g, _pt=p_t[h % 2], _m=pmask[mi], _o=m * 256:
                  g.tensor_tensor(_pt[:, _o:_o + 256], _pt[:, _o:_o + 256],
                                  _m[:], ALU.mult))
            if idx == len(mlist) - 1:
                maskm[(pp, h)] = prog.op("g", fn, inc=b_g)
            else:
                prog.op("g", fn)

    def emit_sums(pp, h):
        nb = nb_of(pp)
        prog.wait("t", b_g, maskm[(pp, h)])
        if (pp, h - 1) in rcm:
            prog.wait("t", b_v, rcm[(pp, h - 1)])
        elif pp >= 1 and (pp - 1, 7) in rcm:
            prog.wait("t", b_v, rcm[(pp - 1, 7)])
        for m in range(nb):
            fn = (lambda t, _ps=ps, _pt=p_t[h % 2], _m=m, _oc=ones_col,
                  _st=(m == 0), _sp=(m == nb - 1):
                  t.matmul(_ps[0:1, SUM0:SUM0 + 256], _oc[:],
                           _pt[:, _m * 256:(_m + 1) * 256], start=_st, stop=_sp))
            if m == nb - 1:
                summ[(pp, h)] = prog.op("t", fn, inc=b_mm)
            else:
                prog.op("t", fn)

    def emit_rep(pp, h):
        prog.wait("t", b_s, lnm2[(pp, h)])
        repm[(pp, h)] = prog.op("t", lambda t, _ps=ps, _or=ones_row_f, _ro=rrow[h % 2]:
                                t.matmul(_ps[:, REP0:REP0 + 256], _or[:],
                                         _ro[0:1, 256:512], start=True, stop=True),
                                inc=b_mm)

    def emit_pv(pp, h):
        nb = nb_of(pp)
        base = 2 * pp + 1 - (nb - 1)
        kv = h // 2
        if h == 0:
            prog.wait("t", vld[pp % 2], vm[2 * pp + 1])
        if (pp, h - 1) in amm:
            prog.wait("t", b_v, amm[(pp, h - 1)])
        elif pp >= 1 and (pp - 1, 7) in amm:
            prog.wait("t", b_v, amm[(pp - 1, 7)])
        for c in range(2):
            for m in range(nb):
                slot = (base + m) % 10
                fn = (lambda t, _ps=ps, _v=v_ring[slot], _voff=kv * 256 + c * 128,
                      _pt=p_t[h % 2], _m=m, _c=c, _st=(m == 0), _sp=(m == nb - 1):
                      t.matmul(_ps[:, PV0 + _c * 256:PV0 + (_c + 1) * 256],
                               _v[:, _voff:_voff + 128],
                               _pt[:, _m * 256:(_m + 1) * 256], start=_st, stop=_sp))
                if m == nb - 1 and c == 1:
                    pvm[(pp, h)] = prog.op("t", fn, inc=b_mm)
                else:
                    prog.op("t", fn)

    def emit_rcopy(pp, h):
        prog.wait("v", b_mm, repm[(pp, h)])
        mb = prog.op("v", lambda v, _r=r_sb[h % 2], _ps=ps:
                     v.tensor_copy(_r[:], _ps[:, REP0:REP0 + 256]), inc=b_v)
        prog.wait("v", b_v, mb)
        rcm[(pp, h)] = mb

    def emit_attnmul(pp, h):
        prog.wait("v", b_mm, pvm[(pp, h)])
        if h == 0 and (pp - 2, 7) in wom:
            prog.wait("v", b_mm, wom[(pp - 2, 7)])
        for c in range(2):
            fn = (lambda v, _ao=attn_out[pp % 2], _o=(2 * h + c) * 256, _ps=ps, _c=c,
                  _r=r_sb[h % 2]:
                  v.tensor_tensor(_ao[:, _o:_o + 256],
                                  _ps[:, PV0 + _c * 256:PV0 + (_c + 1) * 256],
                                  _r[:], ALU.mult))
            if c == 1:
                amm[(pp, h)] = prog.op("v", fn, inc=b_v)
            else:
                prog.op("v", fn)

    def emit_wo_quarter(pp, side, qtr):
        qq = side * 4 + qtr
        if qq == 0:
            prog.wait("t", b_v, amm[(pp, 7)])
        else:
            prog.wait("t", b_v, woev[(pp, (qq - 1) // 4, (qq - 1) % 4)])
        for dc in range(16):
            fn = (lambda t, _ps=ps, _ao=attn_out[pp % 2], _dc=dc, _w=wo_sb, _q=qtr, _s=side,
                  _st=(dc == 0), _sp=(dc == 15):
                  t.matmul(_ps[:, WO0:WO0 + 512],
                           _ao[:, _dc * 256 + _s * 128:_dc * 256 + _s * 128 + 128],
                           _w[:, _dc * 2048 + _q * 512:_dc * 2048 + (_q + 1) * 512],
                           start=_st, stop=_sp))
            if dc == 15:
                wom[(pp, qq)] = prog.op("t", fn, inc=b_mm)
            else:
                prog.op("t", fn)
        prog.wait("v", b_mm, wom[(pp, qq)])
        t = 2 * pp + side
        if qq == 0 or qq == 4:
            prog.wait("v", xld[pp % 2], xm[t])
            if pp >= 2:
                prog.wait("v", x1st[pp % 2], stx[(pp - 2, side)])
        woev[(pp, side, qtr)] = prog.op(
            "v", lambda v, _x1=x1_sb[side], _x=x_sb[t % 4], _ps=ps, _q=qtr:
            v.tensor_tensor(_x1[:, _q * 512:(_q + 1) * 512], _ps[:, WO0:WO0 + 512],
                            _x[:, _q * 512:(_q + 1) * 512], ALU.add), inc=b_v)

    prog.wait("t", wol, wol_all)
    emit_loads(0)
    for pp in range(NP):
        for h in range(8):
            emit_scores(pp, h)
            if h >= 1:
                emit_sums(pp, h - 1)
            emit_tanh(pp, h)
            if h >= 1:
                emit_lnrow(pp, h - 1)
            emit_exp(pp, h)
            if h >= 1:
                emit_pv(pp, h - 1)
                emit_rep(pp, h - 1)
                emit_rcopy(pp, h - 1)
                emit_attnmul(pp, h - 1)
            emit_mask(pp, h)
            if pp >= 1:
                emit_wo_quarter(pp - 1, h // 4, h % 4)
                if h == 7:
                    for side in range(2):
                        prog.wait("g", b_v, woev[(pp - 1, side, 3)])
                        stx[(pp - 1, side)] = prog.dma(
                            "g", x1st[(pp - 1) % 2],
                            X1[(2 * (pp - 1) + side) * 128:(2 * (pp - 1) + side + 1) * 128, :],
                            x1_sb[side][:])
        emit_sums(pp, 7)
        emit_lnrow(pp, 7)
        emit_pv(pp, 7)
        emit_rep(pp, 7)
        emit_rcopy(pp, 7)
        emit_attnmul(pp, 7)
        emit_loads(pp + 1)
    pp = NP - 1
    for side in range(2):
        for qtr in range(4):
            emit_wo_quarter(pp, side, qtr)
        prog.wait("g", b_v, woev[(pp, side, 3)])
        stx[(pp, side)] = prog.dma("g", x1st[pp % 2],
                                   X1[(2 * pp + side) * 128:(2 * pp + side + 1) * 128, :],
                                   x1_sb[side][:])
    return stx


def build_phase_a2_standalone():
    nc = bass.Bass("TRN2", target_bir_lowering=False, debug=False,
                   detect_race_conditions=False)
    x = nc.dram_tensor("x", [S, D], F32, kind="ExternalInput").ap()
    Qs = nc.dram_tensor("Qs", [(NT // 2) * 128, 4096], BF16, kind="ExternalInput").ap()
    Ks = nc.dram_tensor("Ks", [(NT // 2) * 128, 2048], BF16, kind="ExternalInput").ap()
    Vs = nc.dram_tensor("Vs", [S, 1024], BF16, kind="ExternalInput").ap()
    Wo = nc.dram_tensor("Wo", [2048, D], F32, kind="ExternalInput").ap()
    onescol = nc.dram_tensor("onescol", [128, 1], BF16, kind="ExternalInput").ap()
    onesrow = nc.dram_tensor("onesrow", [1, 128], F32, kind="ExternalInput").ap()
    cpool_d = nc.dram_tensor("cpool", [128, 12], F32, kind="ExternalInput").ap()
    pmasks_d = [nc.dram_tensor(f"pm{i}", [128, 256], BF16, kind="ExternalInput").ap()
                for i in range(4)]
    X1 = nc.dram_tensor("X1", [S, D], F32, kind="ExternalOutput").ap()

    with contextlib.ExitStack() as es:
        onesc_sb = es.enter_context(nc.sbuf_tensor("c_oc", [128, 1], BF16))
        onesr_sb = es.enter_context(nc.sbuf_tensor("c_or", [1, 128], F32))
        cpool_sb = es.enter_context(nc.sbuf_tensor("c_cp", [128, 12], F32))
        pmask_sb = [es.enter_context(nc.sbuf_tensor(f"c_pm{i}", [128, 256], BF16))
                    for i in range(4)]
        sems = make_sems(nc, es, A2_SEMS + ["init"])
        prog = Prog()
        ini = sems["init"]
        prog.dma("y", ini, onesc_sb[:], onescol)
        prog.dma("y", ini, onesr_sb[:], onesrow)
        im = prog.dma("y", ini, cpool_sb[:], cpool_d)
        for i in range(4):
            im = prog.dma("y", ini, pmask_sb[i][:], pmasks_d[i])
        for e in ("t", "v", "s"):
            prog.wait(e, ini, im)
        io = dict(x=x, Qs=Qs, Ks=Ks, Vs=Vs, Wo=Wo, X1=X1)
        cst = dict(ones_col=onesc_sb, ones_row_f=onesr_sb, cpool=cpool_sb,
                   pmask=pmask_sb)
        emit_phase_a2(nc, prog, es, sems, io, cst)
        for nme in ("b_st0", "b_st1"):
            prog.wait("g", sems[nme], sems[nme].n)
        with nc.Block("phasea2") as blk:
            prog.install(blk)
    return nc



def emit_phase_f(nc, prog, es, sems, io, cst):
    X1, W1T, W3T, W2T, Y = io["X1"], io["W1T"], io["W3T"], io["W2T"], io["Y"]
    ident_f = cst["ident_f"]
    h128_sb = cst["h128"]
    cpool = cst["cpool"]

    x1ld = [sems["f_x0"], sems["f_x1"]]
    x1rld = [sems["f_xr0"], sems["f_xr1"]]
    w1ld = [sems["f_w10"], sems["f_w11"]]
    w3ld = [sems["f_w30"], sems["f_w31"]]
    w2ld = [sems["f_w20"], sems["f_w21"]]
    f_s = sems["f_s"]
    f_v = sems["f_v"]
    f_mm = sems["f_mm"]
    ost = [sems["f_o0"], sems["f_o1"]]

    w1b = [es.enter_context(nc.sbuf_tensor(f"f_w1{i}", [128, 16 * 256], BF16)) for i in range(2)]
    w3b = [es.enter_context(nc.sbuf_tensor(f"f_w3{i}", [128, 16 * 256], BF16)) for i in range(2)]
    w2b = [es.enter_context(nc.sbuf_tensor(f"f_w2{i}", [128, 4 * 1024], BF16)) for i in range(2)]
    hidden = es.enter_context(nc.sbuf_tensor("f_hid", [128, 64 * 512], BF16))
    ln2fm = es.enter_context(nc.sbuf_tensor("f_lf", [128, 16 * 512], BF16))
    x1_ln = [es.enter_context(nc.sbuf_tensor(f"f_xl{i}", [128, D], F32)) for i in range(2)]
    x1_res = [es.enter_context(nc.sbuf_tensor(f"f_xr{i}", [128, 1024], F32)) for i in range(2)]
    ln_tm = [es.enter_context(nc.sbuf_tensor(f"f_ln{i}", [128, D], F32)) for i in range(2)]
    stats = [es.enter_context(nc.sbuf_tensor(f"f_stt{i}", [128, 16], F32)) for i in range(2)]
    gs_sb = [es.enter_context(nc.sbuf_tensor(f"f_gs{i}", [128, 512], BF16)) for i in range(2)]
    out_sb = [es.enter_context(nc.sbuf_tensor(f"f_out{i}", [128, 1024], F32)) for i in range(2)]
    ps = es.enter_context(nc.psum_tensor("f_ps", [128, 4096], F32))

    NTB = NT // 4

    x1m, lnm, redm = {}, {}, {}
    trm, evm = {}, {}
    gum, silm, mulm = {}, {}, {}
    h2m, h2ev = {}, {}
    w1m, w3m, w2m = {}, {}, {}
    dnm, devm, stm = {}, {}, {}
    lffull = {}

    def emit_ln2(b, t_loc):
        t = b * 4 + t_loc
        sl = t % 2
        if t >= 2:
            prog.wait("y", f_s, lnm[t - 2])
            prog.wait("y", f_v, redm[t - 2])
        xm = prog.dma("y", x1ld[sl], x1_ln[sl][:], X1[t * 128:(t + 1) * 128, :])
        st_ = stats[sl]
        prog.wait("s", x1ld[sl], xm)
        if t >= 2:
            prog.wait("s", f_mm, trm[(t - 2, 3)])
        sqm_t = prog.op("s", lambda s_, _x=x1_ln[sl], _d=ln_tm[sl], _a=st_:
                        s_.activation(_d[:], _x[:], AF.Square, accum_out=_a[:, 1:2]),
                        inc=f_s)
        prog.wait("v", x1ld[sl], xm)
        redm[t] = prog.op("v", lambda v, _x=x1_ln[sl], _a=st_:
                          v.tensor_reduce(_a[:, 0:1], _x[:], AX.X, ALU.add), inc=f_v)
        prog.wait("v", f_v, redm[t])
        mb = prog.op("v", lambda v, _a=st_:
                     v.tensor_scalar(_a[:, 2:3], _a[:, 0:1], 1.0 / D, None, ALU.mult),
                     inc=f_v)
        prog.wait("v", f_s, sqm_t)
        prog.op("v", lambda v, _a=st_:
                v.tensor_scalar(_a[:, 3:4], _a[:, 1:2], 1.0 / D, None, ALU.mult))
        prog.wait("v", f_v, mb)
        mb = prog.op("v", lambda v, _a=st_:
                     v.tensor_tensor(_a[:, 4:5], _a[:, 2:3], _a[:, 2:3], ALU.mult),
                     inc=f_v)
        prog.wait("v", f_v, mb)
        mb = prog.op("v", lambda v, _a=st_:
                     v.tensor_tensor(_a[:, 5:6], _a[:, 3:4], _a[:, 4:5], ALU.subtract),
                     inc=f_v)
        prog.wait("v", f_v, mb)
        mb = prog.op("v", lambda v, _a=st_:
                     v.tensor_scalar(_a[:, 6:7], _a[:, 5:6], LN_EPS, None, ALU.add),
                     inc=f_v)
        prog.wait("v", f_v, mb)
        vm = prog.op("v", lambda v, _a=st_:
                     v.reciprocal(_a[:, 7:8], _a[:, 6:7]), inc=f_v)
        prog.wait("s", f_v, vm)
        m1 = prog.op("s", lambda s_, _a=st_:
                     s_.activation(_a[:, 8:9], _a[:, 7:8], AF.Sqrt), inc=f_s)
        prog.wait("s", f_s, m1)
        m2 = prog.op("s", lambda s_, _a=st_:
                     s_.activation(_a[:, 9:10], _a[:, 2:3], AF.Identity,
                                   scale=_a[:, 8:9]), inc=f_s)
        prog.wait("s", f_s, m2)
        m3 = prog.op("s", lambda s_, _a=st_, _cp=cpool:
                     s_.activation(_a[:, 10:11], _a[:, 9:10], AF.Identity,
                                   scale=_cp[:, CP_NEG1:CP_NEG1 + 1]), inc=f_s)
        prog.wait("s", f_s, m3)
        lnm[t] = prog.op("s", lambda s_, _x=x1_ln[sl], _d=ln_tm[sl], _a=st_:
                         s_.activation(_d[:], _x[:], AF.Identity, scale=_a[:, 8:9],
                                       bias=_a[:, 10:11]), inc=f_s)

    def emit_transpose2(b, t_loc):
        t = b * 4 + t_loc
        sl = t % 2
        prog.wait("t", f_s, lnm[t])
        if b >= 1 and t_loc == 0:
            prog.wait("t", f_v, devm[(b - 1, 1, 0)])
        for grp in range(4):
            gseq = t * 4 + grp
            if gseq >= 2:
                prog.wait("t", f_v, evm[gseq - 2])
            for k in range(4):
                dc = grp * 4 + k
                fn = (lambda tt, _ps=ps, _o=(gseq % 2) * 512 + k * 128, _ln=ln_tm[sl],
                      _dc=dc, _id=ident_f:
                      tt.transpose(_ps[:, _o:_o + 128],
                                   _ln[:, _dc * 128:(_dc + 1) * 128], _id[:]))
                if k == 3:
                    trm[(t, grp)] = prog.op("t", fn, inc=f_mm)
                else:
                    prog.op("t", fn)
            prog.wait("v", f_mm, trm[(t, grp)])
            if b >= 1 and t_loc == 0 and grp == 0:
                prog.wait("v", f_mm, gum[(b - 1, "u", 63)])
            dst = ap2(ln2fm, grp * 4 * 512 + t_loc * 128,
                      [[16 * 512, 128], [512, 4], [1, 128]])
            evm[gseq] = prog.op(
                "v", lambda v, _d=dst, _ps=ps, _o=(gseq % 2) * 512:
                v.tensor_copy(_d, _ps[:, _o:_o + 512]), inc=f_v)
        if t_loc == 3:
            lffull[b] = evm[t * 4 + 3]

    def emit_w13_load(b, hc):
        sl = hc % 2
        seq = b * 32 + hc
        if seq >= 2:
            pb, phc = (b, hc - 2) if hc >= 2 else (b - 1, hc + 30)
            prog.wait("y", f_mm, gum[(pb, "g", phc * 2 + 1)])
            prog.wait("y", f_mm, gum[(pb, "u", phc * 2 + 1)])
        src1 = ap2(W1T.tensor, hc * 256, [[HID, 128], [128 * HID, 16], [1, 256]])
        w1m[(b, hc)] = prog.dma("y", w1ld[sl], w1b[sl][:], src1)
        src3 = ap2(W3T.tensor, hc * 256, [[HID, 128], [128 * HID, 16], [1, 256]])
        w3m[(b, hc)] = prog.dma("y", w3ld[sl], w3b[sl][:], src3)

    def emit_gu_block(b, hc, hs):
        cblk = hc * 2 + hs
        sl = hc % 2
        par = cblk % 2
        prog.wait("t", w1ld[sl], w1m[(b, hc)])
        prog.wait("t", w3ld[sl], w3m[(b, hc)])
        if hc == 0 and hs == 0:
            prog.wait("t", f_v, lffull[b])
            if b >= 1:
                prog.wait("t", f_v, devm[(b - 1, 1, 1)])
        seq = b * 64 + cblk
        if seq >= 2:
            pb, pc = (b, cblk - 2) if cblk >= 2 else (b - 1, cblk + 62)
            prog.wait("t", f_s, silm[(pb, pc)])
            prog.wait("t", f_v, mulm[(pb, pc)])
        for dc in range(16):
            fn = (lambda tt, _ps=ps, _o=par * 512, _w=w1b[sl], _dc=dc, _hs=hs,
                  _lf=ln2fm, _st=(dc == 0), _sp=(dc == 15):
                  tt.matmul(_ps[:, _o:_o + 512],
                            _w[:, _dc * 256 + _hs * 128:_dc * 256 + _hs * 128 + 128],
                            _lf[:, _dc * 512:(_dc + 1) * 512], start=_st, stop=_sp))
            if dc == 15:
                gum[(b, "g", cblk)] = prog.op("t", fn, inc=f_mm)
            else:
                prog.op("t", fn)
        for dc in range(16):
            fn = (lambda tt, _ps=ps, _o=1024 + par * 512, _w=w3b[sl], _dc=dc, _hs=hs,
                  _lf=ln2fm, _st=(dc == 0), _sp=(dc == 15):
                  tt.matmul(_ps[:, _o:_o + 512],
                            _w[:, _dc * 256 + _hs * 128:_dc * 256 + _hs * 128 + 128],
                            _lf[:, _dc * 512:(_dc + 1) * 512], start=_st, stop=_sp))
            if dc == 15:
                gum[(b, "u", cblk)] = prog.op("t", fn, inc=f_mm)
            else:
                prog.op("t", fn)
        prog.wait("s", f_mm, gum[(b, "g", cblk)])
        if seq >= 2:
            pb, pc = (b, cblk - 2) if cblk >= 2 else (b - 1, cblk + 62)
            prog.wait("s", f_v, mulm[(pb, pc)])
        silm[(b, cblk)] = prog.op("s", lambda s_, _ps=ps, _o=par * 512, _g=gs_sb[par]:
                                  s_.activation(_g[:], _ps[:, _o:_o + 512], AF.Silu),
                                  inc=f_s)
        prog.wait("v", f_mm, gum[(b, "u", cblk)])
        prog.wait("v", f_s, silm[(b, cblk)])
        if b >= 1 and cblk == 0:
            prog.wait("v", f_mm, dnm[(b - 1, 1, 15)])
        mulm[(b, cblk)] = prog.op(
            "v", lambda v, _h=hidden, _c=cblk, _ps=ps, _o=1024 + par * 512, _g=gs_sb[par]:
            v.tensor_tensor(_h[:, _c * 512:(_c + 1) * 512], _ps[:, _o:_o + 512],
                            _g[:], ALU.mult), inc=f_v)

    def emit_h2(b, c):
        par = c % 2
        prog.wait("t", f_v, mulm[(b, c)])
        if c == 0 and b >= 1:
            prog.wait("t", f_v, devm[(b - 1, 1, 2)])
        seq = b * 64 + c
        if seq >= 2:
            pb, pc = (b, c - 2) if c >= 2 else (b - 1, c + 62)
            prog.wait("t", f_v, h2ev[(pb, pc)])
        h2m[(b, c)] = prog.op("t", lambda tt, _ps=ps, _o=2048 + par * 512, _h=hidden,
                              _c=c, _hk=h128_sb:
                              tt.matmul(_ps[:, _o:_o + 512], _hk[:],
                                        _h[:, _c * 512:(_c + 1) * 512],
                                        start=True, stop=True), inc=f_mm)
        prog.wait("v", f_mm, h2m[(b, c)])
        h2ev[(b, c)] = prog.op("v", lambda v, _ps=ps, _o=2048 + par * 512, _h=hidden,
                               _c=c:
                               v.tensor_copy(_h[:, _c * 512:(_c + 1) * 512],
                                             _ps[:, _o:_o + 512]), inc=f_v)

    def emit_w2_load(b, half, cg):
        sl = cg % 2
        seq = (b * 2 + half) * 16 + cg
        if seq >= 2:
            pb, ph, pcg = b, half, cg - 2
            if cg < 2:
                ph = half - 1
                pcg = cg + 14
                if ph < 0:
                    pb, ph = b - 1, 1
            prog.wait("y", f_mm, dnm[(pb, ph, pcg)])
        src = ap2(W2T.tensor, (cg * 4) * 128 * D + half * 1024,
                  [[D, 128], [128 * D, 4], [1, 1024]])
        w2m[(b, half, cg)] = prog.dma("y", w2ld[sl], w2b[sl][:], src)

    def emit_down_group(b, half, cg):
        sl = cg % 2
        prog.wait("t", w2ld[sl], w2m[(b, half, cg)])
        if cg == 0:
            if half == 0:
                prog.wait("t", f_v, h2ev[(b, 63)])
                if b >= 1:
                    prog.wait("t", f_v, devm[(b - 1, 1, 3)])
            else:
                for tl in range(4):
                    prog.wait("t", f_v, devm[(b, 0, tl)])
        for t_loc in range(4):
            for j in range(2):
                for c4 in range(4):
                    c = cg * 4 + c4
                    is_last = (c4 == 3 and t_loc == 3 and j == 1)
                    fn = (lambda tt, _ps=ps, _o=t_loc * 1024 + j * 512, _h=hidden,
                          _hoff=c * 512 + t_loc * 128, _w=w2b[sl],
                          _woff=c4 * 1024 + j * 512, _st=(c == 0), _sp=(c == 63):
                          tt.matmul(_ps[:, _o:_o + 512],
                                    _h[:, _hoff:_hoff + 128],
                                    _w[:, _woff:_woff + 512], start=_st, stop=_sp))
                    if is_last:
                        dnm[(b, half, cg)] = prog.op("t", fn, inc=f_mm)
                    else:
                        prog.op("t", fn)

    xresm = {}

    def emit_x1res_load(b, half, t_loc):
        t = b * 4 + t_loc
        sl = t % 2
        seq = (b * 2 + half) * 4 + t_loc
        if seq >= 2:
            pb, ph, pt = b, half, t_loc - 2
            if t_loc < 2:
                ph = half - 1
                pt = t_loc + 2
                if ph < 0:
                    pb, ph = b - 1, 1
            prog.wait("y", f_v, devm[(pb, ph, pt)])
        src = ap2(X1.tensor, t * 128 * D + half * 1024, [[D, 128], [1, 1024]])
        xresm[(b, half, t_loc)] = prog.dma("y", x1rld[sl], x1_res[sl][:], src)

    def emit_down_evac(b, half, t_loc):
        t = b * 4 + t_loc
        sl = t % 2
        prog.wait("v", f_mm, dnm[(b, half, 15)])
        prog.wait("v", x1rld[sl], xresm[(b, half, t_loc)])
        seq = (b * 2 + half) * 4 + t_loc
        if seq >= 2:
            pb, ph, pt = b, half, t_loc - 2
            if t_loc < 2:
                ph = half - 1
                pt = t_loc + 2
                if ph < 0:
                    pb, ph = b - 1, 1
            prog.wait("v", ost[sl], stm[(pb, ph, pt)])
        devm[(b, half, t_loc)] = prog.op(
            "v", lambda v, _o=out_sb[sl], _ps=ps, _po=t_loc * 1024, _x=x1_res[sl]:
            v.tensor_tensor(_o[:], _ps[:, _po:_po + 1024], _x[:], ALU.add), inc=f_v)
        prog.wait("g", f_v, devm[(b, half, t_loc)])
        dst = ap2(Y.tensor, t * 128 * D + half * 1024, [[D, 128], [1, 1024]])
        stm[(b, half, t_loc)] = prog.dma("g", ost[sl], dst, out_sb[sl][:])

    for b in range(NTB):
        for t_loc in range(4):
            emit_ln2(b, t_loc)
            emit_transpose2(b, t_loc)
        emit_w13_load(b, 0)
        emit_w13_load(b, 1)
        for hc in range(32):
            emit_gu_block(b, hc, 0)
            emit_gu_block(b, hc, 1)
            if hc + 2 < 32:
                emit_w13_load(b, hc + 2)
        for c in range(64):
            emit_h2(b, c)
        for half in range(2):
            emit_w2_load(b, half, 0)
            for cg in range(16):
                emit_down_group(b, half, cg)
                if cg + 1 < 16:
                    emit_w2_load(b, half, cg + 1)
                elif half == 0:
                    emit_w2_load(b, 1, 0)
                if cg >= 14:
                    emit_x1res_load(b, half, cg - 14)
            for t_loc in range(4):
                emit_down_evac(b, half, t_loc)
                if t_loc + 2 < 4:
                    emit_x1res_load(b, half, t_loc + 2)
    return stm


def build_phase_f_standalone():
    nc = bass.Bass("TRN2", target_bir_lowering=False, debug=False,
                   detect_race_conditions=False)
    X1 = nc.dram_tensor("X1", [S, D], F32, kind="ExternalInput").ap()
    W1T = nc.dram_tensor("W1T", [D, HID], BF16, kind="ExternalInput").ap()
    W3T = nc.dram_tensor("W3T", [D, HID], BF16, kind="ExternalInput").ap()
    W2T = nc.dram_tensor("W2T", [HID, D], BF16, kind="ExternalInput").ap()
    identf = nc.dram_tensor("identf", [128, 128], F32, kind="ExternalInput").ap()
    h128 = nc.dram_tensor("h128", [128, 128], BF16, kind="ExternalInput").ap()
    cpool_d = nc.dram_tensor("cpool", [128, 12], F32, kind="ExternalInput").ap()
    Y = nc.dram_tensor("Y", [S, D], F32, kind="ExternalOutput").ap()
    with contextlib.ExitStack() as es:
        identf_sb = es.enter_context(nc.sbuf_tensor("c_idf", [128, 128], F32))
        h128_sb = es.enter_context(nc.sbuf_tensor("c_h128", [128, 128], BF16))
        cpool_sb = es.enter_context(nc.sbuf_tensor("c_cp", [128, 12], F32))
        sems = make_sems(nc, es, F_SEMS + ["init"])
        prog = Prog()
        ini = sems["init"]
        prog.dma("y", ini, identf_sb[:], identf)
        prog.dma("y", ini, h128_sb[:], h128)
        im = prog.dma("y", ini, cpool_sb[:], cpool_d)
        for e in ("t", "v", "s"):
            prog.wait(e, ini, im)
        io = dict(X1=X1, W1T=W1T, W3T=W3T, W2T=W2T, Y=Y)
        cst = dict(ident_f=identf_sb, h128=h128_sb, cpool=cpool_sb)
        emit_phase_f(nc, prog, es, sems, io, cst)
        for nme in ("f_o0", "f_o1"):
            prog.wait("g", sems[nme], sems[nme].n)
        with nc.Block("phasef") as blk:
            prog.install(blk)
    return nc



P_SEMS = ["p_ld0", "p_ld1", "p_w2ld", "p_tr", "p_ev", "p_st0", "p_st1",
          "p_st20", "p_st21", "p_bfv", "p_bfg", "p_bfs"]
A1_SEMS = ["a_x0", "a_x1", "a_cs0", "a_cs1", "a_wl", "a_s1", "a_v1", "a_tr",
           "a_mm", "a_sq", "a_sk", "a_sv0", "a_sv1"]
A2_SEMS = ["b_q0", "b_q1", "b_k0", "b_k1", "b_v0", "b_v1", "b_x0", "b_x1",
           "b_wol", "b_s", "b_v", "b_g", "b_mm", "b_st0", "b_st1"]
F_SEMS = ["f_x0", "f_x1", "f_xr0", "f_xr1", "f_w10", "f_w11", "f_w30", "f_w31",
          "f_w20", "f_w21", "f_s", "f_v", "f_mm", "f_o0", "f_o1"]


def entry_waits(prog, sems, names):
    for eng in ENGS:
        for n in names:
            prog.wait(eng, sems[n], sems[n].n)


def build_full():
    nc = bass.Bass("TRN2", target_bir_lowering=False, debug=False,
                   detect_race_conditions=False)
    x = nc.dram_tensor("x", [S, D], F32, kind="ExternalInput").ap()
    Wq = nc.dram_tensor("Wq", [D, 2048], F32, kind="ExternalInput").ap()
    Wk = nc.dram_tensor("Wk", [D, 1024], F32, kind="ExternalInput").ap()
    Wv = nc.dram_tensor("Wv", [D, 1024], F32, kind="ExternalInput").ap()
    Wo = nc.dram_tensor("Wo", [2048, D], F32, kind="ExternalInput").ap()
    W1 = nc.dram_tensor("W1", [HID, D], F32, kind="ExternalInput").ap()
    W2 = nc.dram_tensor("W2", [D, HID], F32, kind="ExternalInput").ap()
    W3 = nc.dram_tensor("W3", [HID, D], F32, kind="ExternalInput").ap()
    qsT = nc.dram_tensor("qsT", [128, 16], F32, kind="ExternalInput").ap()
    ksT = nc.dram_tensor("ksT", [128, 8], F32, kind="ExternalInput").ap()
    cc_d = nc.dram_tensor("cc", [128, S], F32, kind="ExternalInput").ap()
    ss_d = nc.dram_tensor("ss", [128, S], F32, kind="ExternalInput").ap()
    identb = nc.dram_tensor("identb", [128, 128], BF16, kind="ExternalInput").ap()
    identf = nc.dram_tensor("identf", [128, 128], F32, kind="ExternalInput").ap()
    h128 = nc.dram_tensor("h128", [128, 128], BF16, kind="ExternalInput").ap()
    onescol = nc.dram_tensor("onescol", [128, 1], BF16, kind="ExternalInput").ap()
    onesrow = nc.dram_tensor("onesrow", [1, 128], F32, kind="ExternalInput").ap()
    cpool_d = nc.dram_tensor("cpool", [128, 12], F32, kind="ExternalInput").ap()
    pmasks_d = [nc.dram_tensor(f"pm{i}", [128, 256], BF16, kind="ExternalInput").ap()
                for i in range(4)]
    Y = nc.dram_tensor("Y", [S, D], F32, kind="ExternalOutput").ap()

    W1T = nc.dram_tensor("W1T", [D, HID], BF16, kind="Internal").ap()
    W3T = nc.dram_tensor("W3T", [D, HID], BF16, kind="Internal").ap()
    W2T = nc.dram_tensor("W2T", [HID, D], BF16, kind="Internal").ap()
    Qs = nc.dram_tensor("Qs", [(NT // 2) * 128, 4096], BF16, kind="Internal").ap()
    Ks = nc.dram_tensor("Ks", [(NT // 2) * 128, 2048], BF16, kind="Internal").ap()
    Vs = nc.dram_tensor("Vs", [S, 1024], BF16, kind="Internal").ap()
    X1 = nc.dram_tensor("X1", [S, D], F32, kind="Internal").ap()

    c = host_constants()
    with contextlib.ExitStack() as es:
        identb_sb = es.enter_context(nc.sbuf_tensor("c_idb", [128, 128], BF16))
        identf_sb = es.enter_context(nc.sbuf_tensor("c_idf", [128, 128], F32))
        h128_sb = es.enter_context(nc.sbuf_tensor("c_h128", [128, 128], BF16))
        onesc_sb = es.enter_context(nc.sbuf_tensor("c_oc", [128, 1], BF16))
        onesr_sb = es.enter_context(nc.sbuf_tensor("c_or", [1, 128], F32))
        cpool_sb = es.enter_context(nc.sbuf_tensor("c_cp", [128, 12], F32))
        pmask_sb = [es.enter_context(nc.sbuf_tensor(f"c_pm{i}", [128, 256], BF16))
                    for i in range(4)]
        sems = make_sems(nc, es, ["init"] + P_SEMS + A1_SEMS + A2_SEMS + F_SEMS)
        ini = sems["init"]

        with contextlib.ExitStack() as esp:
            prog = Prog()
            for dst, src in ((identb_sb, identb), (identf_sb, identf),
                             (h128_sb, h128), (onesc_sb, onescol),
                             (onesr_sb, onesrow), (cpool_sb, cpool_d),
                             (pmask_sb[0], pmasks_d[0]), (pmask_sb[1], pmasks_d[1]),
                             (pmask_sb[2], pmasks_d[2]), (pmask_sb[3], pmasks_d[3])):
                im = prog.dma("y", ini, dst[:], src)
            for e in ENGS:
                prog.wait(e, ini, im)
            emit_phase_p(nc, prog, esp, sems, W1, W3, W2, W1T, W3T, W2T,
                         identf_sb, cpool_sb)
            with nc.Block("ph_p") as blk:
                prog.install(blk)

        with contextlib.ExitStack() as esa:
            prog = Prog()
            entry_waits(prog, sems, P_SEMS)
            io = dict(x=x, Wq=Wq, Wk=Wk, Wv=Wv, qsT=qsT, ksT=ksT, cc=cc_d, ss=ss_d,
                      Qs=Qs, Ks=Ks, Vs=Vs)
            cst = dict(ident_bf=identb_sb, ones_col=onesc_sb, ones_row_f=onesr_sb,
                       cpool=cpool_sb, pairswap=c["pairswap"])
            emit_phase_a1(nc, prog, esa, sems, io, cst)
            with nc.Block("ph_a1") as blk:
                prog.install(blk)

        with contextlib.ExitStack() as esb:
            prog = Prog()
            entry_waits(prog, sems, A1_SEMS)
            io = dict(x=x, Qs=Qs, Ks=Ks, Vs=Vs, Wo=Wo, X1=X1)
            cst = dict(ones_col=onesc_sb, ones_row_f=onesr_sb, cpool=cpool_sb,
                       pmask=pmask_sb)
            emit_phase_a2(nc, prog, esb, sems, io, cst)
            with nc.Block("ph_a2") as blk:
                prog.install(blk)

        with contextlib.ExitStack() as esf:
            prog = Prog()
            entry_waits(prog, sems, A2_SEMS)
            io = dict(X1=X1, W1T=W1T, W3T=W3T, W2T=W2T, Y=Y)
            cst = dict(ident_f=identf_sb, h128=h128_sb, cpool=cpool_sb)
            emit_phase_f(nc, prog, esf, sems, io, cst)
            for nme in ("f_o0", "f_o1"):
                prog.wait("g", sems[nme], sems[nme].n)
            with nc.Block("ph_f") as blk:
                prog.install(blk)
    return nc


_NC_CACHE = {}


def kernel(**inputs):
    lay = layout_inputs(inputs)
    c = host_constants()
    x_all = np.ascontiguousarray(np.asarray(inputs["L0_attn_x"], np.float32))
    if "full" not in _NC_CACHE:
        _NC_CACHE["full"] = build_full()
    nc = _NC_CACHE["full"]
    common = {
        "Wq": lay["Wq"], "Wk": lay["Wk"], "Wv": lay["Wv"], "Wo": lay["Wo"],
        "W1": lay["W1"], "W2": lay["W2"], "W3": lay["W3"],
        "qsT": lay["qs_colsT"], "ksT": lay["ks_colsT"],
        "cc": c["cc"], "ss": c["ss"], "identb": c["ident_bf"],
        "identf": c["ident_f32"], "h128": c["h128_bf"],
        "onescol": c["ones_col_bf"], "onesrow": c["ones_row_f32"],
        "cpool": c["cpool"], "pm0": c["pm0"], "pm1": c["pm1"],
        "pm2": c["pm8"], "pm3": c["pm9"],
    }
    in_maps = [dict(common, x=np.ascontiguousarray(x_all[core])) for core in range(8)]
    res = run_bass_kernel_spmd(nc, in_maps, core_ids=list(range(8)))
    return np.stack([np.asarray(res.results[i]["Y"]) for i in range(8)], axis=0)
